# version 32
# baseline (speedup 1.0000x reference)
import numpy as np
import ml_dtypes
from contextlib import ExitStack
import concourse.bass as bass
import concourse.mybir as mybir
from concourse.bass_utils import run_bass_kernel_spmd

F32 = mybir.dt.float32
BF16 = mybir.dt.bfloat16
AF = mybir.ActivationFunctionType
ALU = mybir.AluOpType
AX = mybir.AxisListType
BFD = ml_dtypes.bfloat16

T = 2048
D = 1024
TC = 16
KC = 8
NE = 12
NQ = 4
TCQ = TC // NQ
NEG = -1.0e30
EPS_LN = 1e-5

ENGS = ("tensor", "vector", "scalar", "sync", "gpsimd")
_SEM_OF = {"tensor": "pe", "vector": "ve", "scalar": "sc",
           "sync": "ds", "gpsimd": "dg"}


class Ctx:
    def __init__(self, nc, es):
        self.nc = nc
        self.sem = {}
        self.c = {}
        for n in ("pe", "ve", "sc", "ds", "dg", "dv", "so"):
            self.sem[n] = es.enter_context(nc.semaphore("S" + n))
            self.c[n] = 0

    def inc(self, instr, name, k=1):
        instr.then_inc(self.sem[name], k)
        self.c[name] += k
        return self.c[name]

    def w(self, eng, name, val=None):
        v = self.c[name] if val is None else val
        if v > 0:
            eng.wait_ge(self.sem[name], v)


def _barrier(nc, ctx):
    with nc.Block() as b:
        for e in ENGS:
            def f(eng, e=e):
                eng.drain()
                nm = _SEM_OF[e]
                ctx.inc(eng.nop(), nm, 16 if nm in ("ds", "dg") else 1)
            getattr(b, e)(f)
    snap = dict(ctx.c)
    with nc.Block() as b:
        for e in ENGS:
            def f(eng):
                for nm in ("pe", "ve", "sc", "ds", "dg"):
                    ctx.w(eng, nm, snap[nm])
            getattr(b, e)(f)


def build_nc(debug=False):
    nc = bass.Bass("TRN2", target_bir_lowering=False)

    def din(name, shape, dt=F32):
        return nc.dram_tensor(name, shape, dt, kind="ExternalInput").ap()

    x0_d = din("x0", [T, D])
    x0t_d = din("x0t", [D, T])
    wqt_d = din("wqt", [D, D])
    wkt_d = din("wkt", [D, D])
    fc1wt_d = din("fc1wt", [1, 17])
    fc1b_d = din("fc1b", [17, 1])
    fc2wta_d = din("fc2wta", [17, D])
    pwt_d = din("pwt", [D, 64])
    uallt_d = din("uallt", [64, 36])
    b2_d = din("b2", [NE, D])
    w1t_d = din("w1t", [NE, 4, 128, 2 * D])
    w2t_d = din("w2t", [NE, 4, 128, 2 * D])
    costab_d = din("costab", [D, T], BF16)
    sintab_d = din("sintab", [D, T], BF16)
    trimask_d = din("trimask", [128, 128])
    identf_d = din("identf", [128, 128])
    out_d = nc.dram_tensor("out", [T, D], F32, kind="ExternalOutput").ap()

    dbg = {}
    if debug:
        for nm, shape, dt in [
            ("d_q", [D, T], BF16), ("d_k", [D, T], BF16),
            ("d_s", [128, TC], F32), ("d_resid1", [T, D], F32),
            ("d_xln", [T, D], F32), ("d_xa", [64, T], F32),
            ("d_C", [128, TC * 36], F32),
            ("d_y1", [T, D], F32), ("d_y2", [T, D], F32),
        ]:
            dbg[nm] = nc.dram_tensor(nm, shape, dt, kind="ExternalOutput").ap()

    es0 = ExitStack()
    ctx = Ctx(nc, es0)

    def sb(esx, name, shape, dt):
        return esx.enter_context(nc.sbuf_tensor(name, shape, dt))

    smat = sb(es0, "smat", [128, TC], F32)
    fc1wt_s = sb(es0, "fc1wt_s", [1, 17], F32)
    fc1b_s = sb(es0, "fc1b_s", [17, 1], F32)
    identf_s = sb(es0, "identf_s", [128, 128], F32)
    ctok = sb(es0, "ctok", [128, TC * 36], F32)
    bnb = sb(es0, "bnb", [128, 8 * 12], F32)
    mvv = sb(es0, "mvv", [128, 2 * TC], F32)
    rv = sb(es0, "rv", [128, TC], F32)
    b2_s = sb(es0, "b2_s", [NE, D], F32)
    uallt_s = sb(es0, "uallt_s", [64, 36], F32)
    zq = sb(es0, "zq", [128, 144], F32)
    eq1 = sb(es0, "eq1", [128, 144], BF16)
    eq2 = sb(es0, "eq2", [128, 144], BF16)
    eq3 = sb(es0, "eq3", [128, 144], BF16)
    rtg = sb(es0, "rtg", [128, 144], F32)

    PS = [es0.enter_context(nc.psum_tensor(f"ps{i}", [128, 512], F32))
          for i in range(8)]

    with nc.Block() as b:
        def f(s):
            for tgt, src in [(fc1wt_s, fc1wt_d), (fc1b_s, fc1b_d),
                             (identf_s, identf_d), (b2_s, b2_d),
                             (uallt_s, uallt_d)]:
                ctx.inc(s.dma_start(out=tgt[:, :], in_=src[:, :]), "ds", 16)
        b.sync(f)

    es2 = ExitStack()
    xln = sb(es2, "xln", [128, TC * D], F32)

    es1 = ExitStack()
    qTb = sb(es1, "qTb", [128, KC * T], BF16)
    kTb = sb(es1, "kTb", [128, KC * T], BF16)
    trimask_s = sb(es1, "trimask_s", [128, 128], F32)

    es1a = ExitStack()
    x0th = xln[:, 0:KC * T // 2].bitcast(BF16)
    x0tl = xln[:, KC * T // 2:KC * T].bitcast(BF16)
    wsh = sb(es1a, "wsh", [128, KC * D], BF16)
    wsl = sb(es1a, "wsl", [128, KC * D], BF16)
    stg = [sb(es1a, "stg0", [128, T], F32)]
    stgx = sb(es1a, "stgx", [128, T], F32)
    tabs = [sb(es1a, f"tab{i}", [128, T], BF16) for i in range(4)]
    rsc = [stg[0][:, 0:1024].bitcast(BF16), stg[0][:, 1024:2048].bitcast(BF16),
           stgx[:, 0:1024].bitcast(BF16)]

    with nc.Block() as b:
        xsp = {}
        wsp_q = {}
        xslot = [stgx, qTb[:, 0:2 * T].bitcast(F32)]

        def emit_xload(bb, k):
            qn = "dg" if k % 2 == 0 else "dv"

            def f(eng, k=k, qn=qn):
                if k >= 2:
                    ctx.w(eng, "ve", xsp[k - 2][1])
                ctx.inc(eng.dma_start(out=xslot[k % 2][:, :],
                                      in_=x0t_d[k * 128:(k + 1) * 128, :]),
                        qn, 16)
            (bb.gpsimd if k % 2 == 0 else bb.scalar)(f)
            xsp[k] = [0, 0, (qn, ctx.c[qn])]

        def emit_wqload(bb, k):
            def f(s, k=k):
                if k >= 2:
                    ctx.w(s, "ve", wsp_q[k - 2][1])
                ctx.inc(s.dma_start(out=stg[0][:, (k % 2) * D:(k % 2 + 1) * D],
                                    in_=wqt_d[k * 128:(k + 1) * 128, :]),
                        "ds", 16)
            bb.sync(f)
            wsp_q[k] = [0, 0, ctx.c["ds"]]

        emit_xload(b, 0)
        emit_xload(b, 1)
        emit_wqload(b, 0)
        emit_wqload(b, 1)
        for k in range(KC):
            def f_hi(s, k=k):
                qn, cv = xsp[k][2]
                ctx.w(s, qn, cv)
                r = s.copy(x0th[:, k * T:(k + 1) * T], xslot[k % 2][:, :])
                xsp[k][0] = ctx.inc(r, "sc")
            b.scalar(f_hi)

            def f_whi(s, k=k):
                ctx.w(s, "ds", wsp_q[k][2])
                r = s.copy(wsh[:, k * D:(k + 1) * D],
                           stg[0][:, (k % 2) * D:(k % 2 + 1) * D])
                wsp_q[k][0] = ctx.inc(r, "sc")
            b.scalar(f_whi)

            def f_lo(v, k=k):
                ctx.w(v, "sc", xsp[k][0])
                r = v.tensor_tensor(x0tl[:, k * T:(k + 1) * T], xslot[k % 2][:, :],
                                    x0th[:, k * T:(k + 1) * T], ALU.subtract)
                xsp[k][1] = ctx.inc(r, "ve")
            b.vector(f_lo)

            def f_wlo(v, k=k):
                ctx.w(v, "sc", wsp_q[k][0])
                r = v.tensor_tensor(wsl[:, k * D:(k + 1) * D],
                                    stg[0][:, (k % 2) * D:(k % 2 + 1) * D],
                                    wsh[:, k * D:(k + 1) * D], ALU.subtract)
                wsp_q[k][1] = ctx.inc(r, "ve")
            b.vector(f_wlo)
            if k + 2 < KC:
                emit_xload(b, k + 2)
                emit_wqload(b, k + 2)
        xsp_sc, xsp_ve = ctx.c["sc"], ctx.c["ve"]
        xlo_marks = {k: xsp[k][1] for k in range(KC)}
        wqlo_marks = {k: wsp_q[k][1] for k in range(KC)}

        def f_tm(s):
            ctx.inc(s.dma_start(out=trimask_s[:, :], in_=trimask_d[:, :]), "ds", 16)
        b.sync(f_tm)

        rope_ve = {}
        evacs = {}

        def emit_rope_pair(pair, evq, evk, scw=0):
            c, c4 = pair, pair + 4

            def f_g(g, pair=pair, c=c, c4=c4, scw=scw):
                if scw:
                    ctx.w(g, "sc", scw)
                if pair >= 1:
                    ctx.w(g, "ve", rope_ve[pair - 1])
                for i, (tsrc, cc) in enumerate([(costab_d, c), (sintab_d, c),
                                                (costab_d, c4), (sintab_d, c4)]):
                    ctx.inc(g.dma_start(out=tabs[i][:, :],
                                        in_=tsrc[cc * 128:(cc + 1) * 128, :]),
                            "dg", 16)
            b.gpsimd(f_g)
            dg_p = ctx.c["dg"]

            def f_v(v, pair=pair, c=c, c4=c4, dg_p=dg_p, evq=evq, evk=evk):
                ctx.w(v, "dg", dg_p)
                ctx.w(v, "sc", max(evq, evk))
                r = None
                for tens in (qTb, kTb):
                    a = tens[:, c * T:(c + 1) * T]
                    a4 = tens[:, c4 * T:(c4 + 1) * T]
                    v.tensor_tensor(rsc[0], a, tabs[0][:, :], ALU.mult)
                    v.tensor_tensor(rsc[1], a4, tabs[1][:, :], ALU.mult)
                    v.tensor_tensor(rsc[2], a, tabs[3][:, :], ALU.mult)
                    v.tensor_tensor(a, rsc[0], rsc[1], ALU.add)
                    v.tensor_tensor(rsc[0], a4, tabs[2][:, :], ALU.mult)
                    r = v.tensor_tensor(a4, rsc[0], rsc[2], ALU.add)
                rope_ve[pair] = ctx.inc(r, "ve")
            b.vector(f_v)

        for proj, (w_dram, outb) in enumerate([(wqt_d, qTb), (wkt_d, kTb)]):
            evac = {}
            evacs[proj] = evac
            pe_guard = ctx.c["pe"]
            if proj == 0:
                wsp = wsp_q
            else:
                wsp = {}
                kslots = [stg[0][:, 0:D], stg[0][:, D:2 * D],
                          stgx[:, 0:D], stgx[:, D:2 * D],
                          tabs[0][:, 0:2 * D].bitcast(F32),
                          tabs[1][:, 0:2 * D].bitcast(F32)]
                kguard = [wqlo_marks[6], wqlo_marks[7], xlo_marks[6],
                          xlo_marks[6], 0, 0]

                def emit_wkload(bb, k):
                    qn = "ds" if k % 2 == 0 else "dg"

                    def f(eng, k=k, qn=qn):
                        if k < 6:
                            ctx.w(eng, "ve", kguard[k])
                        else:
                            ctx.w(eng, "ve", wsp[k - 6][1])
                        ctx.inc(eng.dma_start(
                            out=kslots[k % 6],
                            in_=w_dram[k * 128:(k + 1) * 128, :]), qn, 16)
                    (bb.sync if k % 2 == 0 else bb.gpsimd)(f)
                    wsp[k] = [0, 0, (qn, ctx.c[qn])]

                for k in range(6):
                    emit_wkload(b, k)
                for k in range(KC):
                    def f_whik(s, k=k):
                        qn, cv = wsp[k][2]
                        ctx.w(s, qn, cv)
                        if k == 0:
                            ctx.w(s, "pe", pe_guard)
                        r = s.copy(wsh[:, k * D:(k + 1) * D], kslots[k % 6])
                        wsp[k][0] = ctx.inc(r, "sc")
                    b.scalar(f_whik)

                    def f_wlok(v, k=k):
                        ctx.w(v, "sc", wsp[k][0])
                        r = v.tensor_tensor(wsl[:, k * D:(k + 1) * D],
                                            kslots[k % 6],
                                            wsh[:, k * D:(k + 1) * D],
                                            ALU.subtract)
                        wsp[k][1] = ctx.inc(r, "ve")
                    b.vector(f_wlok)
                    if k + 6 < KC:
                        emit_wkload(b, k + 6)

            for oc in range(8):
                for tb in range(4):
                    idx = oc * 4 + tb
                    bank = PS[idx % 2]
                    pe_t = {}

                    def f_pe(t, oc=oc, tb=tb, idx=idx, bank=bank, pe_t=pe_t,
                             wsp=wsp, evac=evac):
                        if idx == 0:
                            ctx.w(t, "sc", wsp[KC - 1][0])
                            ctx.w(t, "ve", wsp[KC - 1][1])
                            ctx.w(t, "sc", xsp_sc)
                            ctx.w(t, "ve", xsp_ve)
                        if idx >= 2:
                            ctx.w(t, "sc", evac[idx - 2])
                        r = None
                        for k in range(KC):
                            wh_sl = wsh[:, k * D + oc * 128: k * D + (oc + 1) * 128]
                            wl_sl = wsl[:, k * D + oc * 128: k * D + (oc + 1) * 128]
                            xh_sl = x0th[:, k * T + tb * 512: k * T + (tb + 1) * 512]
                            xl_sl = x0tl[:, k * T + tb * 512: k * T + (tb + 1) * 512]
                            t.matmul(bank[:, :], wh_sl, xh_sl,
                                     start=(k == 0), stop=False)
                            t.matmul(bank[:, :], wh_sl, xl_sl,
                                     start=False, stop=False)
                            r = t.matmul(bank[:, :], wl_sl, xh_sl,
                                         start=False, stop=(k == KC - 1))
                        pe_t[0] = ctx.inc(r, "pe")
                    b.tensor(f_pe)

                    def f_ev(s, oc=oc, tb=tb, idx=idx, bank=bank, outb=outb,
                             pe_t=pe_t, evac=evac):
                        ctx.w(s, "pe", pe_t[0])
                        r = s.copy(outb[:, oc * T + tb * 512: oc * T + (tb + 1) * 512],
                                   bank[:, :])
                        evac[idx] = ctx.inc(r, "sc")
                    b.scalar(f_ev)
                    if proj == 1 and oc >= 4 and tb == 3:
                        pair = oc - 4
                        emit_rope_pair(pair,
                                       max(evacs[0].values()),
                                       evac[oc * 4 + 3],
                                       scw=(wsp[5][0] if pair == 0 else 0))

    _barrier(nc, ctx)
    es1a.close()


    es2a = ExitStack()
    relu16 = sb(es2a, "relu16", [17, T], F32)
    srow = sb(es2a, "srow", [1, T], F32)
    fc2wta_s = sb(es2a, "fc2wta_s", [17, D], F32)
    es1c = ExitStack()
    expbuf2 = [sb(es1c, f"expbuf{i}", [128, T], F32) for i in range(2)]

    with nc.Block() as b:
        x0_dg = {}

        def f_gx(g):
            for tc in range(TC):
                r = g.dma_start(out=xln[:, tc * D:(tc + 1) * D],
                                in_=x0_d[tc * 128:(tc + 1) * 128, :])
                x0_dg[tc] = ctx.inc(r, "dg", 16)
        b.gpsimd(f_gx)

        def f_sw(s):
            ctx.inc(s.dma_start(out=fc2wta_s[:, :], in_=fc2wta_d[:, :]), "ds", 16)
        b.sync(f_sw)
        fc2_ds = ctx.c["ds"]

        bank_ready = {}
        red_ve = {}
        sr_sc = {}
        r16_sc = {}
        last_ao = {"ve": 0}
        ln_done = {}
        for qc in range(TC):
            nblk = qc // 4 + 1
            valid = (qc + 1) * 128
            rem = valid - (nblk - 1) * 512
            pe_q = {}
            b0 = (qc % 2) * 4
            expbuf = expbuf2[qc % 2]

            def f_pe(t, qc=qc, nblk=nblk, rem=rem, pe_q=pe_q, b0=b0):
                for jb in range(nblk):
                    if (b0 + jb) in bank_ready:
                        nm, val = bank_ready[b0 + jb]
                        ctx.w(t, nm, val)
                for dk in range(KC):
                    r = None
                    for jb in range(nblk):
                        N = 512 if jb < nblk - 1 else rem
                        r = t.matmul(PS[b0 + jb][:, 0:N],
                                     qTb[:, dk * T + qc * 128: dk * T + (qc + 1) * 128],
                                     kTb[:, dk * T + jb * 512: dk * T + jb * 512 + N],
                                     start=(dk == 0), stop=(dk == KC - 1))
                    if dk == KC - 1:
                        pe_q[0] = ctx.inc(r, "pe")
            b.tensor(f_pe)

            def f_vm(v, qc=qc, nblk=nblk, rem=rem, pe_q=pe_q, b0=b0):
                ctx.w(v, "pe", pe_q[0])
                r = v.tensor_tensor(PS[b0 + nblk - 1][:, rem - 128:rem],
                                    PS[b0 + nblk - 1][:, rem - 128:rem],
                                    trimask_s[:, :], ALU.add)
                pe_q[1] = ctx.inc(r, "ve")
            b.vector(f_vm)

            def f_sc(s, qc=qc, nblk=nblk, rem=rem, pe_q=pe_q, b0=b0,
                     expbuf=expbuf):
                ctx.w(s, "pe", pe_q[0])
                ctx.w(s, "ve", pe_q[1])
                if qc >= 2:
                    ctx.w(s, "ve", red_ve[qc - 2])
                for jb in range(nblk):
                    N = 512 if jb < nblk - 1 else rem
                    r = s.activation(out=expbuf[:, jb * 512: jb * 512 + N],
                                     in_=PS[b0 + jb][:, 0:N], func=AF.Exp,
                                     scale=1.0 / 32.0)
                    bank_ready[b0 + jb] = ("sc", ctx.inc(r, "sc"))
            b.scalar(f_sc)

            def f_red(v, qc=qc, valid=valid, expbuf=expbuf):
                ctx.w(v, "sc")
                r = v.reduce_sum(smat[:, qc:qc + 1], expbuf[:, 0:valid], axis=AX.X)
                red_ve[qc] = ctx.inc(r, "ve")
            b.vector(f_red)

            if qc % 4 == 3:
                g4 = qc // 4
                Tb = (g4 % 2) * 4 + 3
                tr = {}

                def f_trg(t, g4=g4, Tb=Tb, tr=tr):
                    ctx.w(t, "ve", red_ve[4 * g4 + 3])
                    if Tb in bank_ready:
                        nm, val = bank_ready[Tb]
                        ctx.w(t, nm, val)
                    r = None
                    for j in range(4):
                        qq = g4 * 4 + j
                        r = t.transpose(PS[Tb][0:1, j * 128:(j + 1) * 128],
                                        smat[:, qq:qq + 1], identf_s[:, :])
                    tr[0] = ctx.inc(r, "pe")
                b.tensor(f_trg)

                def f_srg(s, g4=g4, Tb=Tb, tr=tr):
                    ctx.w(s, "pe", tr[0])
                    s.copy(srow[0:1, g4 * 512:(g4 + 1) * 512], PS[Tb][0:1, 0:512])
                    s.drain()
                    sr_sc[g4] = ctx.inc(s.nop(), "sc")
                b.scalar(f_srg)

                r16 = {}

                def f_r16(t, g4=g4, Tb=Tb, r16=r16):
                    ctx.w(t, "sc", sr_sc[g4])
                    r = t.matmul(PS[Tb][0:17, 0:512], fc1wt_s[0:1, :],
                                 srow[0:1, g4 * 512:(g4 + 1) * 512],
                                 start=True, stop=True)
                    r16[0] = ctx.inc(r, "pe")
                b.tensor(f_r16)

                def f_r16s(s, g4=g4, Tb=Tb, r16=r16):
                    ctx.w(s, "pe", r16[0])
                    s.activation(out=relu16[0:17, g4 * 512:(g4 + 1) * 512],
                                 in_=PS[Tb][0:17, 0:512], func=AF.Relu,
                                 bias=fc1b_s[:, 0:1])
                    s.drain()
                    r16_sc[g4] = ctx.inc(s.nop(), "sc")
                b.scalar(f_r16s)

                for j in range(4):
                    tc = g4 * 4 + j
                    for oh in range(2):
                        ao = {}

                        def f_ao(t, tc=tc, oh=oh, g4=g4, Tb=Tb, ao=ao, j=j):
                            if j == 0 and oh == 0:
                                ctx.w(t, "sc", r16_sc[g4])
                                ctx.w(t, "ds", fc2_ds)
                            else:
                                ctx.w(t, "ve", last_ao["ve"])
                            r = t.matmul(PS[Tb][:, :],
                                         relu16[:, tc * 128:(tc + 1) * 128],
                                         fc2wta_s[:, oh * 512:(oh + 1) * 512],
                                         start=True, stop=True)
                            ao[0] = ctx.inc(r, "pe")
                        b.tensor(f_ao)

                        def f_aov(v, tc=tc, oh=oh, Tb=Tb, ao=ao):
                            ctx.w(v, "pe", ao[0])
                            ctx.w(v, "dg", x0_dg[tc])
                            sl = xln[:, tc * D + oh * 512: tc * D + (oh + 1) * 512]
                            r = v.scalar_tensor_tensor(sl, PS[Tb][:, :], 1.0, sl,
                                                       ALU.mult, ALU.add)
                            last_ao["ve"] = ctx.inc(r, "ve")
                        b.vector(f_aov)

                    def f_lns(v, tc=tc):
                        v.drain()
                        tl = tc % 8
                        v.bn_stats(bnb[:, tl * 12:tl * 12 + 6],
                                   xln[:, tc * D: tc * D + 512])
                        v.bn_stats(bnb[:, tl * 12 + 6:tl * 12 + 12],
                                   xln[:, tc * D + 512: tc * D + 1024])
                    b.vector(f_lns)
                bank_ready[Tb] = ("ve", last_ao["ve"])

                if g4 % 2 == 1:
                    half = g4 // 2

                    def f_lna(v, half=half):
                        v.drain()
                        for tl in range(8):
                            tc = half * 8 + tl
                            v.bn_aggr(mvv[:, 2 * tc:2 * tc + 2],
                                      bnb[:, tl * 12:(tl + 1) * 12])
                        v.drain()
                        for tl in range(8):
                            tc = half * 8 + tl
                            v.tensor_scalar(rv[:, tc:tc + 1],
                                            mvv[:, 2 * tc + 1:2 * tc + 2],
                                            EPS_LN, None, ALU.add)
                        v.drain()
                        ln_done[half] = ctx.inc(v.nop(), "ve")
                    b.vector(f_lna)

        def f_rstd(s):
            ctx.w(s, "ve", ln_done[1])
            s.activation(out=rv[:, :], in_=rv[:, :], func=AF.Ln)
            s.drain()
            s.activation(out=rv[:, :], in_=rv[:, :], func=AF.Exp, scale=-0.5)
            s.drain()
            ctx.inc(s.nop(), "sc")
        b.scalar(f_rstd)
        rstd_sc = ctx.c["sc"]

        if debug:
            def f_dbg(g):
                ctx.w(g, "ve")
                ctx.inc(g.dma_start(out=dbg["d_s"][:, :], in_=smat[:, :]), "dg", 16)
                for tc in range(TC):
                    ctx.inc(g.dma_start(out=dbg["d_resid1"][tc * 128:(tc + 1) * 128, :],
                                        in_=xln[:, tc * D:(tc + 1) * D]), "dg", 16)
            b.gpsimd(f_dbg)

        def f_ap(v):
            ctx.w(v, "sc", rstd_sc)
            if debug:
                ctx.w(v, "dg")
            for tc in range(TC):
                v.tensor_scalar(xln[:, tc * D:(tc + 1) * D],
                                xln[:, tc * D:(tc + 1) * D],
                                mvv[:, 2 * tc:2 * tc + 1], rv[:, tc:tc + 1],
                                ALU.subtract, ALU.mult)
            v.drain()
            ctx.inc(v.nop(), "ve")
        b.vector(f_ap)

        if debug:
            def f_dbg2(g):
                ctx.w(g, "ve")
                for tc in range(TC):
                    ctx.inc(g.dma_start(out=dbg["d_xln"][tc * 128:(tc + 1) * 128, :],
                                        in_=xln[:, tc * D:(tc + 1) * D]), "dg", 16)
            b.gpsimd(f_dbg2)

    _barrier(nc, ctx)
    es1c.close()
    es2a.close()
    es1.close()

    es2b = ExitStack()
    xth = sb(es2b, "xth", [128, KC * 512], BF16)
    xtl = sb(es2b, "xtl", [128, KC * 512], BF16)
    hth = sb(es2b, "hth", [128, KC * 512], BF16)
    htl = sb(es2b, "htl", [128, KC * 512], BF16)
    yq = sb(es2b, "yq", [128, TCQ * D], F32)
    wbh = [sb(es2b, f"wbh{i}", [128, KC * D], BF16) for i in range(2)]
    wbl = [sb(es2b, f"wbl{i}", [128, KC * D], BF16) for i in range(2)]
    wstg = [sb(es2b, f"wstg{i}", [128, 2 * D], F32) for i in range(2)]
    pwh = sb(es2b, "pwh", [128, KC * 64], BF16)
    pwl = sb(es2b, "pwl", [128, KC * 64], BF16)
    xa_s = sb(es2b, "xa_s", [64, 512], F32)
    c3tq = sb(es2b, "c3tq", [NE, 512], F32)

    wstg_guard = {0: 0, 1: 0}
    wbuf_guard = {0: 0, 1: 0}
    bank_guard = {i: 0 for i in range(6)}

    with nc.Block() as b:
        pw_ds = {}
        for k in range(KC):
            def f_pwl(s, k=k):
                ctx.inc(s.dma_start(out=wstg[0][:, k * 64:(k + 1) * 64],
                                    in_=pwt_d[k * 128:(k + 1) * 128, :]), "ds", 16)
                pw_ds[k] = ctx.c["ds"]
            b.sync(f_pwl)

        def f_pwh(s):
            ctx.w(s, "ds", pw_ds[KC - 1])
            ctx.inc(s.copy(pwh[:, 0:KC * 64], wstg[0][:, 0:KC * 64]), "sc")
        b.scalar(f_pwh)
        pwh_sc = ctx.c["sc"]

        def f_pwlo(v):
            ctx.w(v, "sc", pwh_sc)
            ctx.inc(v.tensor_tensor(pwl[:, 0:KC * 64], wstg[0][:, 0:KC * 64],
                                    pwh[:, 0:KC * 64], ALU.subtract), "ve")
        b.vector(f_pwlo)
        wstg_guard[0] = ctx.c["ve"]

    def emit_wload(b, src_dram, e, buf, defer_subs=False):
        marks = {}

        def f_cast(g, e=e, src_dram=src_dram, buf=buf):
            ctx.w(g, "pe", wbuf_guard[buf])
            src = src_dram[e].rearrange("a p d -> p a d")
            dst = wbh[buf][:, :].rearrange("p (a d) -> p a d", d=2 * D)
            ctx.inc(g.dma_start(out=dst, in_=src), "dg", 16)
        b.gpsimd(f_cast)
        cast_dg = ctx.c["dg"]
        dq_marks = {}

        def emit_dma(bb, d):
            k0 = 2 * d
            qsem = "ds" if d % 2 == 0 else "dg"

            def f_l(q, d=d, k0=k0, qsem=qsem):
                if d >= 2:
                    ctx.w(q, "ve", marks[d - 2])
                else:
                    ctx.w(q, "ve", wstg_guard[d % 2])
                    ctx.w(q, "pe", wbuf_guard[buf])
                ctx.inc(q.dma_start(out=wstg[d % 2][:, :],
                                    in_=src_dram[e, d]), qsem, 16)
            (bb.sync if d % 2 == 0 else bb.gpsimd)(f_l)
            dq_marks[d] = (qsem, ctx.c[qsem])

        def emit_sub(bb, d):
            k0 = 2 * d
            qsem, dq_k = dq_marks[d]

            def f_lo(v, d=d, k0=k0, dq_k=dq_k, qsem=qsem):
                ctx.w(v, qsem, dq_k)
                ctx.w(v, "dg", cast_dg)
                r = v.tensor_tensor(wbl[buf][:, k0 * D:(k0 + 2) * D],
                                    wstg[d % 2][:, :],
                                    wbh[buf][:, k0 * D:(k0 + 2) * D], ALU.subtract)
                marks[d] = ctx.inc(r, "ve")
                wstg_guard[d % 2] = marks[d]
            bb.vector(f_lo)

        emit_dma(b, 0)
        emit_dma(b, 1)
        if not defer_subs:
            emit_sub(b, 0)
            emit_sub(b, 1)
            emit_dma(b, 2)
            emit_dma(b, 3)
            emit_sub(b, 2)
            emit_sub(b, 3)
            jobs = []
        else:
            def j0(bb):
                emit_sub(bb, 0)
                emit_sub(bb, 1)
                emit_dma(bb, 2)
                emit_dma(bb, 3)

            def j1(bb):
                emit_sub(bb, 2)

            def j2(bb):
                emit_sub(bb, 3)

            def j3(bb):
                pass
            jobs = [j0, j1, j2, j3]
        return {"dg": cast_dg, "marks": marks, "jobs": jobs}

    def emit_transpose_split(b, src, src_col0, dsth, dstl):
        ev_prev = {}
        for k in range(KC):
            bank = PS[4 + (k % 2)]
            mark = {}

            def f_t(t, k=k, bank=bank, mark=mark):
                if k >= 2:
                    ctx.w(t, "ve", ev_prev[k - 2])
                else:
                    ctx.w(t, "ve")
                    ctx.w(t, "sc")
                r = None
                for tcl in range(TCQ):
                    r = t.transpose(bank[:, tcl * 128:(tcl + 1) * 128],
                                    src[:, src_col0 + tcl * D + k * 128:
                                        src_col0 + tcl * D + (k + 1) * 128],
                                    identf_s[:, :])
                mark[0] = ctx.inc(r, "pe")
            b.tensor(f_t)

            def f_h(s_, k=k, bank=bank, mark=mark):
                ctx.w(s_, "pe", mark[0])
                r = s_.copy(dsth[:, k * 512:(k + 1) * 512], bank[:, :])
                mark[1] = ctx.inc(r, "sc")
            b.scalar(f_h)

            def f_l(v, k=k, bank=bank, mark=mark):
                ctx.w(v, "sc", mark[1])
                r = v.tensor_tensor(dstl[:, k * 512:(k + 1) * 512], bank[:, :],
                                    dsth[:, k * 512:(k + 1) * 512], ALU.subtract)
                ev_prev[k] = ctx.inc(r, "ve")
            b.vector(f_l)

        def f_fence(v):
            v.drain()
            ctx.inc(v.nop(), "ve")
        b.vector(f_fence)

        def f_fence_s(s_):
            s_.drain()
            ctx.inc(s_.nop(), "sc")
        b.scalar(f_fence_s)
        bank_guard[4] = ctx.c["ve"]
        bank_guard[5] = ctx.c["ve"]
        return (ctx.c["sc"], ctx.c["ve"])

    def emit_routing(b, q, xt_ready):
        xa_mark = {}

        def f_xa(t):
            ctx.w(t, "sc", xt_ready[0])
            ctx.w(t, "ve", xt_ready[1])
            r = None
            for k in range(KC):
                ph = pwh[:, k * 64:(k + 1) * 64]
                pl = pwl[:, k * 64:(k + 1) * 64]
                xh_sl = xth[:, k * 512:(k + 1) * 512]
                xl_sl = xtl[:, k * 512:(k + 1) * 512]
                t.matmul(PS[4][0:64, 0:512], ph, xh_sl, start=(k == 0), stop=False)
                t.matmul(PS[4][0:64, 0:512], ph, xl_sl, start=False, stop=False)
                r = t.matmul(PS[4][0:64, 0:512], pl, xh_sl, start=False,
                             stop=(k == KC - 1))
            xa_mark[0] = ctx.inc(r, "pe")
        b.tensor(f_xa)

        def f_xaev(s):
            ctx.w(s, "pe", xa_mark[0])
            s.copy(xa_s[:, :], PS[4][0:64, 0:512])
            s.drain()
            xa_mark[1] = ctx.inc(s.nop(), "sc")
        b.scalar(f_xaev)

        if debug:
            def f_dbgxa(g, q=q):
                ctx.w(g, "sc", xa_mark[1])
                ctx.inc(g.dma_start(out=dbg["d_xa"][:, q * 512:(q + 1) * 512],
                                    in_=xa_s[:, :]), "dg", 16)
            b.gpsimd(f_dbgxa)

        zmark = {}

        def f_z(t):
            ctx.w(t, "sc", xa_mark[1])
            r = None
            for tcl in range(TCQ):
                r = t.matmul(PS[5][0:128, tcl * 36:tcl * 36 + 36],
                             xa_s[:, tcl * 128:(tcl + 1) * 128],
                             uallt_s[:, :], start=True, stop=True)
            zmark[0] = ctx.inc(r, "pe")
        b.tensor(f_z)
        rt = {"c3_sc": 0}
        return rt, lambda bb: emit_routing_fin(bb, q, zmark, rt)

    def emit_routing_fin(b, q, zmark, rt):
        def f_zev(s_):
            ctx.w(s_, "pe", zmark[0])
            s_.copy(zq[:, :], PS[5][0:128, 0:144])
            s_.activation(out=PS[6][:, :], in_=PS[6][:, :], func=AF.Copy)
            s_.drain()
            zmark[1] = ctx.inc(s_.nop(), "sc")
        b.scalar(f_zev)

        SQ = 12

        def seg(ap):
            return ap.rearrange("p (s e) -> p s e", e=12)

        m1 = rtg[:, 0 * SQ:1 * SQ]
        m2 = rtg[:, 1 * SQ:2 * SQ]
        m3 = rtg[:, 2 * SQ:3 * SQ]
        d2 = rtg[:, 3 * SQ:4 * SQ]
        d3 = rtg[:, 4 * SQ:5 * SQ]
        e2x = rtg[:, 5 * SQ:6 * SQ]
        e3x = rtg[:, 6 * SQ:7 * SQ]
        den = rtg[:, 7 * SQ:8 * SQ]
        w1 = rtg[:, 8 * SQ:9 * SQ]
        w2 = rtg[:, 9 * SQ:10 * SQ]
        w3 = rtg[:, 10 * SQ:11 * SQ]
        tmark = {}

        def f_t3a(v):
            ctx.w(v, "sc", zmark[1])
            v.tensor_reduce(m1, seg(zq[:, :]), AX.X, ALU.max)
            v.drain()
            v.tensor_tensor(seg(eq1[:, :]), seg(zq[:, :]),
                            m1.broadcast_to([128, SQ, 12]), ALU.is_equal)
            v.drain()
            v.scalar_tensor_tensor(zq[:, :], eq1[:, :], NEG, zq[:, :],
                                   ALU.mult, ALU.add)
            v.drain()
            v.tensor_reduce(m2, seg(zq[:, :]), AX.X, ALU.max)
            v.drain()
            v.tensor_tensor(seg(eq2[:, :]), seg(zq[:, :]),
                            m2.broadcast_to([128, SQ, 12]), ALU.is_equal)
            v.drain()
            v.scalar_tensor_tensor(zq[:, :], eq2[:, :], NEG, zq[:, :],
                                   ALU.mult, ALU.add)
            v.drain()
            v.tensor_reduce(m3, seg(zq[:, :]), AX.X, ALU.max)
            v.drain()
            v.tensor_tensor(seg(eq3[:, :]), seg(zq[:, :]),
                            m3.broadcast_to([128, SQ, 12]), ALU.is_equal)
            v.tensor_tensor(d2, m2, m1, ALU.subtract)
            v.tensor_tensor(d3, m3, m1, ALU.subtract)
            v.drain()
            tmark[0] = ctx.inc(v.nop(), "ve")
        b.vector(f_t3a)

        def f_t3exp(s_):
            ctx.w(s_, "ve", tmark[0])
            s_.activation(out=rtg[:, 5 * SQ:7 * SQ], in_=rtg[:, 3 * SQ:5 * SQ],
                          func=AF.Exp)
            s_.activation(out=PS[6][:, :], in_=PS[6][:, :], func=AF.Copy)
            s_.drain()
            tmark[1] = ctx.inc(s_.nop(), "sc")
        b.scalar(f_t3exp)

        def f_t3b(v, q=q):
            ctx.w(v, "sc", tmark[1])
            cdst = ctok[:, q * TCQ * 36:(q + 1) * TCQ * 36]
            v.tensor_tensor(den, e2x, e3x, ALU.add)
            v.drain()
            v.tensor_scalar(den, den, 1.0, None, ALU.add)
            v.drain()
            v.reciprocal(w1, den)
            v.drain()
            v.tensor_tensor(w2, e2x, w1, ALU.mult)
            v.tensor_tensor(w3, e3x, w1, ALU.mult)
            v.drain()
            v.tensor_tensor(seg(cdst), seg(eq1[:, :]),
                            w1.broadcast_to([128, SQ, 12]), ALU.mult)
            v.tensor_tensor(seg(eq2[:, :]), seg(eq2[:, :]),
                            w2.broadcast_to([128, SQ, 12]), ALU.mult)
            v.drain()
            v.tensor_tensor(cdst, cdst, eq2[:, :], ALU.add)
            v.tensor_tensor(seg(eq3[:, :]), seg(eq3[:, :]),
                            w3.broadcast_to([128, SQ, 12]), ALU.mult)
            v.drain()
            v.tensor_tensor(cdst, cdst, eq3[:, :], ALU.add)
            v.drain()
            ctx.inc(v.nop(), "ve")
        b.vector(f_t3b)

        ctprev = {"ev": 0}
        for tcl in range(TCQ):
            tc = q * TCQ + tcl
            cm = {}

            def f_ct(t, tc=tc, cm=cm):
                ctx.w(t, "ve")
                ctx.w(t, "sc", ctprev["ev"])
                r = t.transpose(PS[5][0:12, 0:128],
                                ctok[:, tc * 36 + 24: tc * 36 + 36],
                                identf_s[:, :])
                cm[0] = ctx.inc(r, "pe")
            b.tensor(f_ct)

            def f_cte(s, tcl=tcl, cm=cm):
                ctx.w(s, "pe", cm[0])
                r = s.copy(c3tq[:, tcl * 128:(tcl + 1) * 128], PS[5][0:12, 0:128])
                ctprev["ev"] = ctx.inc(r, "sc")
            b.scalar(f_cte)
        rt["c3_sc"] = ctprev["ev"]

        if debug:
            def f_dbgc(g, q=q):
                ctx.w(g, "ve")
                ctx.w(g, "sc")
                cols = slice(q * TCQ * 36, (q + 1) * TCQ * 36)
                ctx.inc(g.dma_start(out=dbg["d_C"][:, cols], in_=ctok[:, cols]),
                        "dg", 16)
            b.gpsimd(f_dbgc)

    def emit_mixture_expert(b, q, m, e, buf, wready, xh_t, xl_t, first,
                            seed_sc=0, xt_marks=None, vejobs=(), postjobs=(),
                            c3_wait=None):
        vejobs = list(vejobs)
        postjobs = list(postjobs)
        for tcl in range(TCQ):
            tc = q * TCQ + tcl
            pedone = {}

            def f_mm(t, tcl=tcl, buf=buf, wready=wready, pedone=pedone,
                     xh_t=xh_t, xl_t=xl_t, first=first, seed_sc=seed_sc,
                     xt_marks=xt_marks, c3_wait=c3_wait):
                if tcl == 2 and c3_wait is not None:
                    ctx.w(t, "sc", c3_wait["c3_sc"])
                if tcl == 0:
                    ctx.w(t, "dg", wready["dg"])
                    ctx.w(t, "ve", wready["marks"][3])
                    if seed_sc:
                        ctx.w(t, "sc", seed_sc)
                    if xt_marks is not None:
                        ctx.w(t, "sc", xt_marks[0])
                        ctx.w(t, "ve", xt_marks[1])
                if first and tcl == 0:
                    ctx.w(t, "sc")
                ctx.w(t, "ve", max(bank_guard[(tcl % 3) * 2],
                                   bank_guard[(tcl % 3) * 2 + 1]))
                r = None
                for k in range(KC):
                    xh_sl = xh_t[:, k * 512 + tcl * 128: k * 512 + (tcl + 1) * 128]
                    xl_sl = xl_t[:, k * 512 + tcl * 128: k * 512 + (tcl + 1) * 128]
                    for oh in range(2):
                        bank = PS[(tcl % 3) * 2 + oh]
                        wh_sl = wbh[buf][:, k * D + oh * 512: k * D + (oh + 1) * 512]
                        wl_sl = wbl[buf][:, k * D + oh * 512: k * D + (oh + 1) * 512]
                        t.matmul(bank[:, :], xh_sl, wh_sl,
                                 start=(k == 0), stop=False)
                        t.matmul(bank[:, :], xh_sl, wl_sl, start=False, stop=False)
                        r = t.matmul(bank[:, :], xl_sl, wh_sl, start=False,
                                     stop=(k == KC - 1))
                pedone[0] = ctx.inc(r, "pe")
            b.tensor(f_mm)

            def f_ep(v, tcl=tcl, tc=tc, e=e, m=m, first=first, pedone=pedone,
                     seed_sc=seed_sc):
                ctx.w(v, "pe", pedone[0])
                if first and m == 1:
                    ctx.w(v, "sc", seed_sc)
                r = None
                for oh in range(2):
                    bank = PS[(tcl % 3) * 2 + oh]
                    ysl = yq[:, tcl * D + oh * 512: tcl * D + (oh + 1) * 512]
                    cc = ctok[:, tc * 36 + m * 12 + e: tc * 36 + m * 12 + e + 1]
                    if first and m == 0:
                        r = v.tensor_scalar(ysl, bank[:, :], cc, None, ALU.mult)
                    else:
                        r = v.scalar_tensor_tensor(ysl, bank[:, :], cc, ysl,
                                                   ALU.mult, ALU.add)
                n = ctx.inc(r, "ve")
                bank_guard[(tcl % 3) * 2] = n
                bank_guard[(tcl % 3) * 2 + 1] = n
            if tcl < len(vejobs):
                vejobs[tcl](b)
            b.vector(f_ep)
            if tcl < len(postjobs):
                postjobs[tcl](b)

    def emit_head(b, q):
        xt_ready = emit_transpose_split(b, xln, q * TCQ * D, xth, xtl)
        rt, fin = emit_routing(b, q, xt_ready)
        return xt_ready, rt, fin

    def emit_tail(b, q):
        r2_ve = {}

        def f_r2(v, q=q):
            for tcl in range(TCQ):
                tc = q * TCQ + tcl
                xsl = xln[:, tc * D:(tc + 1) * D]
                v.scalar_tensor_tensor(xsl, yq[:, tcl * D:(tcl + 1) * D], 1.0,
                                       xsl, ALU.mult, ALU.add)
            v.drain()
            for tcl in range(TCQ):
                tc = q * TCQ + tcl
                xsl = xln[:, tc * D:(tc + 1) * D]
                v.bn_stats(bnb[:, tcl * 12:tcl * 12 + 6], xsl[:, 0:512])
                v.bn_stats(bnb[:, tcl * 12 + 6:tcl * 12 + 12], xsl[:, 512:1024])
            v.drain()
            for tcl in range(TCQ):
                v.bn_aggr(mvv[:, 2 * tcl:2 * tcl + 2],
                          bnb[:, tcl * 12:(tcl + 1) * 12])
            v.drain()
            for tcl in range(TCQ):
                v.tensor_scalar(rv[:, tcl:tcl + 1],
                                mvv[:, 2 * tcl + 1:2 * tcl + 2],
                                EPS_LN, None, ALU.add)
            v.drain()
            r2_ve[TCQ - 1] = ctx.inc(v.nop(), "ve")
        b.vector(f_r2)

        def f_rstd2(s_):
            ctx.w(s_, "ve", r2_ve[TCQ - 1])
            s_.activation(out=rv[:, 0:TCQ], in_=rv[:, 0:TCQ], func=AF.Ln)
            s_.drain()
            s_.activation(out=rv[:, 0:TCQ], in_=rv[:, 0:TCQ], func=AF.Exp,
                          scale=-0.5)
            s_.drain()
            ctx.inc(s_.nop(), "sc")
        b.scalar(f_rstd2)
        rstd2_sc = ctx.c["sc"]

        for tcl in range(TCQ):
            tc = q * TCQ + tcl
            out_m = {}

            def f_ap2(v, tcl=tcl, tc=tc, out_m=out_m):
                if tcl == 0:
                    ctx.w(v, "sc", rstd2_sc)
                xsl = xln[:, tc * D:(tc + 1) * D]
                v.tensor_scalar(xsl, xsl, mvv[:, 2 * tcl:2 * tcl + 1],
                                rv[:, tcl:tcl + 1], ALU.subtract, ALU.mult)
                v.drain()
                out_m[0] = ctx.inc(v.nop(), "ve")
            b.vector(f_ap2)

            def f_st(g, tcl=tcl, tc=tc, out_m=out_m):
                ctx.w(g, "ve", out_m[0])
                ctx.inc(g.dma_start(out=out_d[tc * 128:(tc + 1) * 128, :],
                                    in_=xln[:, tc * D:(tc + 1) * D]), "so", 16)
            b.gpsimd(f_st)

    with nc.Block() as b:
        xt_ready_q = emit_head(b, 0)
        pre_w1 = {0: emit_wload(b, w1t_d, 0, 0)}
        for q in range(NQ):
            wready = pre_w1
            w2e0 = {}
            gelu_sc = {}
            xt_marks_q, rt_q, fin_q = xt_ready_q
            for e in range(NE):
                buf = e % 2
                jobs = ()
                if e + 1 < NE:
                    wready[e + 1] = emit_wload(b, w1t_d, e + 1, (e + 1) % 2,
                                               defer_subs=True)
                    jobs = wready[e + 1]["jobs"]
                    if e == 0:
                        j0o = jobs[0]
                        jobs = [lambda bb, f=fin_q, g=j0o: (f(bb), g(bb))] \
                            + list(jobs[1:])
                else:
                    w2e0[0] = emit_wload(b, w2t_d, 0, 0, defer_subs=True)
                    wjobs = w2e0[0]["jobs"]

                    def mkgelu(tcl):
                        def j(bb):
                            def f_vf(v):
                                v.drain()
                                ctx.inc(v.nop(), "ve")
                            bb.vector(f_vf)

                            def f_g(s_, tcl=tcl):
                                ctx.w(s_, "ve")
                                r = s_.activation(
                                    out=yq[:, tcl * D:(tcl + 1) * D],
                                    in_=yq[:, tcl * D:(tcl + 1) * D],
                                    func=AF.Gelu)
                                gelu_sc[0] = ctx.inc(r, "sc")
                            bb.scalar(f_g)
                        return j
                    pjobs = [mkgelu(i) for i in range(4)]
                    jobs = wjobs
                emit_mixture_expert(b, q, 0, e, buf, wready[e], xth, xtl,
                                    first=(e == 0), vejobs=jobs,
                                    postjobs=(pjobs if e + 1 == NE else ()),
                                    xt_marks=(xt_marks_q if e == 0 else None),
                                    c3_wait=(rt_q if e == 0 else None))
                wbuf_guard[buf] = ctx.c["pe"]

            if debug:
                def f_dbgy1(g, q=q):
                    ctx.w(g, "ve")
                    for tcl in range(TCQ):
                        tc = q * TCQ + tcl
                        ctx.inc(g.dma_start(
                            out=dbg["d_y1"][tc * 128:(tc + 1) * 128, :],
                            in_=yq[:, tcl * D:(tcl + 1) * D]), "dg", 16)
                b.gpsimd(f_dbgy1)

            def f_wait_gelu(t):
                ctx.w(t, "sc", gelu_sc[0])
            b.tensor(f_wait_gelu)

            ht_ready = emit_transpose_split(b, yq, 0, hth, htl)
            h_done_pe = ctx.c["pe"]

            seed_fin = {}
            for tcl in range(TCQ):
                sm = {}

                def f_seed(t, tcl=tcl, sm=sm):
                    if tcl == 0:
                        ctx.w(t, "sc")
                    for oh in range(2):
                        ctx.w(t, "ve", bank_guard[(tcl % 2) * 2 + oh])
                    r = None
                    for oh in range(2):
                        r = t.matmul(PS[(tcl % 2) * 2 + oh][:, :],
                                     c3tq[:, tcl * 128:(tcl + 1) * 128],
                                     b2_s[:, oh * 512:(oh + 1) * 512],
                                     start=True, stop=True)
                    sm[0] = ctx.inc(r, "pe")
                b.tensor(f_seed)

                def f_seedev(s_, tcl=tcl, sm=sm):
                    ctx.w(s_, "pe", sm[0])
                    ctx.w(s_, "pe", h_done_pe)
                    r = None
                    for oh in range(2):
                        r = s_.copy(yq[:, tcl * D + oh * 512:
                                       tcl * D + (oh + 1) * 512],
                                    PS[(tcl % 2) * 2 + oh][:, :])
                    seed_fin[tcl] = ctx.inc(r, "sc")
                b.scalar(f_seedev)
            seed_all = seed_fin[TCQ - 1]

            wready2 = w2e0
            pre_w1 = {}
            for e in range(NE):
                buf = e % 2
                jobs = ()
                if e + 1 < NE:
                    wready2[e + 1] = emit_wload(b, w2t_d, e + 1, (e + 1) % 2,
                                                defer_subs=True)
                    jobs = wready2[e + 1]["jobs"]
                elif q + 1 < NQ:
                    pre_w1[0] = emit_wload(b, w1t_d, 0, 0, defer_subs=True)
                    jobs = pre_w1[0]["jobs"]
                emit_mixture_expert(b, q, 1, e, buf, wready2[e], hth, htl,
                                    first=(e == 0), seed_sc=seed_all,
                                    xt_marks=(ht_ready if e == 0 else None),
                                    vejobs=jobs)
                wbuf_guard[buf] = ctx.c["pe"]

            if debug:
                def f_dbgy2(g, q=q):
                    ctx.w(g, "ve")
                    for tcl in range(TCQ):
                        tc = q * TCQ + tcl
                        ctx.inc(g.dma_start(
                            out=dbg["d_y2"][tc * 128:(tc + 1) * 128, :],
                            in_=yq[:, tcl * D:(tcl + 1) * D]), "dg", 16)
                b.gpsimd(f_dbgy2)

            if q + 1 < NQ:
                xt_ready_q = emit_head(b, q + 1)
            emit_tail(b, q)

    with nc.Block() as b:
        def f_fin(g):
            g.wait_ge(ctx.sem["so"], 16 * TC)
        b.gpsimd(f_fin)

    nc._keepalive = (es0, es2, es2b)
    return nc



def _rope_tables():
    pos = np.arange(T, dtype=np.float32)
    inv = (np.float32(32.0) ** (-(np.arange(0, D, 2).astype(np.float32))
                                / np.float32(D))).astype(np.float32)
    ang = pos[:, None] * inv[None, :]
    emb = np.concatenate([ang, ang], axis=1)
    cos = np.cos(emb).astype(np.float32)
    sin = np.sin(emb).astype(np.float32)
    cosT = np.ascontiguousarray(cos.T).astype(BFD)
    sinS = np.ascontiguousarray(sin.T)
    sinS[:D // 2, :] *= np.float32(-1.0)
    sinST = sinS.astype(BFD)
    return cosT, sinST


def make_in_maps(inputs):
    hs = np.asarray(inputs["L0_attn_hidden_states"], dtype=np.float32)
    Wq = np.asarray(inputs["L0_attn_Wq"], dtype=np.float32)
    Wk = np.asarray(inputs["L0_attn_Wk"], dtype=np.float32)
    fc1w = np.asarray(inputs["L0_attn_fc1_w"], dtype=np.float32)
    fc1b = np.asarray(inputs["L0_attn_fc1_b"], dtype=np.float32)
    fc2w = np.asarray(inputs["L0_attn_fc2_w"], dtype=np.float32)
    fc2b = np.asarray(inputs["L0_attn_fc2_b"], dtype=np.float32)
    Pw = np.asarray(inputs["L0_ffn_P_w"], dtype=np.float32)
    U1 = np.asarray(inputs["L0_ffn_U1"], dtype=np.float32)
    U2 = np.asarray(inputs["L0_ffn_U2"], dtype=np.float32)
    U3 = np.asarray(inputs["L0_ffn_U3"], dtype=np.float32)
    W1 = np.asarray(inputs["L0_ffn_W1"], dtype=np.float32)
    W2 = np.asarray(inputs["L0_ffn_W2"], dtype=np.float32)
    b2 = np.asarray(inputs["L0_ffn_b2"], dtype=np.float32)

    cosT, sinST = _rope_tables()
    shared = {
        "wqt": np.ascontiguousarray(Wq.T),
        "wkt": np.ascontiguousarray(Wk.T),
        "fc1wt": np.ascontiguousarray(
            np.concatenate([fc1w.T, np.zeros((1, 1), np.float32)], axis=1)),
        "fc1b": np.ascontiguousarray(
            np.concatenate([fc1b, np.ones((1,), np.float32)])[:, None]),
        "fc2wta": np.ascontiguousarray(
            np.concatenate([fc2w.T, fc2b[None, :]], axis=0)),
        "pwt": np.ascontiguousarray(Pw.T),
        "uallt": np.ascontiguousarray(
            np.concatenate([U1.T, U2.T, U3.T], axis=1)),
        "b2": np.ascontiguousarray(b2),
        "w1t": np.ascontiguousarray(
            W1.transpose(0, 2, 1).reshape(NE, 4, 2, 128, D)
            .transpose(0, 1, 3, 2, 4).reshape(NE, 4, 128, 2 * D)),
        "w2t": np.ascontiguousarray(
            W2.transpose(0, 2, 1).reshape(NE, 4, 2, 128, D)
            .transpose(0, 1, 3, 2, 4).reshape(NE, 4, 128, 2 * D)),
        "costab": cosT,
        "sintab": sinST,
        "trimask": np.triu(np.full((128, 128), np.float32(NEG),
                                   dtype=np.float32), k=1),
        "identf": np.eye(128, dtype=np.float32),
    }
    in_maps = []
    for c in range(8):
        m = dict(shared)
        m["x0"] = np.ascontiguousarray(hs[c])
        m["x0t"] = np.ascontiguousarray(hs[c].T)
        in_maps.append(m)
    return in_maps


def kernel(**inputs):
    in_maps = make_in_maps(inputs)
    nc = build_nc(debug=False)
    res = run_bass_kernel_spmd(nc, in_maps, core_ids=list(range(8)))
    out = np.stack([np.asarray(res.results[c]["out"]) for c in range(8)], axis=0)
    return out.astype(np.float32)


# version 33
# speedup vs baseline: 1.0450x; 1.0450x over previous
import numpy as np
import ml_dtypes
from contextlib import ExitStack
import concourse.bass as bass
import concourse.mybir as mybir
from concourse.bass_utils import run_bass_kernel_spmd

F32 = mybir.dt.float32
BF16 = mybir.dt.bfloat16
AF = mybir.ActivationFunctionType
ALU = mybir.AluOpType
AX = mybir.AxisListType
BFD = ml_dtypes.bfloat16

T = 2048
D = 1024
TC = 16
KC = 8
NE = 12
NQ = 4
TCQ = TC // NQ
NEG = -1.0e30
EPS_LN = 1e-5

ENGS = ("tensor", "vector", "scalar", "sync", "gpsimd")
_SEM_OF = {"tensor": "pe", "vector": "ve", "scalar": "sc",
           "sync": "ds", "gpsimd": "dg"}


class Ctx:
    def __init__(self, nc, es):
        self.nc = nc
        self.sem = {}
        self.c = {}
        for n in ("pe", "ve", "sc", "ds", "dg", "dv", "so"):
            self.sem[n] = es.enter_context(nc.semaphore("S" + n))
            self.c[n] = 0

    def inc(self, instr, name, k=1):
        instr.then_inc(self.sem[name], k)
        self.c[name] += k
        return self.c[name]

    def w(self, eng, name, val=None):
        v = self.c[name] if val is None else val
        if v > 0:
            eng.wait_ge(self.sem[name], v)


def _barrier(nc, ctx):
    with nc.Block() as b:
        for e in ENGS:
            def f(eng, e=e):
                eng.drain()
                nm = _SEM_OF[e]
                ctx.inc(eng.nop(), nm, 16 if nm in ("ds", "dg") else 1)
            getattr(b, e)(f)
    snap = dict(ctx.c)
    with nc.Block() as b:
        for e in ENGS:
            def f(eng):
                for nm in ("pe", "ve", "sc", "ds", "dg"):
                    ctx.w(eng, nm, snap[nm])
            getattr(b, e)(f)


def build_nc(debug=False):
    nc = bass.Bass("TRN2", target_bir_lowering=False)

    def din(name, shape, dt=F32):
        return nc.dram_tensor(name, shape, dt, kind="ExternalInput").ap()

    x0_d = din("x0", [T, D])
    x0t_d = din("x0t", [D, T])
    wqt_d = din("wqt", [D, D])
    wkt_d = din("wkt", [D, D])
    fc1wt_d = din("fc1wt", [1, 17])
    fc1b_d = din("fc1b", [17, 1])
    fc2wta_d = din("fc2wta", [17, D])
    pwt_d = din("pwt", [D, 64])
    uallt_d = din("uallt", [64, 36])
    b2_d = din("b2", [NE, D])
    w1t_d = din("w1t", [NE, D, D])
    w2t_d = din("w2t", [NE, D, D])
    costab_d = din("costab", [D, T], BF16)
    sintab_d = din("sintab", [D, T], BF16)
    trimask_d = din("trimask", [128, 128])
    identf_d = din("identf", [128, 128])
    out_d = nc.dram_tensor("out", [T, D], F32, kind="ExternalOutput").ap()

    dbg = {}
    if debug:
        for nm, shape, dt in [
            ("d_q", [D, T], BF16), ("d_k", [D, T], BF16),
            ("d_s", [128, TC], F32), ("d_resid1", [T, D], F32),
            ("d_xln", [T, D], F32), ("d_xa", [64, T], F32),
            ("d_C", [128, TC * 36], F32),
            ("d_y1", [T, D], F32), ("d_y2", [T, D], F32),
        ]:
            dbg[nm] = nc.dram_tensor(nm, shape, dt, kind="ExternalOutput").ap()

    es0 = ExitStack()
    ctx = Ctx(nc, es0)

    def sb(esx, name, shape, dt):
        return esx.enter_context(nc.sbuf_tensor(name, shape, dt))

    smat = sb(es0, "smat", [128, TC], F32)
    fc1wt_s = sb(es0, "fc1wt_s", [1, 17], F32)
    fc1b_s = sb(es0, "fc1b_s", [17, 1], F32)
    identf_s = sb(es0, "identf_s", [128, 128], F32)
    ctok = sb(es0, "ctok", [128, TC * 36], F32)
    bnb = sb(es0, "bnb", [128, 8 * 12], F32)
    mvv = sb(es0, "mvv", [128, 2 * TC], F32)
    rv = sb(es0, "rv", [128, TC], F32)
    b2_s = sb(es0, "b2_s", [NE, D], F32)
    uallt_s = sb(es0, "uallt_s", [64, 36], F32)
    zq = sb(es0, "zq", [128, 144], F32)
    eq1 = sb(es0, "eq1", [128, 144], BF16)
    eq2 = sb(es0, "eq2", [128, 144], BF16)
    eq3 = sb(es0, "eq3", [128, 144], BF16)
    rtg = sb(es0, "rtg", [128, 144], F32)

    PS = [es0.enter_context(nc.psum_tensor(f"ps{i}", [128, 512], F32))
          for i in range(8)]

    with nc.Block() as b:
        def f(s):
            for tgt, src in [(fc1wt_s, fc1wt_d), (fc1b_s, fc1b_d),
                             (identf_s, identf_d), (b2_s, b2_d),
                             (uallt_s, uallt_d)]:
                ctx.inc(s.dma_start(out=tgt[:, :], in_=src[:, :]), "ds", 16)
        b.sync(f)

    es2 = ExitStack()
    xln = sb(es2, "xln", [128, TC * D], F32)

    es1 = ExitStack()
    qTb = sb(es1, "qTb", [128, KC * T], BF16)
    kTb = sb(es1, "kTb", [128, KC * T], BF16)
    trimask_s = sb(es1, "trimask_s", [128, 128], F32)

    es1a = ExitStack()
    x0th = xln[:, 0:KC * T // 2].bitcast(BF16)
    x0tl = xln[:, KC * T // 2:KC * T].bitcast(BF16)
    wsh = sb(es1a, "wsh", [128, KC * D], BF16)
    wsl = sb(es1a, "wsl", [128, KC * D], BF16)
    stg = [sb(es1a, "stg0", [128, T], F32)]
    stgx = sb(es1a, "stgx", [128, T], F32)
    tabs = [sb(es1a, f"tab{i}", [128, T], BF16) for i in range(4)]
    rsc = [stg[0][:, 0:1024].bitcast(BF16), stg[0][:, 1024:2048].bitcast(BF16),
           stgx[:, 0:1024].bitcast(BF16)]

    with nc.Block() as b:
        xsp = {}
        wsp_q = {}
        xslot = [stgx, qTb[:, 0:2 * T].bitcast(F32)]

        def emit_xload(bb, k):
            qn = "dg" if k % 2 == 0 else "dv"

            def f(eng, k=k, qn=qn):
                if k >= 2:
                    ctx.w(eng, "ve", xsp[k - 2][1])
                ctx.inc(eng.dma_start(out=xslot[k % 2][:, :],
                                      in_=x0t_d[k * 128:(k + 1) * 128, :]),
                        qn, 16)
            (bb.gpsimd if k % 2 == 0 else bb.scalar)(f)
            xsp[k] = [0, 0, (qn, ctx.c[qn])]

        def emit_wqload(bb, k):
            def f(s, k=k):
                if k >= 2:
                    ctx.w(s, "ve", wsp_q[k - 2][1])
                ctx.inc(s.dma_start(out=stg[0][:, (k % 2) * D:(k % 2 + 1) * D],
                                    in_=wqt_d[k * 128:(k + 1) * 128, :]),
                        "ds", 16)
            bb.sync(f)
            wsp_q[k] = [0, 0, ctx.c["ds"]]

        emit_xload(b, 0)
        emit_xload(b, 1)
        emit_wqload(b, 0)
        emit_wqload(b, 1)
        for k in range(KC):
            def f_hi(s, k=k):
                qn, cv = xsp[k][2]
                ctx.w(s, qn, cv)
                r = s.copy(x0th[:, k * T:(k + 1) * T], xslot[k % 2][:, :])
                xsp[k][0] = ctx.inc(r, "sc")
            b.scalar(f_hi)

            def f_whi(s, k=k):
                ctx.w(s, "ds", wsp_q[k][2])
                r = s.copy(wsh[:, k * D:(k + 1) * D],
                           stg[0][:, (k % 2) * D:(k % 2 + 1) * D])
                wsp_q[k][0] = ctx.inc(r, "sc")
            b.scalar(f_whi)

            def f_lo(v, k=k):
                ctx.w(v, "sc", xsp[k][0])
                r = v.tensor_tensor(x0tl[:, k * T:(k + 1) * T], xslot[k % 2][:, :],
                                    x0th[:, k * T:(k + 1) * T], ALU.subtract)
                xsp[k][1] = ctx.inc(r, "ve")
            b.vector(f_lo)

            def f_wlo(v, k=k):
                ctx.w(v, "sc", wsp_q[k][0])
                r = v.tensor_tensor(wsl[:, k * D:(k + 1) * D],
                                    stg[0][:, (k % 2) * D:(k % 2 + 1) * D],
                                    wsh[:, k * D:(k + 1) * D], ALU.subtract)
                wsp_q[k][1] = ctx.inc(r, "ve")
            b.vector(f_wlo)
            if k + 2 < KC:
                emit_xload(b, k + 2)
                emit_wqload(b, k + 2)
        xsp_sc, xsp_ve = ctx.c["sc"], ctx.c["ve"]
        xlo_marks = {k: xsp[k][1] for k in range(KC)}
        wqlo_marks = {k: wsp_q[k][1] for k in range(KC)}

        def f_tm(s):
            ctx.inc(s.dma_start(out=trimask_s[:, :], in_=trimask_d[:, :]), "ds", 16)
        b.sync(f_tm)

        rope_ve = {}
        evacs = {}

        def emit_rope_pair(pair, evq, evk, scw=0):
            c, c4 = pair, pair + 4

            def f_g(g, pair=pair, c=c, c4=c4, scw=scw):
                if scw:
                    ctx.w(g, "sc", scw)
                if pair >= 1:
                    ctx.w(g, "ve", rope_ve[pair - 1])
                for i, (tsrc, cc) in enumerate([(costab_d, c), (sintab_d, c),
                                                (costab_d, c4), (sintab_d, c4)]):
                    ctx.inc(g.dma_start(out=tabs[i][:, :],
                                        in_=tsrc[cc * 128:(cc + 1) * 128, :]),
                            "dg", 16)
            b.gpsimd(f_g)
            dg_p = ctx.c["dg"]

            def f_v(v, pair=pair, c=c, c4=c4, dg_p=dg_p, evq=evq, evk=evk):
                ctx.w(v, "dg", dg_p)
                ctx.w(v, "sc", max(evq, evk))
                r = None
                for tens in (qTb, kTb):
                    a = tens[:, c * T:(c + 1) * T]
                    a4 = tens[:, c4 * T:(c4 + 1) * T]
                    v.tensor_tensor(rsc[0], a, tabs[0][:, :], ALU.mult)
                    v.tensor_tensor(rsc[1], a4, tabs[1][:, :], ALU.mult)
                    v.tensor_tensor(rsc[2], a, tabs[3][:, :], ALU.mult)
                    v.tensor_tensor(a, rsc[0], rsc[1], ALU.add)
                    v.tensor_tensor(rsc[0], a4, tabs[2][:, :], ALU.mult)
                    r = v.tensor_tensor(a4, rsc[0], rsc[2], ALU.add)
                rope_ve[pair] = ctx.inc(r, "ve")
            b.vector(f_v)

        for proj, (w_dram, outb) in enumerate([(wqt_d, qTb), (wkt_d, kTb)]):
            evac = {}
            evacs[proj] = evac
            pe_guard = ctx.c["pe"]
            if proj == 0:
                wsp = wsp_q
            else:
                wsp = {}
                kslots = [stg[0][:, 0:D], stg[0][:, D:2 * D],
                          stgx[:, 0:D], stgx[:, D:2 * D],
                          tabs[0][:, 0:2 * D].bitcast(F32),
                          tabs[1][:, 0:2 * D].bitcast(F32)]
                kguard = [wqlo_marks[6], wqlo_marks[7], xlo_marks[6],
                          xlo_marks[6], 0, 0]

                def emit_wkload(bb, k):
                    qn = "ds" if k % 2 == 0 else "dg"

                    def f(eng, k=k, qn=qn):
                        if k < 6:
                            ctx.w(eng, "ve", kguard[k])
                        else:
                            ctx.w(eng, "ve", wsp[k - 6][1])
                        ctx.inc(eng.dma_start(
                            out=kslots[k % 6],
                            in_=w_dram[k * 128:(k + 1) * 128, :]), qn, 16)
                    (bb.sync if k % 2 == 0 else bb.gpsimd)(f)
                    wsp[k] = [0, 0, (qn, ctx.c[qn])]

                for k in range(6):
                    emit_wkload(b, k)
                for k in range(KC):
                    def f_whik(s, k=k):
                        qn, cv = wsp[k][2]
                        ctx.w(s, qn, cv)
                        if k == 0:
                            ctx.w(s, "pe", pe_guard)
                        r = s.copy(wsh[:, k * D:(k + 1) * D], kslots[k % 6])
                        wsp[k][0] = ctx.inc(r, "sc")
                    b.scalar(f_whik)

                    def f_wlok(v, k=k):
                        ctx.w(v, "sc", wsp[k][0])
                        r = v.tensor_tensor(wsl[:, k * D:(k + 1) * D],
                                            kslots[k % 6],
                                            wsh[:, k * D:(k + 1) * D],
                                            ALU.subtract)
                        wsp[k][1] = ctx.inc(r, "ve")
                    b.vector(f_wlok)
                    if k + 6 < KC:
                        emit_wkload(b, k + 6)

            for oc in range(8):
                for tb in range(4):
                    idx = oc * 4 + tb
                    bank = PS[idx % 2]
                    pe_t = {}

                    def f_pe(t, oc=oc, tb=tb, idx=idx, bank=bank, pe_t=pe_t,
                             wsp=wsp, evac=evac):
                        if idx == 0:
                            ctx.w(t, "sc", wsp[KC - 1][0])
                            ctx.w(t, "ve", wsp[KC - 1][1])
                            ctx.w(t, "sc", xsp_sc)
                            ctx.w(t, "ve", xsp_ve)
                        if idx >= 2:
                            ctx.w(t, "sc", evac[idx - 2])
                        r = None
                        for k in range(KC):
                            wh_sl = wsh[:, k * D + oc * 128: k * D + (oc + 1) * 128]
                            wl_sl = wsl[:, k * D + oc * 128: k * D + (oc + 1) * 128]
                            xh_sl = x0th[:, k * T + tb * 512: k * T + (tb + 1) * 512]
                            xl_sl = x0tl[:, k * T + tb * 512: k * T + (tb + 1) * 512]
                            t.matmul(bank[:, :], wh_sl, xh_sl,
                                     start=(k == 0), stop=False)
                            t.matmul(bank[:, :], wh_sl, xl_sl,
                                     start=False, stop=False)
                            r = t.matmul(bank[:, :], wl_sl, xh_sl,
                                         start=False, stop=(k == KC - 1))
                        pe_t[0] = ctx.inc(r, "pe")
                    b.tensor(f_pe)

                    def f_ev(s, oc=oc, tb=tb, idx=idx, bank=bank, outb=outb,
                             pe_t=pe_t, evac=evac):
                        ctx.w(s, "pe", pe_t[0])
                        r = s.copy(outb[:, oc * T + tb * 512: oc * T + (tb + 1) * 512],
                                   bank[:, :])
                        evac[idx] = ctx.inc(r, "sc")
                    b.scalar(f_ev)
                    if proj == 1 and oc >= 4 and tb == 3:
                        pair = oc - 4
                        emit_rope_pair(pair,
                                       max(evacs[0].values()),
                                       evac[oc * 4 + 3],
                                       scw=(wsp[5][0] if pair == 0 else 0))

    _barrier(nc, ctx)
    es1a.close()


    es2a = ExitStack()
    relu16 = sb(es2a, "relu16", [17, T], F32)
    srow = sb(es2a, "srow", [1, T], F32)
    fc2wta_s = sb(es2a, "fc2wta_s", [17, D], F32)
    es1c = ExitStack()
    expbuf2 = [sb(es1c, f"expbuf{i}", [128, T], F32) for i in range(2)]

    with nc.Block() as b:
        x0_dg = {}

        def f_gx(g):
            for tc in range(TC):
                r = g.dma_start(out=xln[:, tc * D:(tc + 1) * D],
                                in_=x0_d[tc * 128:(tc + 1) * 128, :])
                x0_dg[tc] = ctx.inc(r, "dg", 16)
        b.gpsimd(f_gx)

        def f_sw(s):
            ctx.inc(s.dma_start(out=fc2wta_s[:, :], in_=fc2wta_d[:, :]), "ds", 16)
        b.sync(f_sw)
        fc2_ds = ctx.c["ds"]

        bank_ready = {}
        red_ve = {}
        sr_sc = {}
        r16_sc = {}
        last_ao = {"ve": 0}
        ln_done = {}
        for qc in range(TC):
            nblk = qc // 4 + 1
            valid = (qc + 1) * 128
            rem = valid - (nblk - 1) * 512
            pe_q = {}
            b0 = (qc % 2) * 4
            expbuf = expbuf2[qc % 2]

            def f_pe(t, qc=qc, nblk=nblk, rem=rem, pe_q=pe_q, b0=b0):
                for jb in range(nblk):
                    if (b0 + jb) in bank_ready:
                        nm, val = bank_ready[b0 + jb]
                        ctx.w(t, nm, val)
                for dk in range(KC):
                    r = None
                    for jb in range(nblk):
                        N = 512 if jb < nblk - 1 else rem
                        r = t.matmul(PS[b0 + jb][:, 0:N],
                                     qTb[:, dk * T + qc * 128: dk * T + (qc + 1) * 128],
                                     kTb[:, dk * T + jb * 512: dk * T + jb * 512 + N],
                                     start=(dk == 0), stop=(dk == KC - 1))
                    if dk == KC - 1:
                        pe_q[0] = ctx.inc(r, "pe")
            b.tensor(f_pe)

            def f_vm(v, qc=qc, nblk=nblk, rem=rem, pe_q=pe_q, b0=b0):
                ctx.w(v, "pe", pe_q[0])
                r = v.tensor_tensor(PS[b0 + nblk - 1][:, rem - 128:rem],
                                    PS[b0 + nblk - 1][:, rem - 128:rem],
                                    trimask_s[:, :], ALU.add)
                pe_q[1] = ctx.inc(r, "ve")
            b.vector(f_vm)

            def f_sc(s, qc=qc, nblk=nblk, rem=rem, pe_q=pe_q, b0=b0,
                     expbuf=expbuf):
                ctx.w(s, "pe", pe_q[0])
                ctx.w(s, "ve", pe_q[1])
                if qc >= 2:
                    ctx.w(s, "ve", red_ve[qc - 2])
                for jb in range(nblk):
                    N = 512 if jb < nblk - 1 else rem
                    r = s.activation(out=expbuf[:, jb * 512: jb * 512 + N],
                                     in_=PS[b0 + jb][:, 0:N], func=AF.Exp,
                                     scale=1.0 / 32.0)
                    bank_ready[b0 + jb] = ("sc", ctx.inc(r, "sc"))
            b.scalar(f_sc)

            def f_red(v, qc=qc, valid=valid, expbuf=expbuf):
                ctx.w(v, "sc")
                r = v.reduce_sum(smat[:, qc:qc + 1], expbuf[:, 0:valid], axis=AX.X)
                red_ve[qc] = ctx.inc(r, "ve")
            b.vector(f_red)

            if qc % 4 == 3:
                g4 = qc // 4
                Tb = (g4 % 2) * 4 + 3
                tr = {}

                def f_trg(t, g4=g4, Tb=Tb, tr=tr):
                    ctx.w(t, "ve", red_ve[4 * g4 + 3])
                    if Tb in bank_ready:
                        nm, val = bank_ready[Tb]
                        ctx.w(t, nm, val)
                    r = None
                    for j in range(4):
                        qq = g4 * 4 + j
                        r = t.transpose(PS[Tb][0:1, j * 128:(j + 1) * 128],
                                        smat[:, qq:qq + 1], identf_s[:, :])
                    tr[0] = ctx.inc(r, "pe")
                b.tensor(f_trg)

                def f_srg(s, g4=g4, Tb=Tb, tr=tr):
                    ctx.w(s, "pe", tr[0])
                    s.copy(srow[0:1, g4 * 512:(g4 + 1) * 512], PS[Tb][0:1, 0:512])
                    s.drain()
                    sr_sc[g4] = ctx.inc(s.nop(), "sc")
                b.scalar(f_srg)

                r16 = {}

                def f_r16(t, g4=g4, Tb=Tb, r16=r16):
                    ctx.w(t, "sc", sr_sc[g4])
                    r = t.matmul(PS[Tb][0:17, 0:512], fc1wt_s[0:1, :],
                                 srow[0:1, g4 * 512:(g4 + 1) * 512],
                                 start=True, stop=True)
                    r16[0] = ctx.inc(r, "pe")
                b.tensor(f_r16)

                def f_r16s(s, g4=g4, Tb=Tb, r16=r16):
                    ctx.w(s, "pe", r16[0])
                    s.activation(out=relu16[0:17, g4 * 512:(g4 + 1) * 512],
                                 in_=PS[Tb][0:17, 0:512], func=AF.Relu,
                                 bias=fc1b_s[:, 0:1])
                    s.drain()
                    r16_sc[g4] = ctx.inc(s.nop(), "sc")
                b.scalar(f_r16s)

                for j in range(4):
                    tc = g4 * 4 + j
                    for oh in range(2):
                        ao = {}

                        def f_ao(t, tc=tc, oh=oh, g4=g4, Tb=Tb, ao=ao, j=j):
                            if j == 0 and oh == 0:
                                ctx.w(t, "sc", r16_sc[g4])
                                ctx.w(t, "ds", fc2_ds)
                            else:
                                ctx.w(t, "ve", last_ao["ve"])
                            r = t.matmul(PS[Tb][:, :],
                                         relu16[:, tc * 128:(tc + 1) * 128],
                                         fc2wta_s[:, oh * 512:(oh + 1) * 512],
                                         start=True, stop=True)
                            ao[0] = ctx.inc(r, "pe")
                        b.tensor(f_ao)

                        def f_aov(v, tc=tc, oh=oh, Tb=Tb, ao=ao):
                            ctx.w(v, "pe", ao[0])
                            ctx.w(v, "dg", x0_dg[tc])
                            sl = xln[:, tc * D + oh * 512: tc * D + (oh + 1) * 512]
                            r = v.scalar_tensor_tensor(sl, PS[Tb][:, :], 1.0, sl,
                                                       ALU.mult, ALU.add)
                            last_ao["ve"] = ctx.inc(r, "ve")
                        b.vector(f_aov)

                    def f_lns(v, tc=tc):
                        v.drain()
                        tl = tc % 8
                        v.bn_stats(bnb[:, tl * 12:tl * 12 + 6],
                                   xln[:, tc * D: tc * D + 512])
                        v.bn_stats(bnb[:, tl * 12 + 6:tl * 12 + 12],
                                   xln[:, tc * D + 512: tc * D + 1024])
                    b.vector(f_lns)
                bank_ready[Tb] = ("ve", last_ao["ve"])

                if g4 % 2 == 1:
                    half = g4 // 2

                    def f_lna(v, half=half):
                        v.drain()
                        for tl in range(8):
                            tc = half * 8 + tl
                            v.bn_aggr(mvv[:, 2 * tc:2 * tc + 2],
                                      bnb[:, tl * 12:(tl + 1) * 12])
                        v.drain()
                        for tl in range(8):
                            tc = half * 8 + tl
                            v.tensor_scalar(rv[:, tc:tc + 1],
                                            mvv[:, 2 * tc + 1:2 * tc + 2],
                                            EPS_LN, None, ALU.add)
                        v.drain()
                        ln_done[half] = ctx.inc(v.nop(), "ve")
                    b.vector(f_lna)

        def f_rstd(s):
            ctx.w(s, "ve", ln_done[1])
            s.activation(out=rv[:, :], in_=rv[:, :], func=AF.Ln)
            s.drain()
            s.activation(out=rv[:, :], in_=rv[:, :], func=AF.Exp, scale=-0.5)
            s.drain()
            ctx.inc(s.nop(), "sc")
        b.scalar(f_rstd)
        rstd_sc = ctx.c["sc"]

        if debug:
            def f_dbg(g):
                ctx.w(g, "ve")
                ctx.inc(g.dma_start(out=dbg["d_s"][:, :], in_=smat[:, :]), "dg", 16)
                for tc in range(TC):
                    ctx.inc(g.dma_start(out=dbg["d_resid1"][tc * 128:(tc + 1) * 128, :],
                                        in_=xln[:, tc * D:(tc + 1) * D]), "dg", 16)
            b.gpsimd(f_dbg)

        def f_ap(v):
            ctx.w(v, "sc", rstd_sc)
            if debug:
                ctx.w(v, "dg")
            for tc in range(TC):
                v.tensor_scalar(xln[:, tc * D:(tc + 1) * D],
                                xln[:, tc * D:(tc + 1) * D],
                                mvv[:, 2 * tc:2 * tc + 1], rv[:, tc:tc + 1],
                                ALU.subtract, ALU.mult)
            v.drain()
            ctx.inc(v.nop(), "ve")
        b.vector(f_ap)

        if debug:
            def f_dbg2(g):
                ctx.w(g, "ve")
                for tc in range(TC):
                    ctx.inc(g.dma_start(out=dbg["d_xln"][tc * 128:(tc + 1) * 128, :],
                                        in_=xln[:, tc * D:(tc + 1) * D]), "dg", 16)
            b.gpsimd(f_dbg2)

    _barrier(nc, ctx)
    es1c.close()
    es2a.close()
    es1.close()

    es2b = ExitStack()
    xth = sb(es2b, "xth", [128, KC * 512], BF16)
    xtl = sb(es2b, "xtl", [128, KC * 512], BF16)
    hth = sb(es2b, "hth", [128, KC * 512], BF16)
    htl = sb(es2b, "htl", [128, KC * 512], BF16)
    yq = sb(es2b, "yq", [128, TCQ * D], F32)
    wbh = [sb(es2b, f"wbh{i}", [128, KC * D], BF16) for i in range(2)]
    wbl = [sb(es2b, f"wbl{i}", [128, KC * D], BF16) for i in range(2)]
    wstg = [sb(es2b, f"wstg{i}", [128, 2 * D], F32) for i in range(2)]
    pwh = sb(es2b, "pwh", [128, KC * 64], BF16)
    pwl = sb(es2b, "pwl", [128, KC * 64], BF16)
    xa_s = sb(es2b, "xa_s", [64, 512], F32)
    c3tq = sb(es2b, "c3tq", [NE, 512], F32)

    wstg_guard = {0: 0, 1: 0}
    wbuf_guard = {0: 0, 1: 0}
    bank_guard = {i: 0 for i in range(6)}

    with nc.Block() as b:
        pw_ds = {}
        for k in range(KC):
            def f_pwl(s, k=k):
                ctx.inc(s.dma_start(out=wstg[0][:, k * 64:(k + 1) * 64],
                                    in_=pwt_d[k * 128:(k + 1) * 128, :]), "ds", 16)
                pw_ds[k] = ctx.c["ds"]
            b.sync(f_pwl)

        def f_pwh(s):
            ctx.w(s, "ds", pw_ds[KC - 1])
            ctx.inc(s.copy(pwh[:, 0:KC * 64], wstg[0][:, 0:KC * 64]), "sc")
        b.scalar(f_pwh)
        pwh_sc = ctx.c["sc"]

        def f_pwlo(v):
            ctx.w(v, "sc", pwh_sc)
            ctx.inc(v.tensor_tensor(pwl[:, 0:KC * 64], wstg[0][:, 0:KC * 64],
                                    pwh[:, 0:KC * 64], ALU.subtract), "ve")
        b.vector(f_pwlo)
        wstg_guard[0] = ctx.c["ve"]

    def emit_wload(b, src_dram, e, buf, defer_subs=False):
        marks = {}
        cast_marks = {}

        def f_cast(g, e=e, src_dram=src_dram, buf=buf):
            ctx.w(g, "pe", wbuf_guard[buf])
            for d in range(4):
                k0 = 2 * d
                src = src_dram[e, k0 * 128:(k0 + 2) * 128, :].rearrange(
                    "(a p) d -> p a d", p=128)
                dst = wbh[buf][:, k0 * D:(k0 + 2) * D].rearrange(
                    "p (a d) -> p a d", d=D)
                cast_marks[d] = ctx.inc(g.dma_start(out=dst, in_=src), "dg", 16)
        b.gpsimd(f_cast)
        cast_dg = ctx.c["dg"]
        dq_marks = {}

        def emit_dma(bb, d):
            k0 = 2 * d
            qsem = "ds" if d % 2 == 0 else "dv"

            def f_l(q, d=d, k0=k0, qsem=qsem):
                if d >= 2:
                    ctx.w(q, "ve", marks[d - 2])
                else:
                    ctx.w(q, "ve", wstg_guard[d % 2])
                src = src_dram[e, k0 * 128:(k0 + 2) * 128, :].rearrange(
                    "(a p) d -> p a d", p=128)
                dst = wstg[d % 2][:, :].rearrange("p (a d) -> p a d", d=D)
                ctx.inc(q.dma_start(out=dst, in_=src), qsem, 16)
            (bb.sync if d % 2 == 0 else bb.scalar)(f_l)
            dq_marks[d] = (qsem, ctx.c[qsem])

        def emit_sub(bb, d):
            k0 = 2 * d
            qsem, dq_k = dq_marks[d]

            def f_lo(v, d=d, k0=k0, dq_k=dq_k, qsem=qsem):
                ctx.w(v, qsem, dq_k)
                ctx.w(v, "dg", cast_marks[d])
                r = v.tensor_tensor(wbl[buf][:, k0 * D:(k0 + 2) * D],
                                    wstg[d % 2][:, :],
                                    wbh[buf][:, k0 * D:(k0 + 2) * D], ALU.subtract)
                marks[d] = ctx.inc(r, "ve")
                wstg_guard[d % 2] = marks[d]
            bb.vector(f_lo)

        emit_dma(b, 0)
        emit_dma(b, 1)
        if not defer_subs:
            emit_sub(b, 0)
            emit_sub(b, 1)
            emit_dma(b, 2)
            emit_dma(b, 3)
            emit_sub(b, 2)
            emit_sub(b, 3)
            jobs = []
        else:
            def j0(bb):
                emit_sub(bb, 0)
                emit_sub(bb, 1)
                emit_dma(bb, 2)
                emit_dma(bb, 3)

            def j1(bb):
                emit_sub(bb, 2)

            def j2(bb):
                emit_sub(bb, 3)

            def j3(bb):
                pass
            jobs = [j0, j1, j2, j3]
        return {"dg": cast_dg, "marks": marks, "jobs": jobs}

    def emit_transpose_split(b, src, src_col0, dsth, dstl):
        ev_prev = {}
        for k in range(KC):
            bank = PS[4 + (k % 2)]
            mark = {}

            def f_t(t, k=k, bank=bank, mark=mark):
                if k >= 2:
                    ctx.w(t, "ve", ev_prev[k - 2])
                else:
                    ctx.w(t, "ve")
                    ctx.w(t, "sc")
                r = None
                for tcl in range(TCQ):
                    r = t.transpose(bank[:, tcl * 128:(tcl + 1) * 128],
                                    src[:, src_col0 + tcl * D + k * 128:
                                        src_col0 + tcl * D + (k + 1) * 128],
                                    identf_s[:, :])
                mark[0] = ctx.inc(r, "pe")
            b.tensor(f_t)

            def f_h(s_, k=k, bank=bank, mark=mark):
                ctx.w(s_, "pe", mark[0])
                r = s_.copy(dsth[:, k * 512:(k + 1) * 512], bank[:, :])
                mark[1] = ctx.inc(r, "sc")
            b.scalar(f_h)

            def f_l(v, k=k, bank=bank, mark=mark):
                ctx.w(v, "sc", mark[1])
                r = v.tensor_tensor(dstl[:, k * 512:(k + 1) * 512], bank[:, :],
                                    dsth[:, k * 512:(k + 1) * 512], ALU.subtract)
                ev_prev[k] = ctx.inc(r, "ve")
            b.vector(f_l)

        def f_fence(v):
            v.drain()
            ctx.inc(v.nop(), "ve")
        b.vector(f_fence)

        def f_fence_s(s_):
            s_.drain()
            ctx.inc(s_.nop(), "sc")
        b.scalar(f_fence_s)
        bank_guard[4] = ctx.c["ve"]
        bank_guard[5] = ctx.c["ve"]
        return (ctx.c["sc"], ctx.c["ve"])

    def emit_routing(b, q, xt_ready):
        xa_mark = {}

        def f_xa(t):
            ctx.w(t, "sc", xt_ready[0])
            ctx.w(t, "ve", xt_ready[1])
            r = None
            for k in range(KC):
                ph = pwh[:, k * 64:(k + 1) * 64]
                pl = pwl[:, k * 64:(k + 1) * 64]
                xh_sl = xth[:, k * 512:(k + 1) * 512]
                xl_sl = xtl[:, k * 512:(k + 1) * 512]
                t.matmul(PS[4][0:64, 0:512], ph, xh_sl, start=(k == 0), stop=False)
                t.matmul(PS[4][0:64, 0:512], ph, xl_sl, start=False, stop=False)
                r = t.matmul(PS[4][0:64, 0:512], pl, xh_sl, start=False,
                             stop=(k == KC - 1))
            xa_mark[0] = ctx.inc(r, "pe")
        b.tensor(f_xa)

        def f_xaev(s):
            ctx.w(s, "pe", xa_mark[0])
            s.copy(xa_s[:, :], PS[4][0:64, 0:512])
            s.drain()
            xa_mark[1] = ctx.inc(s.nop(), "sc")
        b.scalar(f_xaev)

        if debug:
            def f_dbgxa(g, q=q):
                ctx.w(g, "sc", xa_mark[1])
                ctx.inc(g.dma_start(out=dbg["d_xa"][:, q * 512:(q + 1) * 512],
                                    in_=xa_s[:, :]), "dg", 16)
            b.gpsimd(f_dbgxa)

        zmark = {}

        def f_z(t):
            ctx.w(t, "sc", xa_mark[1])
            r = None
            for tcl in range(TCQ):
                r = t.matmul(PS[5][0:128, tcl * 36:tcl * 36 + 36],
                             xa_s[:, tcl * 128:(tcl + 1) * 128],
                             uallt_s[:, :], start=True, stop=True)
            zmark[0] = ctx.inc(r, "pe")
        b.tensor(f_z)
        rt = {"c3_sc": 0}
        return rt, lambda bb: emit_routing_fin(bb, q, zmark, rt)

    def emit_routing_fin(b, q, zmark, rt):
        def f_zev(s_):
            ctx.w(s_, "pe", zmark[0])
            s_.copy(zq[:, :], PS[5][0:128, 0:144])
            s_.activation(out=PS[6][:, :], in_=PS[6][:, :], func=AF.Copy)
            s_.drain()
            zmark[1] = ctx.inc(s_.nop(), "sc")
        b.scalar(f_zev)

        SQ = 12

        def seg(ap):
            return ap.rearrange("p (s e) -> p s e", e=12)

        m1 = rtg[:, 0 * SQ:1 * SQ]
        m2 = rtg[:, 1 * SQ:2 * SQ]
        m3 = rtg[:, 2 * SQ:3 * SQ]
        d2 = rtg[:, 3 * SQ:4 * SQ]
        d3 = rtg[:, 4 * SQ:5 * SQ]
        e2x = rtg[:, 5 * SQ:6 * SQ]
        e3x = rtg[:, 6 * SQ:7 * SQ]
        den = rtg[:, 7 * SQ:8 * SQ]
        w1 = rtg[:, 8 * SQ:9 * SQ]
        w2 = rtg[:, 9 * SQ:10 * SQ]
        w3 = rtg[:, 10 * SQ:11 * SQ]
        tmark = {}

        def f_t3a(v):
            ctx.w(v, "sc", zmark[1])
            v.tensor_reduce(m1, seg(zq[:, :]), AX.X, ALU.max)
            v.drain()
            v.tensor_tensor(seg(eq1[:, :]), seg(zq[:, :]),
                            m1.broadcast_to([128, SQ, 12]), ALU.is_equal)
            v.drain()
            v.scalar_tensor_tensor(zq[:, :], eq1[:, :], NEG, zq[:, :],
                                   ALU.mult, ALU.add)
            v.drain()
            v.tensor_reduce(m2, seg(zq[:, :]), AX.X, ALU.max)
            v.drain()
            v.tensor_tensor(seg(eq2[:, :]), seg(zq[:, :]),
                            m2.broadcast_to([128, SQ, 12]), ALU.is_equal)
            v.drain()
            v.scalar_tensor_tensor(zq[:, :], eq2[:, :], NEG, zq[:, :],
                                   ALU.mult, ALU.add)
            v.drain()
            v.tensor_reduce(m3, seg(zq[:, :]), AX.X, ALU.max)
            v.drain()
            v.tensor_tensor(seg(eq3[:, :]), seg(zq[:, :]),
                            m3.broadcast_to([128, SQ, 12]), ALU.is_equal)
            v.tensor_tensor(d2, m2, m1, ALU.subtract)
            v.tensor_tensor(d3, m3, m1, ALU.subtract)
            v.drain()
            tmark[0] = ctx.inc(v.nop(), "ve")
        b.vector(f_t3a)

        def f_t3exp(s_):
            ctx.w(s_, "ve", tmark[0])
            s_.activation(out=rtg[:, 5 * SQ:7 * SQ], in_=rtg[:, 3 * SQ:5 * SQ],
                          func=AF.Exp)
            s_.activation(out=PS[6][:, :], in_=PS[6][:, :], func=AF.Copy)
            s_.drain()
            tmark[1] = ctx.inc(s_.nop(), "sc")
        b.scalar(f_t3exp)

        def f_t3b(v, q=q):
            ctx.w(v, "sc", tmark[1])
            cdst = ctok[:, q * TCQ * 36:(q + 1) * TCQ * 36]
            v.tensor_tensor(den, e2x, e3x, ALU.add)
            v.drain()
            v.tensor_scalar(den, den, 1.0, None, ALU.add)
            v.drain()
            v.reciprocal(w1, den)
            v.drain()
            v.tensor_tensor(w2, e2x, w1, ALU.mult)
            v.tensor_tensor(w3, e3x, w1, ALU.mult)
            v.drain()
            v.tensor_tensor(seg(cdst), seg(eq1[:, :]),
                            w1.broadcast_to([128, SQ, 12]), ALU.mult)
            v.tensor_tensor(seg(eq2[:, :]), seg(eq2[:, :]),
                            w2.broadcast_to([128, SQ, 12]), ALU.mult)
            v.drain()
            v.tensor_tensor(cdst, cdst, eq2[:, :], ALU.add)
            v.tensor_tensor(seg(eq3[:, :]), seg(eq3[:, :]),
                            w3.broadcast_to([128, SQ, 12]), ALU.mult)
            v.drain()
            v.tensor_tensor(cdst, cdst, eq3[:, :], ALU.add)
            v.drain()
            ctx.inc(v.nop(), "ve")
        b.vector(f_t3b)

        ctprev = {"ev": 0}
        for tcl in range(TCQ):
            tc = q * TCQ + tcl
            cm = {}

            def f_ct(t, tc=tc, cm=cm):
                ctx.w(t, "ve")
                ctx.w(t, "sc", ctprev["ev"])
                r = t.transpose(PS[5][0:12, 0:128],
                                ctok[:, tc * 36 + 24: tc * 36 + 36],
                                identf_s[:, :])
                cm[0] = ctx.inc(r, "pe")
            b.tensor(f_ct)

            def f_cte(s, tcl=tcl, cm=cm):
                ctx.w(s, "pe", cm[0])
                r = s.copy(c3tq[:, tcl * 128:(tcl + 1) * 128], PS[5][0:12, 0:128])
                ctprev["ev"] = ctx.inc(r, "sc")
            b.scalar(f_cte)
        rt["c3_sc"] = ctprev["ev"]

        if debug:
            def f_dbgc(g, q=q):
                ctx.w(g, "ve")
                ctx.w(g, "sc")
                cols = slice(q * TCQ * 36, (q + 1) * TCQ * 36)
                ctx.inc(g.dma_start(out=dbg["d_C"][:, cols], in_=ctok[:, cols]),
                        "dg", 16)
            b.gpsimd(f_dbgc)

    def emit_mixture_expert(b, q, m, e, buf, wready, xh_t, xl_t, first,
                            seed_sc=0, xt_marks=None, vejobs=(), postjobs=(),
                            c3_wait=None):
        vejobs = list(vejobs)
        postjobs = list(postjobs)
        for tcl in range(TCQ):
            tc = q * TCQ + tcl
            pedone = {}

            def f_mm(t, tcl=tcl, buf=buf, wready=wready, pedone=pedone,
                     xh_t=xh_t, xl_t=xl_t, first=first, seed_sc=seed_sc,
                     xt_marks=xt_marks, c3_wait=c3_wait):
                if tcl == 2 and c3_wait is not None:
                    ctx.w(t, "sc", c3_wait["c3_sc"])
                if tcl == 0:
                    ctx.w(t, "dg", wready["dg"])
                    ctx.w(t, "ve", wready["marks"][3])
                    if seed_sc:
                        ctx.w(t, "sc", seed_sc)
                    if xt_marks is not None:
                        ctx.w(t, "sc", xt_marks[0])
                        ctx.w(t, "ve", xt_marks[1])
                if first and tcl == 0:
                    ctx.w(t, "sc")
                ctx.w(t, "ve", max(bank_guard[(tcl % 3) * 2],
                                   bank_guard[(tcl % 3) * 2 + 1]))
                r = None
                for k in range(KC):
                    xh_sl = xh_t[:, k * 512 + tcl * 128: k * 512 + (tcl + 1) * 128]
                    xl_sl = xl_t[:, k * 512 + tcl * 128: k * 512 + (tcl + 1) * 128]
                    for oh in range(2):
                        bank = PS[(tcl % 3) * 2 + oh]
                        wh_sl = wbh[buf][:, k * D + oh * 512: k * D + (oh + 1) * 512]
                        wl_sl = wbl[buf][:, k * D + oh * 512: k * D + (oh + 1) * 512]
                        t.matmul(bank[:, :], xh_sl, wh_sl,
                                 start=(k == 0), stop=False)
                        t.matmul(bank[:, :], xh_sl, wl_sl, start=False, stop=False)
                        r = t.matmul(bank[:, :], xl_sl, wh_sl, start=False,
                                     stop=(k == KC - 1))
                pedone[0] = ctx.inc(r, "pe")
            b.tensor(f_mm)

            def f_ep(v, tcl=tcl, tc=tc, e=e, m=m, first=first, pedone=pedone,
                     seed_sc=seed_sc):
                ctx.w(v, "pe", pedone[0])
                if first and m == 1:
                    ctx.w(v, "sc", seed_sc)
                r = None
                for oh in range(2):
                    bank = PS[(tcl % 3) * 2 + oh]
                    ysl = yq[:, tcl * D + oh * 512: tcl * D + (oh + 1) * 512]
                    cc = ctok[:, tc * 36 + m * 12 + e: tc * 36 + m * 12 + e + 1]
                    if first and m == 0:
                        r = v.tensor_scalar(ysl, bank[:, :], cc, None, ALU.mult)
                    else:
                        r = v.scalar_tensor_tensor(ysl, bank[:, :], cc, ysl,
                                                   ALU.mult, ALU.add)
                n = ctx.inc(r, "ve")
                bank_guard[(tcl % 3) * 2] = n
                bank_guard[(tcl % 3) * 2 + 1] = n
            if tcl < len(vejobs):
                vejobs[tcl](b)
            b.vector(f_ep)
            if tcl < len(postjobs):
                postjobs[tcl](b)

    def emit_head(b, q):
        xt_ready = emit_transpose_split(b, xln, q * TCQ * D, xth, xtl)
        rt, fin = emit_routing(b, q, xt_ready)
        return xt_ready, rt, fin

    def emit_tail(b, q):
        r2_ve = {}

        def f_r2(v, q=q):
            for tcl in range(TCQ):
                tc = q * TCQ + tcl
                xsl = xln[:, tc * D:(tc + 1) * D]
                v.scalar_tensor_tensor(xsl, yq[:, tcl * D:(tcl + 1) * D], 1.0,
                                       xsl, ALU.mult, ALU.add)
            v.drain()
            for tcl in range(TCQ):
                tc = q * TCQ + tcl
                xsl = xln[:, tc * D:(tc + 1) * D]
                v.bn_stats(bnb[:, tcl * 12:tcl * 12 + 6], xsl[:, 0:512])
                v.bn_stats(bnb[:, tcl * 12 + 6:tcl * 12 + 12], xsl[:, 512:1024])
            v.drain()
            for tcl in range(TCQ):
                v.bn_aggr(mvv[:, 2 * tcl:2 * tcl + 2],
                          bnb[:, tcl * 12:(tcl + 1) * 12])
            v.drain()
            for tcl in range(TCQ):
                v.tensor_scalar(rv[:, tcl:tcl + 1],
                                mvv[:, 2 * tcl + 1:2 * tcl + 2],
                                EPS_LN, None, ALU.add)
            v.drain()
            r2_ve[TCQ - 1] = ctx.inc(v.nop(), "ve")
        b.vector(f_r2)

        def f_rstd2(s_):
            ctx.w(s_, "ve", r2_ve[TCQ - 1])
            s_.activation(out=rv[:, 0:TCQ], in_=rv[:, 0:TCQ], func=AF.Ln)
            s_.drain()
            s_.activation(out=rv[:, 0:TCQ], in_=rv[:, 0:TCQ], func=AF.Exp,
                          scale=-0.5)
            s_.drain()
            ctx.inc(s_.nop(), "sc")
        b.scalar(f_rstd2)
        rstd2_sc = ctx.c["sc"]

        for tcl in range(TCQ):
            tc = q * TCQ + tcl
            out_m = {}

            def f_ap2(v, tcl=tcl, tc=tc, out_m=out_m):
                if tcl == 0:
                    ctx.w(v, "sc", rstd2_sc)
                xsl = xln[:, tc * D:(tc + 1) * D]
                v.tensor_scalar(xsl, xsl, mvv[:, 2 * tcl:2 * tcl + 1],
                                rv[:, tcl:tcl + 1], ALU.subtract, ALU.mult)
                v.drain()
                out_m[0] = ctx.inc(v.nop(), "ve")
            b.vector(f_ap2)

            def f_st(g, tcl=tcl, tc=tc, out_m=out_m):
                ctx.w(g, "ve", out_m[0])
                ctx.inc(g.dma_start(out=out_d[tc * 128:(tc + 1) * 128, :],
                                    in_=xln[:, tc * D:(tc + 1) * D]), "so", 16)
            b.gpsimd(f_st)

    with nc.Block() as b:
        xt_ready_q = emit_head(b, 0)
        pre_w1 = {0: emit_wload(b, w1t_d, 0, 0)}
        for q in range(NQ):
            wready = pre_w1
            w2e0 = {}
            gelu_sc = {}
            xt_marks_q, rt_q, fin_q = xt_ready_q
            for e in range(NE):
                buf = e % 2
                jobs = ()
                if e + 1 < NE:
                    wready[e + 1] = emit_wload(b, w1t_d, e + 1, (e + 1) % 2,
                                               defer_subs=True)
                    jobs = wready[e + 1]["jobs"]
                    if e == 0:
                        j0o = jobs[0]
                        jobs = [lambda bb, f=fin_q, g=j0o: (f(bb), g(bb))] \
                            + list(jobs[1:])
                else:
                    w2e0[0] = emit_wload(b, w2t_d, 0, 0, defer_subs=True)
                    wjobs = w2e0[0]["jobs"]

                    def mkgelu(tcl):
                        def j(bb):
                            def f_vf(v):
                                v.drain()
                                ctx.inc(v.nop(), "ve")
                            bb.vector(f_vf)

                            def f_g(s_, tcl=tcl):
                                ctx.w(s_, "ve")
                                r = s_.activation(
                                    out=yq[:, tcl * D:(tcl + 1) * D],
                                    in_=yq[:, tcl * D:(tcl + 1) * D],
                                    func=AF.Gelu)
                                gelu_sc[0] = ctx.inc(r, "sc")
                            bb.scalar(f_g)
                        return j
                    pjobs = [mkgelu(i) for i in range(4)]
                    jobs = wjobs
                emit_mixture_expert(b, q, 0, e, buf, wready[e], xth, xtl,
                                    first=(e == 0), vejobs=jobs,
                                    postjobs=(pjobs if e + 1 == NE else ()),
                                    xt_marks=(xt_marks_q if e == 0 else None),
                                    c3_wait=(rt_q if e == 0 else None))
                wbuf_guard[buf] = ctx.c["pe"]

            if debug:
                def f_dbgy1(g, q=q):
                    ctx.w(g, "ve")
                    for tcl in range(TCQ):
                        tc = q * TCQ + tcl
                        ctx.inc(g.dma_start(
                            out=dbg["d_y1"][tc * 128:(tc + 1) * 128, :],
                            in_=yq[:, tcl * D:(tcl + 1) * D]), "dg", 16)
                b.gpsimd(f_dbgy1)

            def f_wait_gelu(t):
                ctx.w(t, "sc", gelu_sc[0])
            b.tensor(f_wait_gelu)

            ht_ready = emit_transpose_split(b, yq, 0, hth, htl)
            h_done_pe = ctx.c["pe"]

            seed_fin = {}
            for tcl in range(TCQ):
                sm = {}

                def f_seed(t, tcl=tcl, sm=sm):
                    if tcl == 0:
                        ctx.w(t, "sc")
                    for oh in range(2):
                        ctx.w(t, "ve", bank_guard[(tcl % 2) * 2 + oh])
                    r = None
                    for oh in range(2):
                        r = t.matmul(PS[(tcl % 2) * 2 + oh][:, :],
                                     c3tq[:, tcl * 128:(tcl + 1) * 128],
                                     b2_s[:, oh * 512:(oh + 1) * 512],
                                     start=True, stop=True)
                    sm[0] = ctx.inc(r, "pe")
                b.tensor(f_seed)

                def f_seedev(s_, tcl=tcl, sm=sm):
                    ctx.w(s_, "pe", sm[0])
                    ctx.w(s_, "pe", h_done_pe)
                    r = None
                    for oh in range(2):
                        r = s_.copy(yq[:, tcl * D + oh * 512:
                                       tcl * D + (oh + 1) * 512],
                                    PS[(tcl % 2) * 2 + oh][:, :])
                    seed_fin[tcl] = ctx.inc(r, "sc")
                b.scalar(f_seedev)
            seed_all = seed_fin[TCQ - 1]

            wready2 = w2e0
            pre_w1 = {}
            for e in range(NE):
                buf = e % 2
                jobs = ()
                if e + 1 < NE:
                    wready2[e + 1] = emit_wload(b, w2t_d, e + 1, (e + 1) % 2,
                                                defer_subs=True)
                    jobs = wready2[e + 1]["jobs"]
                elif q + 1 < NQ:
                    pre_w1[0] = emit_wload(b, w1t_d, 0, 0, defer_subs=True)
                    jobs = pre_w1[0]["jobs"]
                emit_mixture_expert(b, q, 1, e, buf, wready2[e], hth, htl,
                                    first=(e == 0), seed_sc=seed_all,
                                    xt_marks=(ht_ready if e == 0 else None),
                                    vejobs=jobs)
                wbuf_guard[buf] = ctx.c["pe"]

            if debug:
                def f_dbgy2(g, q=q):
                    ctx.w(g, "ve")
                    for tcl in range(TCQ):
                        tc = q * TCQ + tcl
                        ctx.inc(g.dma_start(
                            out=dbg["d_y2"][tc * 128:(tc + 1) * 128, :],
                            in_=yq[:, tcl * D:(tcl + 1) * D]), "dg", 16)
                b.gpsimd(f_dbgy2)

            if q + 1 < NQ:
                xt_ready_q = emit_head(b, q + 1)
            emit_tail(b, q)

    with nc.Block() as b:
        def f_fin(g):
            g.wait_ge(ctx.sem["so"], 16 * TC)
        b.gpsimd(f_fin)

    nc._keepalive = (es0, es2, es2b)
    return nc



def _rope_tables():
    pos = np.arange(T, dtype=np.float32)
    inv = (np.float32(32.0) ** (-(np.arange(0, D, 2).astype(np.float32))
                                / np.float32(D))).astype(np.float32)
    ang = pos[:, None] * inv[None, :]
    emb = np.concatenate([ang, ang], axis=1)
    cos = np.cos(emb).astype(np.float32)
    sin = np.sin(emb).astype(np.float32)
    cosT = np.ascontiguousarray(cos.T).astype(BFD)
    sinS = np.ascontiguousarray(sin.T)
    sinS[:D // 2, :] *= np.float32(-1.0)
    sinST = sinS.astype(BFD)
    return cosT, sinST


def make_in_maps(inputs):
    hs = np.asarray(inputs["L0_attn_hidden_states"], dtype=np.float32)
    Wq = np.asarray(inputs["L0_attn_Wq"], dtype=np.float32)
    Wk = np.asarray(inputs["L0_attn_Wk"], dtype=np.float32)
    fc1w = np.asarray(inputs["L0_attn_fc1_w"], dtype=np.float32)
    fc1b = np.asarray(inputs["L0_attn_fc1_b"], dtype=np.float32)
    fc2w = np.asarray(inputs["L0_attn_fc2_w"], dtype=np.float32)
    fc2b = np.asarray(inputs["L0_attn_fc2_b"], dtype=np.float32)
    Pw = np.asarray(inputs["L0_ffn_P_w"], dtype=np.float32)
    U1 = np.asarray(inputs["L0_ffn_U1"], dtype=np.float32)
    U2 = np.asarray(inputs["L0_ffn_U2"], dtype=np.float32)
    U3 = np.asarray(inputs["L0_ffn_U3"], dtype=np.float32)
    W1 = np.asarray(inputs["L0_ffn_W1"], dtype=np.float32)
    W2 = np.asarray(inputs["L0_ffn_W2"], dtype=np.float32)
    b2 = np.asarray(inputs["L0_ffn_b2"], dtype=np.float32)

    cosT, sinST = _rope_tables()
    shared = {
        "wqt": np.ascontiguousarray(Wq.T),
        "wkt": np.ascontiguousarray(Wk.T),
        "fc1wt": np.ascontiguousarray(
            np.concatenate([fc1w.T, np.zeros((1, 1), np.float32)], axis=1)),
        "fc1b": np.ascontiguousarray(
            np.concatenate([fc1b, np.ones((1,), np.float32)])[:, None]),
        "fc2wta": np.ascontiguousarray(
            np.concatenate([fc2w.T, fc2b[None, :]], axis=0)),
        "pwt": np.ascontiguousarray(Pw.T),
        "uallt": np.ascontiguousarray(
            np.concatenate([U1.T, U2.T, U3.T], axis=1)),
        "b2": np.ascontiguousarray(b2),
        "w1t": np.ascontiguousarray(W1.transpose(0, 2, 1)),
        "w2t": np.ascontiguousarray(W2.transpose(0, 2, 1)),
        "costab": cosT,
        "sintab": sinST,
        "trimask": np.triu(np.full((128, 128), np.float32(NEG),
                                   dtype=np.float32), k=1),
        "identf": np.eye(128, dtype=np.float32),
    }
    in_maps = []
    for c in range(8):
        m = dict(shared)
        m["x0"] = np.ascontiguousarray(hs[c])
        m["x0t"] = np.ascontiguousarray(hs[c].T)
        in_maps.append(m)
    return in_maps


def kernel(**inputs):
    in_maps = make_in_maps(inputs)
    nc = build_nc(debug=False)
    res = run_bass_kernel_spmd(nc, in_maps, core_ids=list(range(8)))
    out = np.stack([np.asarray(res.results[c]["out"]) for c in range(8)], axis=0)
    return out.astype(np.float32)


# version 35
# speedup vs baseline: 1.0463x; 1.0012x over previous
import numpy as np
import ml_dtypes
from contextlib import ExitStack
import concourse.bass as bass
import concourse.mybir as mybir
from concourse.bass_utils import run_bass_kernel_spmd

F32 = mybir.dt.float32
BF16 = mybir.dt.bfloat16
AF = mybir.ActivationFunctionType
ALU = mybir.AluOpType
AX = mybir.AxisListType
BFD = ml_dtypes.bfloat16

T = 2048
D = 1024
TC = 16
KC = 8
NE = 12
NQ = 4
TCQ = TC // NQ
NEG = -1.0e30
EPS_LN = 1e-5

ENGS = ("tensor", "vector", "scalar", "sync", "gpsimd")
_SEM_OF = {"tensor": "pe", "vector": "ve", "scalar": "sc",
           "sync": "ds", "gpsimd": "dg"}


class Ctx:
    def __init__(self, nc, es):
        self.nc = nc
        self.sem = {}
        self.c = {}
        for n in ("pe", "ve", "sc", "ds", "dg", "dv", "so"):
            self.sem[n] = es.enter_context(nc.semaphore("S" + n))
            self.c[n] = 0

    def inc(self, instr, name, k=1):
        instr.then_inc(self.sem[name], k)
        self.c[name] += k
        return self.c[name]

    def w(self, eng, name, val=None):
        v = self.c[name] if val is None else val
        if v > 0:
            eng.wait_ge(self.sem[name], v)


def _barrier(nc, ctx):
    with nc.Block() as b:
        for e in ENGS:
            def f(eng, e=e):
                eng.drain()
                nm = _SEM_OF[e]
                ctx.inc(eng.nop(), nm, 16 if nm in ("ds", "dg") else 1)
            getattr(b, e)(f)
    snap = dict(ctx.c)
    with nc.Block() as b:
        for e in ENGS:
            def f(eng):
                for nm in ("pe", "ve", "sc", "ds", "dg"):
                    ctx.w(eng, nm, snap[nm])
            getattr(b, e)(f)


def build_nc(debug=False):
    nc = bass.Bass("TRN2", target_bir_lowering=False)

    def din(name, shape, dt=F32):
        return nc.dram_tensor(name, shape, dt, kind="ExternalInput").ap()

    x0_d = din("x0", [T, D])
    x0t_d = din("x0t", [D, T])
    wqt_d = din("wqt", [D, D])
    wkt_d = din("wkt", [D, D])
    fc1wt_d = din("fc1wt", [1, 17])
    fc1b_d = din("fc1b", [17, 1])
    fc2wta_d = din("fc2wta", [17, D])
    pwt_d = din("pwt", [D, 64])
    uallt_d = din("uallt", [64, 36])
    b2_d = din("b2", [NE, D])
    w1t_d = din("w1t", [NE, D, D])
    w2t_d = din("w2t", [NE, D, D])
    costab_d = din("costab", [D, T], BF16)
    sintab_d = din("sintab", [D, T], BF16)
    trimask_d = din("trimask", [128, 128])
    identf_d = din("identf", [128, 128])
    out_d = nc.dram_tensor("out", [T, D], F32, kind="ExternalOutput").ap()
    wch_d = nc.dram_tensor("wch", [2, NE, D, D], BF16, kind="Internal").ap()
    wcl_d = nc.dram_tensor("wcl", [2, NE, D, D], BF16, kind="Internal").ap()

    dbg = {}
    if debug:
        for nm, shape, dt in [
            ("d_q", [D, T], BF16), ("d_k", [D, T], BF16),
            ("d_s", [128, TC], F32), ("d_resid1", [T, D], F32),
            ("d_xln", [T, D], F32), ("d_xa", [64, T], F32),
            ("d_C", [128, TC * 36], F32),
            ("d_y1", [T, D], F32), ("d_y2", [T, D], F32),
        ]:
            dbg[nm] = nc.dram_tensor(nm, shape, dt, kind="ExternalOutput").ap()

    es0 = ExitStack()
    ctx = Ctx(nc, es0)

    def sb(esx, name, shape, dt):
        return esx.enter_context(nc.sbuf_tensor(name, shape, dt))

    smat = sb(es0, "smat", [128, TC], F32)
    fc1wt_s = sb(es0, "fc1wt_s", [1, 17], F32)
    fc1b_s = sb(es0, "fc1b_s", [17, 1], F32)
    identf_s = sb(es0, "identf_s", [128, 128], F32)
    ctok = sb(es0, "ctok", [128, TC * 36], F32)
    bnb = sb(es0, "bnb", [128, 8 * 12], F32)
    mvv = sb(es0, "mvv", [128, 2 * TC], F32)
    rv = sb(es0, "rv", [128, TC], F32)
    b2_s = sb(es0, "b2_s", [NE, D], F32)
    uallt_s = sb(es0, "uallt_s", [64, 36], F32)
    zq = sb(es0, "zq", [128, 144], F32)
    eq1 = sb(es0, "eq1", [128, 144], BF16)
    eq2 = sb(es0, "eq2", [128, 144], BF16)
    eq3 = sb(es0, "eq3", [128, 144], BF16)
    rtg = sb(es0, "rtg", [128, 144], F32)

    PS = [es0.enter_context(nc.psum_tensor(f"ps{i}", [128, 512], F32))
          for i in range(8)]

    with nc.Block() as b:
        def f(s):
            for tgt, src in [(fc1wt_s, fc1wt_d), (fc1b_s, fc1b_d),
                             (identf_s, identf_d), (b2_s, b2_d),
                             (uallt_s, uallt_d)]:
                ctx.inc(s.dma_start(out=tgt[:, :], in_=src[:, :]), "ds", 16)
        b.sync(f)

    es2 = ExitStack()
    xln = sb(es2, "xln", [128, TC * D], F32)

    es1 = ExitStack()
    qTb = sb(es1, "qTb", [128, KC * T], BF16)
    kTb = sb(es1, "kTb", [128, KC * T], BF16)
    trimask_s = sb(es1, "trimask_s", [128, 128], F32)

    es1a = ExitStack()
    x0th = xln[:, 0:KC * T // 2].bitcast(BF16)
    x0tl = xln[:, KC * T // 2:KC * T].bitcast(BF16)
    wsh = sb(es1a, "wsh", [128, KC * D], BF16)
    wsl = sb(es1a, "wsl", [128, KC * D], BF16)
    stg = [sb(es1a, "stg0", [128, T], F32)]
    stgx = sb(es1a, "stgx", [128, T], F32)
    tabs = [sb(es1a, f"tab{i}", [128, T], BF16) for i in range(4)]
    rsc = [stg[0][:, 0:1024].bitcast(BF16), stg[0][:, 1024:2048].bitcast(BF16),
           stgx[:, 0:1024].bitcast(BF16)]

    with nc.Block() as b:
        xsp = {}
        wsp_q = {}
        xslot = [stgx, qTb[:, 0:2 * T].bitcast(F32)]

        def emit_xload(bb, k):
            qn = "dg" if k % 2 == 0 else "dv"

            def f(eng, k=k, qn=qn):
                if k >= 2:
                    ctx.w(eng, "ve", xsp[k - 2][1])
                ctx.inc(eng.dma_start(out=xslot[k % 2][:, :],
                                      in_=x0t_d[k * 128:(k + 1) * 128, :]),
                        qn, 16)
            (bb.gpsimd if k % 2 == 0 else bb.scalar)(f)
            xsp[k] = [0, 0, (qn, ctx.c[qn])]

        def emit_wqload(bb, k):
            def f(s, k=k):
                if k >= 2:
                    ctx.w(s, "ve", wsp_q[k - 2][1])
                ctx.inc(s.dma_start(out=stg[0][:, (k % 2) * D:(k % 2 + 1) * D],
                                    in_=wqt_d[k * 128:(k + 1) * 128, :]),
                        "ds", 16)
            bb.sync(f)
            wsp_q[k] = [0, 0, ctx.c["ds"]]

        emit_xload(b, 0)
        emit_xload(b, 1)
        emit_wqload(b, 0)
        emit_wqload(b, 1)
        for k in range(KC):
            def f_hi(s, k=k):
                qn, cv = xsp[k][2]
                ctx.w(s, qn, cv)
                r = s.copy(x0th[:, k * T:(k + 1) * T], xslot[k % 2][:, :])
                xsp[k][0] = ctx.inc(r, "sc")
            b.scalar(f_hi)

            def f_whi(s, k=k):
                ctx.w(s, "ds", wsp_q[k][2])
                r = s.copy(wsh[:, k * D:(k + 1) * D],
                           stg[0][:, (k % 2) * D:(k % 2 + 1) * D])
                wsp_q[k][0] = ctx.inc(r, "sc")
            b.scalar(f_whi)

            def f_lo(v, k=k):
                ctx.w(v, "sc", xsp[k][0])
                r = v.tensor_tensor(x0tl[:, k * T:(k + 1) * T], xslot[k % 2][:, :],
                                    x0th[:, k * T:(k + 1) * T], ALU.subtract)
                xsp[k][1] = ctx.inc(r, "ve")
            b.vector(f_lo)

            def f_wlo(v, k=k):
                ctx.w(v, "sc", wsp_q[k][0])
                r = v.tensor_tensor(wsl[:, k * D:(k + 1) * D],
                                    stg[0][:, (k % 2) * D:(k % 2 + 1) * D],
                                    wsh[:, k * D:(k + 1) * D], ALU.subtract)
                wsp_q[k][1] = ctx.inc(r, "ve")
            b.vector(f_wlo)
            if k + 2 < KC:
                emit_xload(b, k + 2)
                emit_wqload(b, k + 2)
        xsp_sc, xsp_ve = ctx.c["sc"], ctx.c["ve"]
        xlo_marks = {k: xsp[k][1] for k in range(KC)}
        wqlo_marks = {k: wsp_q[k][1] for k in range(KC)}

        def f_tm(s):
            ctx.inc(s.dma_start(out=trimask_s[:, :], in_=trimask_d[:, :]), "ds", 16)
        b.sync(f_tm)

        rope_ve = {}
        evacs = {}

        def emit_rope_pair(pair, evq, evk, scw=0):
            c, c4 = pair, pair + 4

            def f_g(g, pair=pair, c=c, c4=c4, scw=scw):
                if scw:
                    ctx.w(g, "sc", scw)
                if pair >= 1:
                    ctx.w(g, "ve", rope_ve[pair - 1])
                for i, (tsrc, cc) in enumerate([(costab_d, c), (sintab_d, c),
                                                (costab_d, c4), (sintab_d, c4)]):
                    ctx.inc(g.dma_start(out=tabs[i][:, :],
                                        in_=tsrc[cc * 128:(cc + 1) * 128, :]),
                            "dg", 16)
            b.gpsimd(f_g)
            dg_p = ctx.c["dg"]

            def f_v(v, pair=pair, c=c, c4=c4, dg_p=dg_p, evq=evq, evk=evk):
                ctx.w(v, "dg", dg_p)
                ctx.w(v, "sc", max(evq, evk))
                r = None
                for tens in (qTb, kTb):
                    a = tens[:, c * T:(c + 1) * T]
                    a4 = tens[:, c4 * T:(c4 + 1) * T]
                    v.tensor_tensor(rsc[0], a, tabs[0][:, :], ALU.mult)
                    v.tensor_tensor(rsc[1], a4, tabs[1][:, :], ALU.mult)
                    v.tensor_tensor(rsc[2], a, tabs[3][:, :], ALU.mult)
                    v.tensor_tensor(a, rsc[0], rsc[1], ALU.add)
                    v.tensor_tensor(rsc[0], a4, tabs[2][:, :], ALU.mult)
                    r = v.tensor_tensor(a4, rsc[0], rsc[2], ALU.add)
                rope_ve[pair] = ctx.inc(r, "ve")
            b.vector(f_v)

        for proj, (w_dram, outb) in enumerate([(wqt_d, qTb), (wkt_d, kTb)]):
            evac = {}
            evacs[proj] = evac
            pe_guard = ctx.c["pe"]
            if proj == 0:
                wsp = wsp_q
            else:
                wsp = {}
                kslots = [stg[0][:, 0:D], stg[0][:, D:2 * D],
                          stgx[:, 0:D], stgx[:, D:2 * D],
                          tabs[0][:, 0:2 * D].bitcast(F32),
                          tabs[1][:, 0:2 * D].bitcast(F32)]
                kguard = [wqlo_marks[6], wqlo_marks[7], xlo_marks[6],
                          xlo_marks[6], 0, 0]

                def emit_wkload(bb, k):
                    qn = "ds" if k % 2 == 0 else "dg"

                    def f(eng, k=k, qn=qn):
                        if k < 6:
                            ctx.w(eng, "ve", kguard[k])
                        else:
                            ctx.w(eng, "ve", wsp[k - 6][1])
                        ctx.inc(eng.dma_start(
                            out=kslots[k % 6],
                            in_=w_dram[k * 128:(k + 1) * 128, :]), qn, 16)
                    (bb.sync if k % 2 == 0 else bb.gpsimd)(f)
                    wsp[k] = [0, 0, (qn, ctx.c[qn])]

                for k in range(6):
                    emit_wkload(b, k)
                for k in range(KC):
                    def f_whik(s, k=k):
                        qn, cv = wsp[k][2]
                        ctx.w(s, qn, cv)
                        if k == 0:
                            ctx.w(s, "pe", pe_guard)
                        r = s.copy(wsh[:, k * D:(k + 1) * D], kslots[k % 6])
                        wsp[k][0] = ctx.inc(r, "sc")
                    b.scalar(f_whik)

                    def f_wlok(v, k=k):
                        ctx.w(v, "sc", wsp[k][0])
                        r = v.tensor_tensor(wsl[:, k * D:(k + 1) * D],
                                            kslots[k % 6],
                                            wsh[:, k * D:(k + 1) * D],
                                            ALU.subtract)
                        wsp[k][1] = ctx.inc(r, "ve")
                    b.vector(f_wlok)
                    if k + 6 < KC:
                        emit_wkload(b, k + 6)

            for oc in range(8):
                for tb in range(4):
                    idx = oc * 4 + tb
                    bank = PS[idx % 2]
                    pe_t = {}

                    def f_pe(t, oc=oc, tb=tb, idx=idx, bank=bank, pe_t=pe_t,
                             wsp=wsp, evac=evac):
                        if idx == 0:
                            ctx.w(t, "sc", wsp[KC - 1][0])
                            ctx.w(t, "ve", wsp[KC - 1][1])
                            ctx.w(t, "sc", xsp_sc)
                            ctx.w(t, "ve", xsp_ve)
                        if idx >= 2:
                            ctx.w(t, "sc", evac[idx - 2])
                        r = None
                        for k in range(KC):
                            wh_sl = wsh[:, k * D + oc * 128: k * D + (oc + 1) * 128]
                            wl_sl = wsl[:, k * D + oc * 128: k * D + (oc + 1) * 128]
                            xh_sl = x0th[:, k * T + tb * 512: k * T + (tb + 1) * 512]
                            xl_sl = x0tl[:, k * T + tb * 512: k * T + (tb + 1) * 512]
                            t.matmul(bank[:, :], wh_sl, xh_sl,
                                     start=(k == 0), stop=False)
                            t.matmul(bank[:, :], wh_sl, xl_sl,
                                     start=False, stop=False)
                            r = t.matmul(bank[:, :], wl_sl, xh_sl,
                                         start=False, stop=(k == KC - 1))
                        pe_t[0] = ctx.inc(r, "pe")
                    b.tensor(f_pe)

                    def f_ev(s, oc=oc, tb=tb, idx=idx, bank=bank, outb=outb,
                             pe_t=pe_t, evac=evac):
                        ctx.w(s, "pe", pe_t[0])
                        r = s.copy(outb[:, oc * T + tb * 512: oc * T + (tb + 1) * 512],
                                   bank[:, :])
                        evac[idx] = ctx.inc(r, "sc")
                    b.scalar(f_ev)
                    if proj == 1 and oc >= 4 and tb == 3:
                        pair = oc - 4
                        emit_rope_pair(pair,
                                       max(evacs[0].values()),
                                       evac[oc * 4 + 3],
                                       scw=(wsp[5][0] if pair == 0 else 0))

    _barrier(nc, ctx)
    es1a.close()


    es2a = ExitStack()
    relu16 = sb(es2a, "relu16", [17, T], F32)
    srow = sb(es2a, "srow", [1, T], F32)
    fc2wta_s = sb(es2a, "fc2wta_s", [17, D], F32)
    es1c = ExitStack()
    expbuf2 = [sb(es1c, f"expbuf{i}", [128, T], F32) for i in range(2)]

    with nc.Block() as b:
        x0_dg = {}

        def f_gx(g):
            for tc in range(TC):
                r = g.dma_start(out=xln[:, tc * D:(tc + 1) * D],
                                in_=x0_d[tc * 128:(tc + 1) * 128, :])
                x0_dg[tc] = ctx.inc(r, "dg", 16)
        b.gpsimd(f_gx)

        def f_sw(s):
            ctx.inc(s.dma_start(out=fc2wta_s[:, :], in_=fc2wta_d[:, :]), "ds", 16)
        b.sync(f_sw)
        fc2_ds = ctx.c["ds"]

        bank_ready = {}
        red_ve = {}
        sr_sc = {}
        r16_sc = {}
        last_ao = {"ve": 0}
        ln_done = {}
        for qc in range(TC):
            nblk = qc // 4 + 1
            valid = (qc + 1) * 128
            rem = valid - (nblk - 1) * 512
            pe_q = {}
            b0 = (qc % 2) * 4
            expbuf = expbuf2[qc % 2]

            def f_pe(t, qc=qc, nblk=nblk, rem=rem, pe_q=pe_q, b0=b0):
                for jb in range(nblk):
                    if (b0 + jb) in bank_ready:
                        nm, val = bank_ready[b0 + jb]
                        ctx.w(t, nm, val)
                for dk in range(KC):
                    r = None
                    for jb in range(nblk):
                        N = 512 if jb < nblk - 1 else rem
                        r = t.matmul(PS[b0 + jb][:, 0:N],
                                     qTb[:, dk * T + qc * 128: dk * T + (qc + 1) * 128],
                                     kTb[:, dk * T + jb * 512: dk * T + jb * 512 + N],
                                     start=(dk == 0), stop=(dk == KC - 1))
                    if dk == KC - 1:
                        pe_q[0] = ctx.inc(r, "pe")
            b.tensor(f_pe)

            def f_vm(v, qc=qc, nblk=nblk, rem=rem, pe_q=pe_q, b0=b0):
                ctx.w(v, "pe", pe_q[0])
                r = v.tensor_tensor(PS[b0 + nblk - 1][:, rem - 128:rem],
                                    PS[b0 + nblk - 1][:, rem - 128:rem],
                                    trimask_s[:, :], ALU.add)
                pe_q[1] = ctx.inc(r, "ve")
            b.vector(f_vm)

            def f_sc(s, qc=qc, nblk=nblk, rem=rem, pe_q=pe_q, b0=b0,
                     expbuf=expbuf):
                ctx.w(s, "pe", pe_q[0])
                ctx.w(s, "ve", pe_q[1])
                if qc >= 2:
                    ctx.w(s, "ve", red_ve[qc - 2])
                for jb in range(nblk):
                    N = 512 if jb < nblk - 1 else rem
                    r = s.activation(out=expbuf[:, jb * 512: jb * 512 + N],
                                     in_=PS[b0 + jb][:, 0:N], func=AF.Exp,
                                     scale=1.0 / 32.0)
                    bank_ready[b0 + jb] = ("sc", ctx.inc(r, "sc"))
            b.scalar(f_sc)

            def f_red(v, qc=qc, valid=valid, expbuf=expbuf):
                ctx.w(v, "sc")
                r = v.reduce_sum(smat[:, qc:qc + 1], expbuf[:, 0:valid], axis=AX.X)
                red_ve[qc] = ctx.inc(r, "ve")
            b.vector(f_red)

            if qc % 4 == 3:
                g4 = qc // 4
                Tb = (g4 % 2) * 4 + 3
                tr = {}

                def f_trg(t, g4=g4, Tb=Tb, tr=tr):
                    ctx.w(t, "ve", red_ve[4 * g4 + 3])
                    if Tb in bank_ready:
                        nm, val = bank_ready[Tb]
                        ctx.w(t, nm, val)
                    r = None
                    for j in range(4):
                        qq = g4 * 4 + j
                        r = t.transpose(PS[Tb][0:1, j * 128:(j + 1) * 128],
                                        smat[:, qq:qq + 1], identf_s[:, :])
                    tr[0] = ctx.inc(r, "pe")
                b.tensor(f_trg)

                def f_srg(s, g4=g4, Tb=Tb, tr=tr):
                    ctx.w(s, "pe", tr[0])
                    s.copy(srow[0:1, g4 * 512:(g4 + 1) * 512], PS[Tb][0:1, 0:512])
                    s.drain()
                    sr_sc[g4] = ctx.inc(s.nop(), "sc")
                b.scalar(f_srg)

                r16 = {}

                def f_r16(t, g4=g4, Tb=Tb, r16=r16):
                    ctx.w(t, "sc", sr_sc[g4])
                    r = t.matmul(PS[Tb][0:17, 0:512], fc1wt_s[0:1, :],
                                 srow[0:1, g4 * 512:(g4 + 1) * 512],
                                 start=True, stop=True)
                    r16[0] = ctx.inc(r, "pe")
                b.tensor(f_r16)

                def f_r16s(s, g4=g4, Tb=Tb, r16=r16):
                    ctx.w(s, "pe", r16[0])
                    s.activation(out=relu16[0:17, g4 * 512:(g4 + 1) * 512],
                                 in_=PS[Tb][0:17, 0:512], func=AF.Relu,
                                 bias=fc1b_s[:, 0:1])
                    s.drain()
                    r16_sc[g4] = ctx.inc(s.nop(), "sc")
                b.scalar(f_r16s)

                for j in range(4):
                    tc = g4 * 4 + j
                    for oh in range(2):
                        ao = {}

                        def f_ao(t, tc=tc, oh=oh, g4=g4, Tb=Tb, ao=ao, j=j):
                            if j == 0 and oh == 0:
                                ctx.w(t, "sc", r16_sc[g4])
                                ctx.w(t, "ds", fc2_ds)
                            else:
                                ctx.w(t, "ve", last_ao["ve"])
                            r = t.matmul(PS[Tb][:, :],
                                         relu16[:, tc * 128:(tc + 1) * 128],
                                         fc2wta_s[:, oh * 512:(oh + 1) * 512],
                                         start=True, stop=True)
                            ao[0] = ctx.inc(r, "pe")
                        b.tensor(f_ao)

                        def f_aov(v, tc=tc, oh=oh, Tb=Tb, ao=ao):
                            ctx.w(v, "pe", ao[0])
                            ctx.w(v, "dg", x0_dg[tc])
                            sl = xln[:, tc * D + oh * 512: tc * D + (oh + 1) * 512]
                            r = v.scalar_tensor_tensor(sl, PS[Tb][:, :], 1.0, sl,
                                                       ALU.mult, ALU.add)
                            last_ao["ve"] = ctx.inc(r, "ve")
                        b.vector(f_aov)

                    def f_lns(v, tc=tc):
                        v.drain()
                        tl = tc % 8
                        v.bn_stats(bnb[:, tl * 12:tl * 12 + 6],
                                   xln[:, tc * D: tc * D + 512])
                        v.bn_stats(bnb[:, tl * 12 + 6:tl * 12 + 12],
                                   xln[:, tc * D + 512: tc * D + 1024])
                    b.vector(f_lns)
                bank_ready[Tb] = ("ve", last_ao["ve"])

                if g4 % 2 == 1:
                    half = g4 // 2

                    def f_lna(v, half=half):
                        v.drain()
                        for tl in range(8):
                            tc = half * 8 + tl
                            v.bn_aggr(mvv[:, 2 * tc:2 * tc + 2],
                                      bnb[:, tl * 12:(tl + 1) * 12])
                        v.drain()
                        for tl in range(8):
                            tc = half * 8 + tl
                            v.tensor_scalar(rv[:, tc:tc + 1],
                                            mvv[:, 2 * tc + 1:2 * tc + 2],
                                            EPS_LN, None, ALU.add)
                        v.drain()
                        ln_done[half] = ctx.inc(v.nop(), "ve")
                    b.vector(f_lna)

        def f_rstd(s):
            ctx.w(s, "ve", ln_done[1])
            s.activation(out=rv[:, :], in_=rv[:, :], func=AF.Ln)
            s.drain()
            s.activation(out=rv[:, :], in_=rv[:, :], func=AF.Exp, scale=-0.5)
            s.drain()
            ctx.inc(s.nop(), "sc")
        b.scalar(f_rstd)
        rstd_sc = ctx.c["sc"]

        if debug:
            def f_dbg(g):
                ctx.w(g, "ve")
                ctx.inc(g.dma_start(out=dbg["d_s"][:, :], in_=smat[:, :]), "dg", 16)
                for tc in range(TC):
                    ctx.inc(g.dma_start(out=dbg["d_resid1"][tc * 128:(tc + 1) * 128, :],
                                        in_=xln[:, tc * D:(tc + 1) * D]), "dg", 16)
            b.gpsimd(f_dbg)

        def f_ap(v):
            ctx.w(v, "sc", rstd_sc)
            if debug:
                ctx.w(v, "dg")
            for tc in range(TC):
                v.tensor_scalar(xln[:, tc * D:(tc + 1) * D],
                                xln[:, tc * D:(tc + 1) * D],
                                mvv[:, 2 * tc:2 * tc + 1], rv[:, tc:tc + 1],
                                ALU.subtract, ALU.mult)
            v.drain()
            ctx.inc(v.nop(), "ve")
        b.vector(f_ap)

        if debug:
            def f_dbg2(g):
                ctx.w(g, "ve")
                for tc in range(TC):
                    ctx.inc(g.dma_start(out=dbg["d_xln"][tc * 128:(tc + 1) * 128, :],
                                        in_=xln[:, tc * D:(tc + 1) * D]), "dg", 16)
            b.gpsimd(f_dbg2)

    _barrier(nc, ctx)
    es1c.close()
    es2a.close()
    es1.close()

    es2b = ExitStack()
    xth = sb(es2b, "xth", [128, KC * 512], BF16)
    xtl = sb(es2b, "xtl", [128, KC * 512], BF16)
    hth = sb(es2b, "hth", [128, KC * 512], BF16)
    htl = sb(es2b, "htl", [128, KC * 512], BF16)
    yq = sb(es2b, "yq", [128, TCQ * D], F32)
    wbh = [sb(es2b, f"wbh{i}", [128, KC * D], BF16) for i in range(2)]
    wbl = [sb(es2b, f"wbl{i}", [128, KC * D], BF16) for i in range(2)]
    wstg = [sb(es2b, f"wstg{i}", [128, 2 * D], F32) for i in range(2)]
    pwh = sb(es2b, "pwh", [128, KC * 64], BF16)
    pwl = sb(es2b, "pwl", [128, KC * 64], BF16)
    xa_s = sb(es2b, "xa_s", [64, 512], F32)
    c3tq = sb(es2b, "c3tq", [NE, 512], F32)

    wstg_guard = {0: 0, 1: 0}
    wbuf_guard = {0: 0, 1: 0}
    wsth_guard = {0: 0, 1: 0}
    wstl_guard = {0: 0, 1: 0}
    sth_marks = {}
    stl_marks = {}
    bank_guard = {i: 0 for i in range(6)}

    with nc.Block() as b:
        pw_ds = {}
        for k in range(KC):
            def f_pwl(s, k=k):
                ctx.inc(s.dma_start(out=wstg[0][:, k * 64:(k + 1) * 64],
                                    in_=pwt_d[k * 128:(k + 1) * 128, :]), "ds", 16)
                pw_ds[k] = ctx.c["ds"]
            b.sync(f_pwl)

        def f_pwh(s):
            ctx.w(s, "ds", pw_ds[KC - 1])
            ctx.inc(s.copy(pwh[:, 0:KC * 64], wstg[0][:, 0:KC * 64]), "sc")
        b.scalar(f_pwh)
        pwh_sc = ctx.c["sc"]

        def f_pwlo(v):
            ctx.w(v, "sc", pwh_sc)
            ctx.inc(v.tensor_tensor(pwl[:, 0:KC * 64], wstg[0][:, 0:KC * 64],
                                    pwh[:, 0:KC * 64], ALU.subtract), "ve")
        b.vector(f_pwlo)
        wstg_guard[0] = ctx.c["ve"]

    def emit_wload(b, src_dram, e, buf, defer_subs=False, wi=0, qq=0):
        if qq > 0:
            hi_m = {}

            def f_hih(g, wi=wi, e=e, buf=buf):
                ctx.w(g, "pe", wbuf_guard[buf])
                ctx.w(g, "dv", max(wsth_guard[buf], sth_marks[(wi, e)]))
                src = wch_d[wi, e].rearrange("(a p) d -> p a d", p=128)
                dst = wbh[buf][:, :].rearrange("p (a d) -> p a d", d=D)
                hi_m[0] = ctx.inc(g.dma_start(out=dst, in_=src), "dg", 16)
            b.gpsimd(f_hih)

            def f_lol(q_, wi=wi, e=e, buf=buf):
                ctx.w(q_, "pe", wbuf_guard[buf])
                ctx.w(q_, "ds", max(wstl_guard[buf], stl_marks[(wi, e)]))
                src = wcl_d[wi, e].rearrange("(a p) d -> p a d", p=128)
                dst = wbl[buf][:, :].rearrange("p (a d) -> p a d", d=D)
                ctx.inc(q_.dma_start(out=dst, in_=src), "ds", 16)
            b.sync(f_lol)
            jobs = [lambda bb: None] * 4 if defer_subs else []
            return {"dg": hi_m[0], "ds": ctx.c["ds"], "marks": {3: 0},
                    "jobs": jobs}

        marks = {}

        def f_cast(g, e=e, src_dram=src_dram, buf=buf):
            ctx.w(g, "pe", wbuf_guard[buf])
            ctx.w(g, "dv", wsth_guard[buf])
            src = src_dram[e].rearrange("(a p) d -> p a d", p=128)
            dst = wbh[buf][:, :].rearrange("p (a d) -> p a d", d=D)
            ctx.inc(g.dma_start(out=dst, in_=src), "dg", 16)
        b.gpsimd(f_cast)
        cast_dg = ctx.c["dg"]
        dq_marks = {}

        def emit_dma(bb, d):
            k0 = 2 * d
            qsem = "ds" if d % 2 == 0 else "dg"

            def f_l(q, d=d, k0=k0, qsem=qsem):
                if d >= 2:
                    ctx.w(q, "ve", marks[d - 2])
                else:
                    ctx.w(q, "ve", wstg_guard[d % 2])
                    ctx.w(q, "pe", wbuf_guard[buf])
                src = src_dram[e, k0 * 128:(k0 + 2) * 128, :].rearrange(
                    "(a p) d -> p a d", p=128)
                dst = wstg[d % 2][:, :].rearrange("p (a d) -> p a d", d=D)
                ctx.inc(q.dma_start(out=dst, in_=src), qsem, 16)
            (bb.sync if d % 2 == 0 else bb.gpsimd)(f_l)
            dq_marks[d] = (qsem, ctx.c[qsem])

        def emit_sub(bb, d):
            k0 = 2 * d
            qsem, dq_k = dq_marks[d]

            def f_lo(v, d=d, k0=k0, dq_k=dq_k, qsem=qsem):
                ctx.w(v, qsem, dq_k)
                ctx.w(v, "dg", cast_dg)
                if d == 0:
                    ctx.w(v, "ds", wstl_guard[buf])
                r = v.tensor_tensor(wbl[buf][:, k0 * D:(k0 + 2) * D],
                                    wstg[d % 2][:, :],
                                    wbh[buf][:, k0 * D:(k0 + 2) * D], ALU.subtract)
                marks[d] = ctx.inc(r, "ve")
                wstg_guard[d % 2] = marks[d]
            bb.vector(f_lo)

        def emit_store(bb, wi=wi, e=e, buf=buf):
            def f_sth(s, wi=wi, e=e, buf=buf):
                ctx.w(s, "dg", cast_dg)
                dst = wch_d[wi, e].rearrange("(a p) d -> p a d", p=128)
                src = wbh[buf][:, :].rearrange("p (a d) -> p a d", d=D)
                ctx.inc(s.dma_start(out=dst, in_=src), "dv", 16)
            bb.scalar(f_sth)
            wsth_guard[buf] = ctx.c["dv"]
            sth_marks[(wi, e)] = ctx.c["dv"]

            def f_stl(q_, wi=wi, e=e, buf=buf):
                ctx.w(q_, "ve", marks[3])
                dst = wcl_d[wi, e].rearrange("(a p) d -> p a d", p=128)
                src = wbl[buf][:, :].rearrange("p (a d) -> p a d", d=D)
                ctx.inc(q_.dma_start(out=dst, in_=src), "ds", 16)
            bb.sync(f_stl)
            wstl_guard[buf] = ctx.c["ds"]
            stl_marks[(wi, e)] = ctx.c["ds"]

        emit_dma(b, 0)
        emit_dma(b, 1)
        if not defer_subs:
            emit_sub(b, 0)
            emit_sub(b, 1)
            emit_dma(b, 2)
            emit_dma(b, 3)
            emit_sub(b, 2)
            emit_sub(b, 3)
            emit_store(b)
            jobs = []
        else:
            def j0(bb):
                emit_sub(bb, 0)
                emit_sub(bb, 1)
                emit_dma(bb, 2)
                emit_dma(bb, 3)

            def j1(bb):
                emit_sub(bb, 2)

            def j2(bb):
                emit_sub(bb, 3)

            def j3(bb):
                emit_store(bb)
            jobs = [j0, j1, j2, j3]
        return {"dg": cast_dg, "ds": 0, "marks": marks, "jobs": jobs}

    def emit_transpose_split(b, src, src_col0, dsth, dstl):
        ev_prev = {}
        for k in range(KC):
            bank = PS[4 + (k % 2)]
            mark = {}

            def f_t(t, k=k, bank=bank, mark=mark):
                if k >= 2:
                    ctx.w(t, "ve", ev_prev[k - 2])
                else:
                    ctx.w(t, "ve")
                    ctx.w(t, "sc")
                r = None
                for tcl in range(TCQ):
                    r = t.transpose(bank[:, tcl * 128:(tcl + 1) * 128],
                                    src[:, src_col0 + tcl * D + k * 128:
                                        src_col0 + tcl * D + (k + 1) * 128],
                                    identf_s[:, :])
                mark[0] = ctx.inc(r, "pe")
            b.tensor(f_t)

            def f_h(s_, k=k, bank=bank, mark=mark):
                ctx.w(s_, "pe", mark[0])
                r = s_.copy(dsth[:, k * 512:(k + 1) * 512], bank[:, :])
                mark[1] = ctx.inc(r, "sc")
            b.scalar(f_h)

            def f_l(v, k=k, bank=bank, mark=mark):
                ctx.w(v, "sc", mark[1])
                r = v.tensor_tensor(dstl[:, k * 512:(k + 1) * 512], bank[:, :],
                                    dsth[:, k * 512:(k + 1) * 512], ALU.subtract)
                ev_prev[k] = ctx.inc(r, "ve")
            b.vector(f_l)

        def f_fence(v):
            v.drain()
            ctx.inc(v.nop(), "ve")
        b.vector(f_fence)

        def f_fence_s(s_):
            s_.drain()
            ctx.inc(s_.nop(), "sc")
        b.scalar(f_fence_s)
        bank_guard[4] = ctx.c["ve"]
        bank_guard[5] = ctx.c["ve"]
        return (ctx.c["sc"], ctx.c["ve"])

    def emit_routing(b, q, xt_ready):
        xa_mark = {}

        def f_xa(t):
            ctx.w(t, "sc", xt_ready[0])
            ctx.w(t, "ve", xt_ready[1])
            r = None
            for k in range(KC):
                ph = pwh[:, k * 64:(k + 1) * 64]
                pl = pwl[:, k * 64:(k + 1) * 64]
                xh_sl = xth[:, k * 512:(k + 1) * 512]
                xl_sl = xtl[:, k * 512:(k + 1) * 512]
                t.matmul(PS[4][0:64, 0:512], ph, xh_sl, start=(k == 0), stop=False)
                t.matmul(PS[4][0:64, 0:512], ph, xl_sl, start=False, stop=False)
                r = t.matmul(PS[4][0:64, 0:512], pl, xh_sl, start=False,
                             stop=(k == KC - 1))
            xa_mark[0] = ctx.inc(r, "pe")
        b.tensor(f_xa)

        def f_xaev(s):
            ctx.w(s, "pe", xa_mark[0])
            s.copy(xa_s[:, :], PS[4][0:64, 0:512])
            s.drain()
            xa_mark[1] = ctx.inc(s.nop(), "sc")
        b.scalar(f_xaev)

        if debug:
            def f_dbgxa(g, q=q):
                ctx.w(g, "sc", xa_mark[1])
                ctx.inc(g.dma_start(out=dbg["d_xa"][:, q * 512:(q + 1) * 512],
                                    in_=xa_s[:, :]), "dg", 16)
            b.gpsimd(f_dbgxa)

        zmark = {}

        def f_z(t):
            ctx.w(t, "sc", xa_mark[1])
            r = None
            for tcl in range(TCQ):
                r = t.matmul(PS[5][0:128, tcl * 36:tcl * 36 + 36],
                             xa_s[:, tcl * 128:(tcl + 1) * 128],
                             uallt_s[:, :], start=True, stop=True)
            zmark[0] = ctx.inc(r, "pe")
        b.tensor(f_z)
        rt = {"c3_sc": 0}
        return rt, lambda bb: emit_routing_fin(bb, q, zmark, rt)

    def emit_routing_fin(b, q, zmark, rt):
        def f_zev(s_):
            ctx.w(s_, "pe", zmark[0])
            s_.copy(zq[:, :], PS[5][0:128, 0:144])
            s_.activation(out=PS[6][:, :], in_=PS[6][:, :], func=AF.Copy)
            s_.drain()
            zmark[1] = ctx.inc(s_.nop(), "sc")
        b.scalar(f_zev)

        SQ = 12

        def seg(ap):
            return ap.rearrange("p (s e) -> p s e", e=12)

        m1 = rtg[:, 0 * SQ:1 * SQ]
        m2 = rtg[:, 1 * SQ:2 * SQ]
        m3 = rtg[:, 2 * SQ:3 * SQ]
        d2 = rtg[:, 3 * SQ:4 * SQ]
        d3 = rtg[:, 4 * SQ:5 * SQ]
        e2x = rtg[:, 5 * SQ:6 * SQ]
        e3x = rtg[:, 6 * SQ:7 * SQ]
        den = rtg[:, 7 * SQ:8 * SQ]
        w1 = rtg[:, 8 * SQ:9 * SQ]
        w2 = rtg[:, 9 * SQ:10 * SQ]
        w3 = rtg[:, 10 * SQ:11 * SQ]
        tmark = {}

        def f_t3a(v):
            ctx.w(v, "sc", zmark[1])
            v.tensor_reduce(m1, seg(zq[:, :]), AX.X, ALU.max)
            v.drain()
            v.tensor_tensor(seg(eq1[:, :]), seg(zq[:, :]),
                            m1.broadcast_to([128, SQ, 12]), ALU.is_equal)
            v.drain()
            v.scalar_tensor_tensor(zq[:, :], eq1[:, :], NEG, zq[:, :],
                                   ALU.mult, ALU.add)
            v.drain()
            v.tensor_reduce(m2, seg(zq[:, :]), AX.X, ALU.max)
            v.drain()
            v.tensor_tensor(seg(eq2[:, :]), seg(zq[:, :]),
                            m2.broadcast_to([128, SQ, 12]), ALU.is_equal)
            v.drain()
            v.scalar_tensor_tensor(zq[:, :], eq2[:, :], NEG, zq[:, :],
                                   ALU.mult, ALU.add)
            v.drain()
            v.tensor_reduce(m3, seg(zq[:, :]), AX.X, ALU.max)
            v.drain()
            v.tensor_tensor(seg(eq3[:, :]), seg(zq[:, :]),
                            m3.broadcast_to([128, SQ, 12]), ALU.is_equal)
            v.tensor_tensor(d2, m2, m1, ALU.subtract)
            v.tensor_tensor(d3, m3, m1, ALU.subtract)
            v.drain()
            tmark[0] = ctx.inc(v.nop(), "ve")
        b.vector(f_t3a)

        def f_t3exp(s_):
            ctx.w(s_, "ve", tmark[0])
            s_.activation(out=rtg[:, 5 * SQ:7 * SQ], in_=rtg[:, 3 * SQ:5 * SQ],
                          func=AF.Exp)
            s_.activation(out=PS[6][:, :], in_=PS[6][:, :], func=AF.Copy)
            s_.drain()
            tmark[1] = ctx.inc(s_.nop(), "sc")
        b.scalar(f_t3exp)

        def f_t3b(v, q=q):
            ctx.w(v, "sc", tmark[1])
            cdst = ctok[:, q * TCQ * 36:(q + 1) * TCQ * 36]
            v.tensor_tensor(den, e2x, e3x, ALU.add)
            v.drain()
            v.tensor_scalar(den, den, 1.0, None, ALU.add)
            v.drain()
            v.reciprocal(w1, den)
            v.drain()
            v.tensor_tensor(w2, e2x, w1, ALU.mult)
            v.tensor_tensor(w3, e3x, w1, ALU.mult)
            v.drain()
            v.tensor_tensor(seg(cdst), seg(eq1[:, :]),
                            w1.broadcast_to([128, SQ, 12]), ALU.mult)
            v.tensor_tensor(seg(eq2[:, :]), seg(eq2[:, :]),
                            w2.broadcast_to([128, SQ, 12]), ALU.mult)
            v.drain()
            v.tensor_tensor(cdst, cdst, eq2[:, :], ALU.add)
            v.tensor_tensor(seg(eq3[:, :]), seg(eq3[:, :]),
                            w3.broadcast_to([128, SQ, 12]), ALU.mult)
            v.drain()
            v.tensor_tensor(cdst, cdst, eq3[:, :], ALU.add)
            v.drain()
            ctx.inc(v.nop(), "ve")
        b.vector(f_t3b)

        ctprev = {"ev": 0}
        for tcl in range(TCQ):
            tc = q * TCQ + tcl
            cm = {}

            def f_ct(t, tc=tc, cm=cm):
                ctx.w(t, "ve")
                ctx.w(t, "sc", ctprev["ev"])
                r = t.transpose(PS[5][0:12, 0:128],
                                ctok[:, tc * 36 + 24: tc * 36 + 36],
                                identf_s[:, :])
                cm[0] = ctx.inc(r, "pe")
            b.tensor(f_ct)

            def f_cte(s, tcl=tcl, cm=cm):
                ctx.w(s, "pe", cm[0])
                r = s.copy(c3tq[:, tcl * 128:(tcl + 1) * 128], PS[5][0:12, 0:128])
                ctprev["ev"] = ctx.inc(r, "sc")
            b.scalar(f_cte)
        rt["c3_sc"] = ctprev["ev"]

        if debug:
            def f_dbgc(g, q=q):
                ctx.w(g, "ve")
                ctx.w(g, "sc")
                cols = slice(q * TCQ * 36, (q + 1) * TCQ * 36)
                ctx.inc(g.dma_start(out=dbg["d_C"][:, cols], in_=ctok[:, cols]),
                        "dg", 16)
            b.gpsimd(f_dbgc)

    def emit_mixture_expert(b, q, m, e, buf, wready, xh_t, xl_t, first,
                            seed_sc=0, xt_marks=None, vejobs=(), postjobs=(),
                            c3_wait=None):
        vejobs = list(vejobs)
        postjobs = list(postjobs)
        for tcl in range(TCQ):
            tc = q * TCQ + tcl
            pedone = {}

            def f_mm(t, tcl=tcl, buf=buf, wready=wready, pedone=pedone,
                     xh_t=xh_t, xl_t=xl_t, first=first, seed_sc=seed_sc,
                     xt_marks=xt_marks, c3_wait=c3_wait):
                if tcl == 2 and c3_wait is not None:
                    ctx.w(t, "sc", c3_wait["c3_sc"])
                if tcl == 0:
                    ctx.w(t, "dg", wready["dg"])
                    ctx.w(t, "ds", wready["ds"])
                    ctx.w(t, "ve", wready["marks"][3])
                    if seed_sc:
                        ctx.w(t, "sc", seed_sc)
                    if xt_marks is not None:
                        ctx.w(t, "sc", xt_marks[0])
                        ctx.w(t, "ve", xt_marks[1])
                if first and tcl == 0:
                    ctx.w(t, "sc")
                ctx.w(t, "ve", max(bank_guard[(tcl % 3) * 2],
                                   bank_guard[(tcl % 3) * 2 + 1]))
                r = None
                for k in range(KC):
                    xh_sl = xh_t[:, k * 512 + tcl * 128: k * 512 + (tcl + 1) * 128]
                    xl_sl = xl_t[:, k * 512 + tcl * 128: k * 512 + (tcl + 1) * 128]
                    for oh in range(2):
                        bank = PS[(tcl % 3) * 2 + oh]
                        wh_sl = wbh[buf][:, k * D + oh * 512: k * D + (oh + 1) * 512]
                        wl_sl = wbl[buf][:, k * D + oh * 512: k * D + (oh + 1) * 512]
                        t.matmul(bank[:, :], xh_sl, wh_sl,
                                 start=(k == 0), stop=False)
                        t.matmul(bank[:, :], xh_sl, wl_sl, start=False, stop=False)
                        r = t.matmul(bank[:, :], xl_sl, wh_sl, start=False,
                                     stop=(k == KC - 1))
                pedone[0] = ctx.inc(r, "pe")
            b.tensor(f_mm)

            def f_ep(v, tcl=tcl, tc=tc, e=e, m=m, first=first, pedone=pedone,
                     seed_sc=seed_sc):
                ctx.w(v, "pe", pedone[0])
                if first and m == 1:
                    ctx.w(v, "sc", seed_sc)
                r = None
                for oh in range(2):
                    bank = PS[(tcl % 3) * 2 + oh]
                    ysl = yq[:, tcl * D + oh * 512: tcl * D + (oh + 1) * 512]
                    cc = ctok[:, tc * 36 + m * 12 + e: tc * 36 + m * 12 + e + 1]
                    if first and m == 0:
                        r = v.tensor_scalar(ysl, bank[:, :], cc, None, ALU.mult)
                    else:
                        r = v.scalar_tensor_tensor(ysl, bank[:, :], cc, ysl,
                                                   ALU.mult, ALU.add)
                n = ctx.inc(r, "ve")
                bank_guard[(tcl % 3) * 2] = n
                bank_guard[(tcl % 3) * 2 + 1] = n
            if tcl < len(vejobs):
                vejobs[tcl](b)
            b.vector(f_ep)
            if tcl < len(postjobs):
                postjobs[tcl](b)

    def emit_head(b, q):
        xt_ready = emit_transpose_split(b, xln, q * TCQ * D, xth, xtl)
        rt, fin = emit_routing(b, q, xt_ready)
        return xt_ready, rt, fin

    def emit_tail(b, q):
        r2_ve = {}

        def f_r2(v, q=q):
            for tcl in range(TCQ):
                tc = q * TCQ + tcl
                xsl = xln[:, tc * D:(tc + 1) * D]
                v.scalar_tensor_tensor(xsl, yq[:, tcl * D:(tcl + 1) * D], 1.0,
                                       xsl, ALU.mult, ALU.add)
            v.drain()
            for tcl in range(TCQ):
                tc = q * TCQ + tcl
                xsl = xln[:, tc * D:(tc + 1) * D]
                v.bn_stats(bnb[:, tcl * 12:tcl * 12 + 6], xsl[:, 0:512])
                v.bn_stats(bnb[:, tcl * 12 + 6:tcl * 12 + 12], xsl[:, 512:1024])
            v.drain()
            for tcl in range(TCQ):
                v.bn_aggr(mvv[:, 2 * tcl:2 * tcl + 2],
                          bnb[:, tcl * 12:(tcl + 1) * 12])
            v.drain()
            for tcl in range(TCQ):
                v.tensor_scalar(rv[:, tcl:tcl + 1],
                                mvv[:, 2 * tcl + 1:2 * tcl + 2],
                                EPS_LN, None, ALU.add)
            v.drain()
            r2_ve[TCQ - 1] = ctx.inc(v.nop(), "ve")
        b.vector(f_r2)

        def f_rstd2(s_):
            ctx.w(s_, "ve", r2_ve[TCQ - 1])
            s_.activation(out=rv[:, 0:TCQ], in_=rv[:, 0:TCQ], func=AF.Ln)
            s_.drain()
            s_.activation(out=rv[:, 0:TCQ], in_=rv[:, 0:TCQ], func=AF.Exp,
                          scale=-0.5)
            s_.drain()
            ctx.inc(s_.nop(), "sc")
        b.scalar(f_rstd2)
        rstd2_sc = ctx.c["sc"]

        for tcl in range(TCQ):
            tc = q * TCQ + tcl
            out_m = {}

            def f_ap2(v, tcl=tcl, tc=tc, out_m=out_m):
                if tcl == 0:
                    ctx.w(v, "sc", rstd2_sc)
                xsl = xln[:, tc * D:(tc + 1) * D]
                v.tensor_scalar(xsl, xsl, mvv[:, 2 * tcl:2 * tcl + 1],
                                rv[:, tcl:tcl + 1], ALU.subtract, ALU.mult)
                v.drain()
                out_m[0] = ctx.inc(v.nop(), "ve")
            b.vector(f_ap2)

            def f_st(g, tcl=tcl, tc=tc, out_m=out_m):
                ctx.w(g, "ve", out_m[0])
                ctx.inc(g.dma_start(out=out_d[tc * 128:(tc + 1) * 128, :],
                                    in_=xln[:, tc * D:(tc + 1) * D]), "so", 16)
            b.gpsimd(f_st)

    with nc.Block() as b:
        xt_ready_q = emit_head(b, 0)
        pre_w1 = {0: emit_wload(b, w1t_d, 0, 0, wi=0, qq=0)}
        for q in range(NQ):
            wready = pre_w1
            w2e0 = {}
            gelu_sc = {}
            xt_marks_q, rt_q, fin_q = xt_ready_q
            for e in range(NE):
                buf = e % 2
                jobs = ()
                if e + 1 < NE:
                    wready[e + 1] = emit_wload(b, w1t_d, e + 1, (e + 1) % 2,
                                               defer_subs=True, wi=0, qq=q)
                    jobs = wready[e + 1]["jobs"]
                    if e == 0:
                        j0o = jobs[0]
                        jobs = [lambda bb, f=fin_q, g=j0o: (f(bb), g(bb))] \
                            + list(jobs[1:])
                else:
                    w2e0[0] = emit_wload(b, w2t_d, 0, 0, defer_subs=True,
                                         wi=1, qq=q)
                    wjobs = w2e0[0]["jobs"]

                    def mkgelu(tcl):
                        def j(bb):
                            def f_vf(v):
                                v.drain()
                                ctx.inc(v.nop(), "ve")
                            bb.vector(f_vf)

                            def f_g(s_, tcl=tcl):
                                ctx.w(s_, "ve")
                                r = s_.activation(
                                    out=yq[:, tcl * D:(tcl + 1) * D],
                                    in_=yq[:, tcl * D:(tcl + 1) * D],
                                    func=AF.Gelu)
                                gelu_sc[0] = ctx.inc(r, "sc")
                            bb.scalar(f_g)
                        return j
                    pjobs = [mkgelu(i) for i in range(4)]
                    jobs = wjobs
                emit_mixture_expert(b, q, 0, e, buf, wready[e], xth, xtl,
                                    first=(e == 0), vejobs=jobs,
                                    postjobs=(pjobs if e + 1 == NE else ()),
                                    xt_marks=(xt_marks_q if e == 0 else None),
                                    c3_wait=(rt_q if e == 0 else None))
                wbuf_guard[buf] = ctx.c["pe"]

            if debug:
                def f_dbgy1(g, q=q):
                    ctx.w(g, "ve")
                    for tcl in range(TCQ):
                        tc = q * TCQ + tcl
                        ctx.inc(g.dma_start(
                            out=dbg["d_y1"][tc * 128:(tc + 1) * 128, :],
                            in_=yq[:, tcl * D:(tcl + 1) * D]), "dg", 16)
                b.gpsimd(f_dbgy1)

            def f_wait_gelu(t):
                ctx.w(t, "sc", gelu_sc[0])
            b.tensor(f_wait_gelu)

            ht_ready = emit_transpose_split(b, yq, 0, hth, htl)
            h_done_pe = ctx.c["pe"]

            seed_fin = {}
            for tcl in range(TCQ):
                sm = {}

                def f_seed(t, tcl=tcl, sm=sm):
                    if tcl == 0:
                        ctx.w(t, "sc")
                    for oh in range(2):
                        ctx.w(t, "ve", bank_guard[(tcl % 2) * 2 + oh])
                    r = None
                    for oh in range(2):
                        r = t.matmul(PS[(tcl % 2) * 2 + oh][:, :],
                                     c3tq[:, tcl * 128:(tcl + 1) * 128],
                                     b2_s[:, oh * 512:(oh + 1) * 512],
                                     start=True, stop=True)
                    sm[0] = ctx.inc(r, "pe")
                b.tensor(f_seed)

                def f_seedev(s_, tcl=tcl, sm=sm):
                    ctx.w(s_, "pe", sm[0])
                    ctx.w(s_, "pe", h_done_pe)
                    r = None
                    for oh in range(2):
                        r = s_.copy(yq[:, tcl * D + oh * 512:
                                       tcl * D + (oh + 1) * 512],
                                    PS[(tcl % 2) * 2 + oh][:, :])
                    seed_fin[tcl] = ctx.inc(r, "sc")
                b.scalar(f_seedev)
            seed_all = seed_fin[TCQ - 1]

            wready2 = w2e0
            pre_w1 = {}
            for e in range(NE):
                buf = e % 2
                jobs = ()
                if e + 1 < NE:
                    wready2[e + 1] = emit_wload(b, w2t_d, e + 1, (e + 1) % 2,
                                                defer_subs=True, wi=1, qq=q)
                    jobs = wready2[e + 1]["jobs"]
                elif q + 1 < NQ:
                    pre_w1[0] = emit_wload(b, w1t_d, 0, 0, defer_subs=True,
                                           wi=0, qq=q + 1)
                    jobs = pre_w1[0]["jobs"]
                emit_mixture_expert(b, q, 1, e, buf, wready2[e], hth, htl,
                                    first=(e == 0), seed_sc=seed_all,
                                    xt_marks=(ht_ready if e == 0 else None),
                                    vejobs=jobs)
                wbuf_guard[buf] = ctx.c["pe"]

            if debug:
                def f_dbgy2(g, q=q):
                    ctx.w(g, "ve")
                    for tcl in range(TCQ):
                        tc = q * TCQ + tcl
                        ctx.inc(g.dma_start(
                            out=dbg["d_y2"][tc * 128:(tc + 1) * 128, :],
                            in_=yq[:, tcl * D:(tcl + 1) * D]), "dg", 16)
                b.gpsimd(f_dbgy2)

            if q + 1 < NQ:
                xt_ready_q = emit_head(b, q + 1)
            emit_tail(b, q)

    with nc.Block() as b:
        def f_fin(g):
            g.wait_ge(ctx.sem["so"], 16 * TC)
        b.gpsimd(f_fin)

    nc._keepalive = (es0, es2, es2b)
    return nc



def _rope_tables():
    pos = np.arange(T, dtype=np.float32)
    inv = (np.float32(32.0) ** (-(np.arange(0, D, 2).astype(np.float32))
                                / np.float32(D))).astype(np.float32)
    ang = pos[:, None] * inv[None, :]
    emb = np.concatenate([ang, ang], axis=1)
    cos = np.cos(emb).astype(np.float32)
    sin = np.sin(emb).astype(np.float32)
    cosT = np.ascontiguousarray(cos.T).astype(BFD)
    sinS = np.ascontiguousarray(sin.T)
    sinS[:D // 2, :] *= np.float32(-1.0)
    sinST = sinS.astype(BFD)
    return cosT, sinST


def make_in_maps(inputs):
    hs = np.asarray(inputs["L0_attn_hidden_states"], dtype=np.float32)
    Wq = np.asarray(inputs["L0_attn_Wq"], dtype=np.float32)
    Wk = np.asarray(inputs["L0_attn_Wk"], dtype=np.float32)
    fc1w = np.asarray(inputs["L0_attn_fc1_w"], dtype=np.float32)
    fc1b = np.asarray(inputs["L0_attn_fc1_b"], dtype=np.float32)
    fc2w = np.asarray(inputs["L0_attn_fc2_w"], dtype=np.float32)
    fc2b = np.asarray(inputs["L0_attn_fc2_b"], dtype=np.float32)
    Pw = np.asarray(inputs["L0_ffn_P_w"], dtype=np.float32)
    U1 = np.asarray(inputs["L0_ffn_U1"], dtype=np.float32)
    U2 = np.asarray(inputs["L0_ffn_U2"], dtype=np.float32)
    U3 = np.asarray(inputs["L0_ffn_U3"], dtype=np.float32)
    W1 = np.asarray(inputs["L0_ffn_W1"], dtype=np.float32)
    W2 = np.asarray(inputs["L0_ffn_W2"], dtype=np.float32)
    b2 = np.asarray(inputs["L0_ffn_b2"], dtype=np.float32)

    cosT, sinST = _rope_tables()
    shared = {
        "wqt": np.ascontiguousarray(Wq.T),
        "wkt": np.ascontiguousarray(Wk.T),
        "fc1wt": np.ascontiguousarray(
            np.concatenate([fc1w.T, np.zeros((1, 1), np.float32)], axis=1)),
        "fc1b": np.ascontiguousarray(
            np.concatenate([fc1b, np.ones((1,), np.float32)])[:, None]),
        "fc2wta": np.ascontiguousarray(
            np.concatenate([fc2w.T, fc2b[None, :]], axis=0)),
        "pwt": np.ascontiguousarray(Pw.T),
        "uallt": np.ascontiguousarray(
            np.concatenate([U1.T, U2.T, U3.T], axis=1)),
        "b2": np.ascontiguousarray(b2),
        "w1t": np.ascontiguousarray(W1.transpose(0, 2, 1)),
        "w2t": np.ascontiguousarray(W2.transpose(0, 2, 1)),
        "costab": cosT,
        "sintab": sinST,
        "trimask": np.triu(np.full((128, 128), np.float32(NEG),
                                   dtype=np.float32), k=1),
        "identf": np.eye(128, dtype=np.float32),
    }
    in_maps = []
    for c in range(8):
        m = dict(shared)
        m["x0"] = np.ascontiguousarray(hs[c])
        m["x0t"] = np.ascontiguousarray(hs[c].T)
        in_maps.append(m)
    return in_maps


def kernel(**inputs):
    in_maps = make_in_maps(inputs)
    nc = build_nc(debug=False)
    res = run_bass_kernel_spmd(nc, in_maps, core_ids=list(range(8)))
    out = np.stack([np.asarray(res.results[c]["out"]) for c in range(8)], axis=0)
    return out.astype(np.float32)


# version 36
# speedup vs baseline: 1.0811x; 1.0333x over previous
import numpy as np
import ml_dtypes
from contextlib import ExitStack
import concourse.bass as bass
import concourse.mybir as mybir
from concourse.bass_utils import run_bass_kernel_spmd

F32 = mybir.dt.float32
BF16 = mybir.dt.bfloat16
AF = mybir.ActivationFunctionType
ALU = mybir.AluOpType
AX = mybir.AxisListType
BFD = ml_dtypes.bfloat16

T = 2048
D = 1024
TC = 16
KC = 8
NE = 12
NQ = 4
TCQ = TC // NQ
NEG = -1.0e30
EPS_LN = 1e-5

ENGS = ("tensor", "vector", "scalar", "sync", "gpsimd")
_SEM_OF = {"tensor": "pe", "vector": "ve", "scalar": "sc",
           "sync": "ds", "gpsimd": "dg"}


class Ctx:
    def __init__(self, nc, es):
        self.nc = nc
        self.sem = {}
        self.c = {}
        for n in ("pe", "ve", "sc", "ds", "dg", "dv", "so"):
            self.sem[n] = es.enter_context(nc.semaphore("S" + n))
            self.c[n] = 0

    def inc(self, instr, name, k=1):
        instr.then_inc(self.sem[name], k)
        self.c[name] += k
        return self.c[name]

    def w(self, eng, name, val=None):
        v = self.c[name] if val is None else val
        if v > 0:
            eng.wait_ge(self.sem[name], v)


def _barrier(nc, ctx):
    with nc.Block() as b:
        for e in ENGS:
            def f(eng, e=e):
                eng.drain()
                nm = _SEM_OF[e]
                ctx.inc(eng.nop(), nm, 16 if nm in ("ds", "dg") else 1)
            getattr(b, e)(f)
    snap = dict(ctx.c)
    with nc.Block() as b:
        for e in ENGS:
            def f(eng):
                for nm in ("pe", "ve", "sc", "ds", "dg"):
                    ctx.w(eng, nm, snap[nm])
            getattr(b, e)(f)


def build_nc(debug=False):
    nc = bass.Bass("TRN2", target_bir_lowering=False)

    def din(name, shape, dt=F32):
        return nc.dram_tensor(name, shape, dt, kind="ExternalInput").ap()

    x0_d = din("x0", [T, D])
    x0t_d = din("x0t", [D, T])
    wqt_d = din("wqt", [D, D])
    wkt_d = din("wkt", [D, D])
    fc1wt_d = din("fc1wt", [1, 17])
    fc1b_d = din("fc1b", [17, 1])
    fc2wta_d = din("fc2wta", [17, D])
    pwt_d = din("pwt", [D, 64])
    uallt_d = din("uallt", [64, 36])
    b2_d = din("b2", [NE, D])
    w1t_d = din("w1t", [NE, D, D])
    w2t_d = din("w2t", [NE, D, D])
    costab_d = din("costab", [D, T], BF16)
    sintab_d = din("sintab", [D, T], BF16)
    trimask_d = din("trimask", [128, 128])
    identf_d = din("identf", [128, 128])
    out_d = nc.dram_tensor("out", [T, D], F32, kind="ExternalOutput").ap()
    wch_d = nc.dram_tensor("wch", [2, NE, D, D], BF16, kind="Internal").ap()
    wcl_d = nc.dram_tensor("wcl", [2, NE, D, D], BF16, kind="Internal").ap()

    dbg = {}
    if debug:
        for nm, shape, dt in [
            ("d_q", [D, T], BF16), ("d_k", [D, T], BF16),
            ("d_s", [128, TC], F32), ("d_resid1", [T, D], F32),
            ("d_xln", [T, D], F32), ("d_xa", [64, T], F32),
            ("d_C", [128, TC * 36], F32),
            ("d_y1", [T, D], F32), ("d_y2", [T, D], F32),
        ]:
            dbg[nm] = nc.dram_tensor(nm, shape, dt, kind="ExternalOutput").ap()

    es0 = ExitStack()
    ctx = Ctx(nc, es0)

    def sb(esx, name, shape, dt):
        return esx.enter_context(nc.sbuf_tensor(name, shape, dt))

    smat = sb(es0, "smat", [128, TC], F32)
    fc1wt_s = sb(es0, "fc1wt_s", [1, 17], F32)
    fc1b_s = sb(es0, "fc1b_s", [17, 1], F32)
    identf_s = sb(es0, "identf_s", [128, 128], F32)
    ctok = sb(es0, "ctok", [128, TC * 36], F32)
    bnb = sb(es0, "bnb", [128, 8 * 12], F32)
    mvv = sb(es0, "mvv", [128, 2 * TC], F32)
    rv = sb(es0, "rv", [128, TC], F32)
    b2_s = sb(es0, "b2_s", [NE, D], F32)
    uallt_s = sb(es0, "uallt_s", [64, 36], F32)
    zq = sb(es0, "zq", [128, 144], F32)
    eq1 = sb(es0, "eq1", [128, 144], BF16)
    eq2 = sb(es0, "eq2", [128, 144], BF16)
    eq3 = sb(es0, "eq3", [128, 144], BF16)
    rtg = sb(es0, "rtg", [128, 144], F32)

    PS = [es0.enter_context(nc.psum_tensor(f"ps{i}", [128, 512], F32))
          for i in range(8)]

    with nc.Block() as b:
        def f(s):
            for tgt, src in [(fc1wt_s, fc1wt_d), (fc1b_s, fc1b_d),
                             (identf_s, identf_d), (b2_s, b2_d),
                             (uallt_s, uallt_d)]:
                ctx.inc(s.dma_start(out=tgt[:, :], in_=src[:, :]), "ds", 16)
        b.sync(f)

    es2 = ExitStack()
    xln = sb(es2, "xln", [128, TC * D], F32)

    es1 = ExitStack()
    qTb = sb(es1, "qTb", [128, KC * T], BF16)
    kTb = sb(es1, "kTb", [128, KC * T], BF16)
    trimask_s = sb(es1, "trimask_s", [128, 128], F32)

    es1a = ExitStack()
    x0th = xln[:, 0:KC * T // 2].bitcast(BF16)
    x0tl = xln[:, KC * T // 2:KC * T].bitcast(BF16)
    wsh = sb(es1a, "wsh", [128, KC * D], BF16)
    wsl = sb(es1a, "wsl", [128, KC * D], BF16)
    stg = [sb(es1a, "stg0", [128, T], F32)]
    stgx = sb(es1a, "stgx", [128, T], F32)
    tabs = [sb(es1a, f"tab{i}", [128, T], BF16) for i in range(4)]
    rsc = [stg[0][:, 0:1024].bitcast(BF16), stg[0][:, 1024:2048].bitcast(BF16),
           stgx[:, 0:1024].bitcast(BF16)]

    with nc.Block() as b:
        xsp = {}
        wsp_q = {}
        xslot = [stgx, qTb[:, 0:2 * T].bitcast(F32)]

        def emit_xload(bb, k):
            qn = "dg" if k % 2 == 0 else "dv"

            def f(eng, k=k, qn=qn):
                if k >= 2:
                    ctx.w(eng, "ve", xsp[k - 2][1])
                ctx.inc(eng.dma_start(out=xslot[k % 2][:, :],
                                      in_=x0t_d[k * 128:(k + 1) * 128, :]),
                        qn, 16)
            (bb.gpsimd if k % 2 == 0 else bb.scalar)(f)
            xsp[k] = [0, 0, (qn, ctx.c[qn])]

        def emit_wqload(bb, k):
            def f(s, k=k):
                if k >= 2:
                    ctx.w(s, "ve", wsp_q[k - 2][1])
                ctx.inc(s.dma_start(out=stg[0][:, (k % 2) * D:(k % 2 + 1) * D],
                                    in_=wqt_d[k * 128:(k + 1) * 128, :]),
                        "ds", 16)
            bb.sync(f)
            wsp_q[k] = [0, 0, ctx.c["ds"]]

        emit_xload(b, 0)
        emit_xload(b, 1)
        emit_wqload(b, 0)
        emit_wqload(b, 1)
        for k in range(KC):
            def f_hi(s, k=k):
                qn, cv = xsp[k][2]
                ctx.w(s, qn, cv)
                r = s.copy(x0th[:, k * T:(k + 1) * T], xslot[k % 2][:, :])
                xsp[k][0] = ctx.inc(r, "sc")
            b.scalar(f_hi)

            def f_whi(s, k=k):
                ctx.w(s, "ds", wsp_q[k][2])
                r = s.copy(wsh[:, k * D:(k + 1) * D],
                           stg[0][:, (k % 2) * D:(k % 2 + 1) * D])
                wsp_q[k][0] = ctx.inc(r, "sc")
            b.scalar(f_whi)

            def f_lo(v, k=k):
                ctx.w(v, "sc", xsp[k][0])
                r = v.tensor_tensor(x0tl[:, k * T:(k + 1) * T], xslot[k % 2][:, :],
                                    x0th[:, k * T:(k + 1) * T], ALU.subtract)
                xsp[k][1] = ctx.inc(r, "ve")
            b.vector(f_lo)

            def f_wlo(v, k=k):
                ctx.w(v, "sc", wsp_q[k][0])
                r = v.tensor_tensor(wsl[:, k * D:(k + 1) * D],
                                    stg[0][:, (k % 2) * D:(k % 2 + 1) * D],
                                    wsh[:, k * D:(k + 1) * D], ALU.subtract)
                wsp_q[k][1] = ctx.inc(r, "ve")
            b.vector(f_wlo)
            if k + 2 < KC:
                emit_xload(b, k + 2)
                emit_wqload(b, k + 2)
        xsp_sc, xsp_ve = ctx.c["sc"], ctx.c["ve"]
        xlo_marks = {k: xsp[k][1] for k in range(KC)}
        wqlo_marks = {k: wsp_q[k][1] for k in range(KC)}

        def f_tm(s):
            ctx.inc(s.dma_start(out=trimask_s[:, :], in_=trimask_d[:, :]), "ds", 16)
        b.sync(f_tm)

        rope_ve = {}
        evacs = {}

        def emit_rope_pair(pair, evq, evk, scw=0):
            c, c4 = pair, pair + 4

            def f_g(g, pair=pair, c=c, c4=c4, scw=scw):
                if scw:
                    ctx.w(g, "sc", scw)
                if pair >= 1:
                    ctx.w(g, "ve", rope_ve[pair - 1])
                for i, (tsrc, cc) in enumerate([(costab_d, c), (sintab_d, c),
                                                (costab_d, c4), (sintab_d, c4)]):
                    ctx.inc(g.dma_start(out=tabs[i][:, :],
                                        in_=tsrc[cc * 128:(cc + 1) * 128, :]),
                            "dg", 16)
            b.gpsimd(f_g)
            dg_p = ctx.c["dg"]

            def f_v(v, pair=pair, c=c, c4=c4, dg_p=dg_p, evq=evq, evk=evk):
                ctx.w(v, "dg", dg_p)
                ctx.w(v, "sc", max(evq, evk))
                r = None
                for tens in (qTb, kTb):
                    a = tens[:, c * T:(c + 1) * T]
                    a4 = tens[:, c4 * T:(c4 + 1) * T]
                    v.tensor_tensor(rsc[0], a, tabs[0][:, :], ALU.mult)
                    v.tensor_tensor(rsc[1], a4, tabs[1][:, :], ALU.mult)
                    v.tensor_tensor(rsc[2], a, tabs[3][:, :], ALU.mult)
                    v.tensor_tensor(a, rsc[0], rsc[1], ALU.add)
                    v.tensor_tensor(rsc[0], a4, tabs[2][:, :], ALU.mult)
                    r = v.tensor_tensor(a4, rsc[0], rsc[2], ALU.add)
                rope_ve[pair] = ctx.inc(r, "ve")
            b.vector(f_v)

        for proj, (w_dram, outb) in enumerate([(wqt_d, qTb), (wkt_d, kTb)]):
            evac = {}
            evacs[proj] = evac
            pe_guard = ctx.c["pe"]
            if proj == 0:
                wsp = wsp_q
            else:
                wsp = {}
                kslots = [stg[0][:, 0:D], stg[0][:, D:2 * D],
                          stgx[:, 0:D], stgx[:, D:2 * D],
                          tabs[0][:, 0:2 * D].bitcast(F32),
                          tabs[1][:, 0:2 * D].bitcast(F32)]
                kguard = [wqlo_marks[6], wqlo_marks[7], xlo_marks[6],
                          xlo_marks[6], 0, 0]

                def emit_wkload(bb, k):
                    qn = "ds" if k % 2 == 0 else "dg"

                    def f(eng, k=k, qn=qn):
                        if k < 6:
                            ctx.w(eng, "ve", kguard[k])
                        else:
                            ctx.w(eng, "ve", wsp[k - 6][1])
                        ctx.inc(eng.dma_start(
                            out=kslots[k % 6],
                            in_=w_dram[k * 128:(k + 1) * 128, :]), qn, 16)
                    (bb.sync if k % 2 == 0 else bb.gpsimd)(f)
                    wsp[k] = [0, 0, (qn, ctx.c[qn])]

                for k in range(6):
                    emit_wkload(b, k)
                for k in range(KC):
                    def f_whik(s, k=k):
                        qn, cv = wsp[k][2]
                        ctx.w(s, qn, cv)
                        if k == 0:
                            ctx.w(s, "pe", pe_guard)
                        r = s.copy(wsh[:, k * D:(k + 1) * D], kslots[k % 6])
                        wsp[k][0] = ctx.inc(r, "sc")
                    b.scalar(f_whik)

                    def f_wlok(v, k=k):
                        ctx.w(v, "sc", wsp[k][0])
                        r = v.tensor_tensor(wsl[:, k * D:(k + 1) * D],
                                            kslots[k % 6],
                                            wsh[:, k * D:(k + 1) * D],
                                            ALU.subtract)
                        wsp[k][1] = ctx.inc(r, "ve")
                    b.vector(f_wlok)
                    if k + 6 < KC:
                        emit_wkload(b, k + 6)

            for oc in range(8):
                for tb in range(4):
                    idx = oc * 4 + tb
                    bank = PS[idx % 2]
                    pe_t = {}

                    def f_pe(t, oc=oc, tb=tb, idx=idx, bank=bank, pe_t=pe_t,
                             wsp=wsp, evac=evac):
                        if idx == 0:
                            ctx.w(t, "sc", wsp[KC - 1][0])
                            ctx.w(t, "ve", wsp[KC - 1][1])
                            ctx.w(t, "sc", xsp_sc)
                            ctx.w(t, "ve", xsp_ve)
                        if idx >= 2:
                            ctx.w(t, "sc", evac[idx - 2])
                        r = None
                        for k in range(KC):
                            wh_sl = wsh[:, k * D + oc * 128: k * D + (oc + 1) * 128]
                            wl_sl = wsl[:, k * D + oc * 128: k * D + (oc + 1) * 128]
                            xh_sl = x0th[:, k * T + tb * 512: k * T + (tb + 1) * 512]
                            xl_sl = x0tl[:, k * T + tb * 512: k * T + (tb + 1) * 512]
                            t.matmul(bank[:, :], wh_sl, xh_sl,
                                     start=(k == 0), stop=False)
                            t.matmul(bank[:, :], wh_sl, xl_sl,
                                     start=False, stop=False)
                            r = t.matmul(bank[:, :], wl_sl, xh_sl,
                                         start=False, stop=(k == KC - 1))
                        pe_t[0] = ctx.inc(r, "pe")
                    b.tensor(f_pe)

                    def f_ev(s, oc=oc, tb=tb, idx=idx, bank=bank, outb=outb,
                             pe_t=pe_t, evac=evac):
                        ctx.w(s, "pe", pe_t[0])
                        r = s.copy(outb[:, oc * T + tb * 512: oc * T + (tb + 1) * 512],
                                   bank[:, :])
                        evac[idx] = ctx.inc(r, "sc")
                    b.scalar(f_ev)
                    if proj == 1 and oc >= 4 and tb == 3:
                        pair = oc - 4
                        emit_rope_pair(pair,
                                       max(evacs[0].values()),
                                       evac[oc * 4 + 3],
                                       scw=(wsp[5][0] if pair == 0 else 0))

    _barrier(nc, ctx)
    es1a.close()


    es2a = ExitStack()
    relu16 = sb(es2a, "relu16", [17, T], F32)
    srow = sb(es2a, "srow", [1, T], F32)
    fc2wta_s = sb(es2a, "fc2wta_s", [17, D], F32)
    es1c = ExitStack()
    expbuf2 = [sb(es1c, f"expbuf{i}", [128, T], F32) for i in range(2)]

    with nc.Block() as b:
        x0_dg = {}

        def f_gx(g):
            for tc in range(TC):
                r = g.dma_start(out=xln[:, tc * D:(tc + 1) * D],
                                in_=x0_d[tc * 128:(tc + 1) * 128, :])
                x0_dg[tc] = ctx.inc(r, "dg", 16)
        b.gpsimd(f_gx)

        def f_sw(s):
            ctx.inc(s.dma_start(out=fc2wta_s[:, :], in_=fc2wta_d[:, :]), "ds", 16)
        b.sync(f_sw)
        fc2_ds = ctx.c["ds"]

        bank_ready = {}
        red_ve = {}
        sr_sc = {}
        r16_sc = {}
        last_ao = {"ve": 0}
        ln_done = {}
        for qc in range(TC):
            nblk = qc // 4 + 1
            valid = (qc + 1) * 128
            rem = valid - (nblk - 1) * 512
            pe_q = {}
            b0 = (qc % 2) * 4
            expbuf = expbuf2[qc % 2]

            def f_pe(t, qc=qc, nblk=nblk, rem=rem, pe_q=pe_q, b0=b0):
                for jb in range(nblk):
                    if (b0 + jb) in bank_ready:
                        nm, val = bank_ready[b0 + jb]
                        ctx.w(t, nm, val)
                for dk in range(KC):
                    r = None
                    for jb in range(nblk):
                        N = 512 if jb < nblk - 1 else rem
                        r = t.matmul(PS[b0 + jb][:, 0:N],
                                     qTb[:, dk * T + qc * 128: dk * T + (qc + 1) * 128],
                                     kTb[:, dk * T + jb * 512: dk * T + jb * 512 + N],
                                     start=(dk == 0), stop=(dk == KC - 1))
                    if dk == KC - 1:
                        pe_q[0] = ctx.inc(r, "pe")
            b.tensor(f_pe)

            def f_vm(v, qc=qc, nblk=nblk, rem=rem, pe_q=pe_q, b0=b0):
                ctx.w(v, "pe", pe_q[0])
                r = v.tensor_tensor(PS[b0 + nblk - 1][:, rem - 128:rem],
                                    PS[b0 + nblk - 1][:, rem - 128:rem],
                                    trimask_s[:, :], ALU.add)
                pe_q[1] = ctx.inc(r, "ve")
            b.vector(f_vm)

            def f_sc(s, qc=qc, nblk=nblk, rem=rem, pe_q=pe_q, b0=b0,
                     expbuf=expbuf):
                ctx.w(s, "pe", pe_q[0])
                ctx.w(s, "ve", pe_q[1])
                if qc >= 2:
                    ctx.w(s, "ve", red_ve[qc - 2])
                for jb in range(nblk):
                    N = 512 if jb < nblk - 1 else rem
                    r = s.activation(out=expbuf[:, jb * 512: jb * 512 + N],
                                     in_=PS[b0 + jb][:, 0:N], func=AF.Exp,
                                     scale=1.0 / 32.0)
                    bank_ready[b0 + jb] = ("sc", ctx.inc(r, "sc"))
            b.scalar(f_sc)

            def f_red(v, qc=qc, valid=valid, expbuf=expbuf):
                ctx.w(v, "sc")
                r = v.reduce_sum(smat[:, qc:qc + 1], expbuf[:, 0:valid], axis=AX.X)
                red_ve[qc] = ctx.inc(r, "ve")
            b.vector(f_red)

            if qc % 4 == 3:
                g4 = qc // 4
                Tb = (g4 % 2) * 4 + 3
                tr = {}

                def f_trg(t, g4=g4, Tb=Tb, tr=tr):
                    ctx.w(t, "ve", red_ve[4 * g4 + 3])
                    if Tb in bank_ready:
                        nm, val = bank_ready[Tb]
                        ctx.w(t, nm, val)
                    r = None
                    for j in range(4):
                        qq = g4 * 4 + j
                        r = t.transpose(PS[Tb][0:1, j * 128:(j + 1) * 128],
                                        smat[:, qq:qq + 1], identf_s[:, :])
                    tr[0] = ctx.inc(r, "pe")
                b.tensor(f_trg)

                def f_srg(s, g4=g4, Tb=Tb, tr=tr):
                    ctx.w(s, "pe", tr[0])
                    s.copy(srow[0:1, g4 * 512:(g4 + 1) * 512], PS[Tb][0:1, 0:512])
                    s.drain()
                    sr_sc[g4] = ctx.inc(s.nop(), "sc")
                b.scalar(f_srg)

                r16 = {}

                def f_r16(t, g4=g4, Tb=Tb, r16=r16):
                    ctx.w(t, "sc", sr_sc[g4])
                    r = t.matmul(PS[Tb][0:17, 0:512], fc1wt_s[0:1, :],
                                 srow[0:1, g4 * 512:(g4 + 1) * 512],
                                 start=True, stop=True)
                    r16[0] = ctx.inc(r, "pe")
                b.tensor(f_r16)

                def f_r16s(s, g4=g4, Tb=Tb, r16=r16):
                    ctx.w(s, "pe", r16[0])
                    s.activation(out=relu16[0:17, g4 * 512:(g4 + 1) * 512],
                                 in_=PS[Tb][0:17, 0:512], func=AF.Relu,
                                 bias=fc1b_s[:, 0:1])
                    s.drain()
                    r16_sc[g4] = ctx.inc(s.nop(), "sc")
                b.scalar(f_r16s)

                for j in range(4):
                    tc = g4 * 4 + j
                    for oh in range(2):
                        ao = {}

                        def f_ao(t, tc=tc, oh=oh, g4=g4, Tb=Tb, ao=ao, j=j):
                            if j == 0 and oh == 0:
                                ctx.w(t, "sc", r16_sc[g4])
                                ctx.w(t, "ds", fc2_ds)
                            else:
                                ctx.w(t, "ve", last_ao["ve"])
                            r = t.matmul(PS[Tb][:, :],
                                         relu16[:, tc * 128:(tc + 1) * 128],
                                         fc2wta_s[:, oh * 512:(oh + 1) * 512],
                                         start=True, stop=True)
                            ao[0] = ctx.inc(r, "pe")
                        b.tensor(f_ao)

                        def f_aov(v, tc=tc, oh=oh, Tb=Tb, ao=ao):
                            ctx.w(v, "pe", ao[0])
                            ctx.w(v, "dg", x0_dg[tc])
                            sl = xln[:, tc * D + oh * 512: tc * D + (oh + 1) * 512]
                            r = v.scalar_tensor_tensor(sl, PS[Tb][:, :], 1.0, sl,
                                                       ALU.mult, ALU.add)
                            last_ao["ve"] = ctx.inc(r, "ve")
                        b.vector(f_aov)

                    def f_lns(v, tc=tc):
                        v.drain()
                        tl = tc % 8
                        v.bn_stats(bnb[:, tl * 12:tl * 12 + 6],
                                   xln[:, tc * D: tc * D + 512])
                        v.bn_stats(bnb[:, tl * 12 + 6:tl * 12 + 12],
                                   xln[:, tc * D + 512: tc * D + 1024])
                    b.vector(f_lns)
                bank_ready[Tb] = ("ve", last_ao["ve"])

                if g4 % 2 == 1:
                    half = g4 // 2

                    def f_lna(v, half=half):
                        v.drain()
                        for tl in range(8):
                            tc = half * 8 + tl
                            v.bn_aggr(mvv[:, 2 * tc:2 * tc + 2],
                                      bnb[:, tl * 12:(tl + 1) * 12])
                        v.drain()
                        for tl in range(8):
                            tc = half * 8 + tl
                            v.tensor_scalar(rv[:, tc:tc + 1],
                                            mvv[:, 2 * tc + 1:2 * tc + 2],
                                            EPS_LN, None, ALU.add)
                        v.drain()
                        ln_done[half] = ctx.inc(v.nop(), "ve")
                    b.vector(f_lna)

        def f_rstd(s):
            ctx.w(s, "ve", ln_done[1])
            s.activation(out=rv[:, :], in_=rv[:, :], func=AF.Ln)
            s.drain()
            s.activation(out=rv[:, :], in_=rv[:, :], func=AF.Exp, scale=-0.5)
            s.drain()
            ctx.inc(s.nop(), "sc")
        b.scalar(f_rstd)
        rstd_sc = ctx.c["sc"]

        if debug:
            def f_dbg(g):
                ctx.w(g, "ve")
                ctx.inc(g.dma_start(out=dbg["d_s"][:, :], in_=smat[:, :]), "dg", 16)
                for tc in range(TC):
                    ctx.inc(g.dma_start(out=dbg["d_resid1"][tc * 128:(tc + 1) * 128, :],
                                        in_=xln[:, tc * D:(tc + 1) * D]), "dg", 16)
            b.gpsimd(f_dbg)

        def f_ap(v):
            ctx.w(v, "sc", rstd_sc)
            if debug:
                ctx.w(v, "dg")
            for tc in range(TC):
                v.tensor_scalar(xln[:, tc * D:(tc + 1) * D],
                                xln[:, tc * D:(tc + 1) * D],
                                mvv[:, 2 * tc:2 * tc + 1], rv[:, tc:tc + 1],
                                ALU.subtract, ALU.mult)
            v.drain()
            ctx.inc(v.nop(), "ve")
        b.vector(f_ap)

        if debug:
            def f_dbg2(g):
                ctx.w(g, "ve")
                for tc in range(TC):
                    ctx.inc(g.dma_start(out=dbg["d_xln"][tc * 128:(tc + 1) * 128, :],
                                        in_=xln[:, tc * D:(tc + 1) * D]), "dg", 16)
            b.gpsimd(f_dbg2)

    _barrier(nc, ctx)
    es1c.close()
    es2a.close()
    es1.close()

    es2b = ExitStack()
    xth = sb(es2b, "xth", [128, KC * 512], BF16)
    xtl = sb(es2b, "xtl", [128, KC * 512], BF16)
    hth = sb(es2b, "hth", [128, KC * 512], BF16)
    htl = sb(es2b, "htl", [128, KC * 512], BF16)
    yq = sb(es2b, "yq", [128, TCQ * D], F32)
    wbh = [sb(es2b, f"wbh{i}", [128, KC * D], BF16) for i in range(2)]
    wbl = [sb(es2b, f"wbl{i}", [128, KC * D], BF16) for i in range(2)]
    wstg = [sb(es2b, f"wstg{i}", [128, 2 * D], F32) for i in range(2)]
    pwh = sb(es2b, "pwh", [128, KC * 64], BF16)
    pwl = sb(es2b, "pwl", [128, KC * 64], BF16)
    xa_s = sb(es2b, "xa_s", [64, 512], F32)
    c3tq = sb(es2b, "c3tq", [NE, 512], F32)

    wstg_guard = {0: 0, 1: 0}
    wbuf_guard = {0: 0, 1: 0}
    wsth_guard = {0: 0, 1: 0}
    wstl_guard = {0: 0, 1: 0}
    sth_marks = {}
    stl_marks = {}
    bank_guard = {i: 0 for i in range(6)}

    with nc.Block() as b:
        pw_ds = {}
        for k in range(KC):
            def f_pwl(s, k=k):
                ctx.inc(s.dma_start(out=wstg[0][:, k * 64:(k + 1) * 64],
                                    in_=pwt_d[k * 128:(k + 1) * 128, :]), "ds", 16)
                pw_ds[k] = ctx.c["ds"]
            b.sync(f_pwl)

        def f_pwh(s):
            ctx.w(s, "ds", pw_ds[KC - 1])
            ctx.inc(s.copy(pwh[:, 0:KC * 64], wstg[0][:, 0:KC * 64]), "sc")
        b.scalar(f_pwh)
        pwh_sc = ctx.c["sc"]

        def f_pwlo(v):
            ctx.w(v, "sc", pwh_sc)
            ctx.inc(v.tensor_tensor(pwl[:, 0:KC * 64], wstg[0][:, 0:KC * 64],
                                    pwh[:, 0:KC * 64], ALU.subtract), "ve")
        b.vector(f_pwlo)
        wstg_guard[0] = ctx.c["ve"]

    def emit_wload(b, src_dram, e, buf, defer_subs=False, wi=0, qq=0):
        if qq > 0:
            hi_m = {}

            def f_hih(g, wi=wi, e=e, buf=buf):
                ctx.w(g, "pe", wbuf_guard[buf])
                ctx.w(g, "dv", max(wsth_guard[buf], sth_marks[(wi, e)]))
                src = wch_d[wi, e].rearrange("(a p) d -> p a d", p=128)
                dst = wbh[buf][:, :].rearrange("p (a d) -> p a d", d=D)
                hi_m[0] = ctx.inc(g.dma_start(out=dst, in_=src), "dg", 16)
            b.gpsimd(f_hih)

            def f_lol(q_, wi=wi, e=e, buf=buf):
                ctx.w(q_, "pe", wbuf_guard[buf])
                ctx.w(q_, "ds", max(wstl_guard[buf], stl_marks[(wi, e)]))
                src = wcl_d[wi, e].rearrange("(a p) d -> p a d", p=128)
                dst = wbl[buf][:, :].rearrange("p (a d) -> p a d", d=D)
                ctx.inc(q_.dma_start(out=dst, in_=src), "ds", 16)
            b.sync(f_lol)
            jobs = [lambda bb: None] * 4 if defer_subs else []
            return {"dg": hi_m[0], "ds": ctx.c["ds"], "marks": {3: 0},
                    "jobs": jobs}

        marks = {}

        def f_cast(g, e=e, src_dram=src_dram, buf=buf):
            ctx.w(g, "pe", wbuf_guard[buf])
            ctx.w(g, "dv", wsth_guard[buf])
            src = src_dram[e].rearrange("(a p) d -> p a d", p=128)
            dst = wbh[buf][:, :].rearrange("p (a d) -> p a d", d=D)
            ctx.inc(g.dma_start(out=dst, in_=src), "dg", 16)
        b.gpsimd(f_cast)
        cast_dg = ctx.c["dg"]
        dq_marks = {}

        def emit_dma(bb, d):
            k0 = 2 * d
            qsem = "ds" if d % 2 == 0 else "dg"

            def f_l(q, d=d, k0=k0, qsem=qsem):
                if d >= 2:
                    ctx.w(q, "ve", marks[d - 2])
                else:
                    ctx.w(q, "ve", wstg_guard[d % 2])
                    ctx.w(q, "pe", wbuf_guard[buf])
                src = src_dram[e, k0 * 128:(k0 + 2) * 128, :].rearrange(
                    "(a p) d -> p a d", p=128)
                dst = wstg[d % 2][:, :].rearrange("p (a d) -> p a d", d=D)
                ctx.inc(q.dma_start(out=dst, in_=src), qsem, 16)
            (bb.sync if d % 2 == 0 else bb.gpsimd)(f_l)
            dq_marks[d] = (qsem, ctx.c[qsem])

        def emit_sub(bb, d):
            k0 = 2 * d
            qsem, dq_k = dq_marks[d]

            def f_lo(v, d=d, k0=k0, dq_k=dq_k, qsem=qsem):
                ctx.w(v, qsem, dq_k)
                ctx.w(v, "dg", cast_dg)
                if d == 0:
                    ctx.w(v, "ds", wstl_guard[buf])
                r = v.tensor_tensor(wbl[buf][:, k0 * D:(k0 + 2) * D],
                                    wstg[d % 2][:, :],
                                    wbh[buf][:, k0 * D:(k0 + 2) * D], ALU.subtract)
                marks[d] = ctx.inc(r, "ve")
                wstg_guard[d % 2] = marks[d]
            bb.vector(f_lo)

        def emit_store(bb, wi=wi, e=e, buf=buf):
            def f_sth(s, wi=wi, e=e, buf=buf):
                ctx.w(s, "dg", cast_dg)
                dst = wch_d[wi, e].rearrange("(a p) d -> p a d", p=128)
                src = wbh[buf][:, :].rearrange("p (a d) -> p a d", d=D)
                ctx.inc(s.dma_start(out=dst, in_=src), "dv", 16)
            bb.scalar(f_sth)
            wsth_guard[buf] = ctx.c["dv"]
            sth_marks[(wi, e)] = ctx.c["dv"]

            def f_stl(q_, wi=wi, e=e, buf=buf):
                ctx.w(q_, "ve", marks[3])
                dst = wcl_d[wi, e].rearrange("(a p) d -> p a d", p=128)
                src = wbl[buf][:, :].rearrange("p (a d) -> p a d", d=D)
                ctx.inc(q_.dma_start(out=dst, in_=src), "ds", 16)
            bb.sync(f_stl)
            wstl_guard[buf] = ctx.c["ds"]
            stl_marks[(wi, e)] = ctx.c["ds"]

        emit_dma(b, 0)
        emit_dma(b, 1)
        if not defer_subs:
            emit_sub(b, 0)
            emit_sub(b, 1)
            emit_dma(b, 2)
            emit_dma(b, 3)
            emit_sub(b, 2)
            emit_sub(b, 3)
            emit_store(b)
            jobs = []
        else:
            def j0(bb):
                emit_sub(bb, 0)
                emit_sub(bb, 1)
                emit_dma(bb, 2)
                emit_dma(bb, 3)

            def j1(bb):
                emit_sub(bb, 2)

            def j2(bb):
                emit_sub(bb, 3)

            def j3(bb):
                emit_store(bb)
            jobs = [j0, j1, j2, j3]
        return {"dg": cast_dg, "ds": 0, "marks": marks, "jobs": jobs}

    def emit_transpose_split(b, src, src_col0, dsth, dstl):
        ev_prev = {}
        for k in range(KC):
            bank = PS[4 + (k % 2)]
            mark = {}

            def f_t(t, k=k, bank=bank, mark=mark):
                if k >= 2:
                    ctx.w(t, "ve", ev_prev[k - 2])
                else:
                    ctx.w(t, "ve")
                    ctx.w(t, "sc")
                r = None
                for tcl in range(TCQ):
                    r = t.transpose(bank[:, tcl * 128:(tcl + 1) * 128],
                                    src[:, src_col0 + tcl * D + k * 128:
                                        src_col0 + tcl * D + (k + 1) * 128],
                                    identf_s[:, :])
                mark[0] = ctx.inc(r, "pe")
            b.tensor(f_t)

            def f_h(s_, k=k, bank=bank, mark=mark):
                ctx.w(s_, "pe", mark[0])
                r = s_.copy(dsth[:, k * 512:(k + 1) * 512], bank[:, :])
                mark[1] = ctx.inc(r, "sc")
            b.scalar(f_h)

            def f_l(v, k=k, bank=bank, mark=mark):
                ctx.w(v, "sc", mark[1])
                r = v.tensor_tensor(dstl[:, k * 512:(k + 1) * 512], bank[:, :],
                                    dsth[:, k * 512:(k + 1) * 512], ALU.subtract)
                ev_prev[k] = ctx.inc(r, "ve")
            b.vector(f_l)

        def f_fence(v):
            v.drain()
            ctx.inc(v.nop(), "ve")
        b.vector(f_fence)

        def f_fence_s(s_):
            s_.drain()
            ctx.inc(s_.nop(), "sc")
        b.scalar(f_fence_s)
        bank_guard[4] = ctx.c["ve"]
        bank_guard[5] = ctx.c["ve"]
        return (ctx.c["sc"], ctx.c["ve"])

    def emit_routing(b, q, xt_ready):
        xa_mark = {}

        def f_xa(t):
            ctx.w(t, "sc", xt_ready[0])
            ctx.w(t, "ve", xt_ready[1])
            r = None
            for k in range(KC):
                ph = pwh[:, k * 64:(k + 1) * 64]
                pl = pwl[:, k * 64:(k + 1) * 64]
                xh_sl = xth[:, k * 512:(k + 1) * 512]
                xl_sl = xtl[:, k * 512:(k + 1) * 512]
                t.matmul(PS[4][0:64, 0:512], ph, xh_sl, start=(k == 0), stop=False)
                t.matmul(PS[4][0:64, 0:512], ph, xl_sl, start=False, stop=False)
                r = t.matmul(PS[4][0:64, 0:512], pl, xh_sl, start=False,
                             stop=(k == KC - 1))
            xa_mark[0] = ctx.inc(r, "pe")
        b.tensor(f_xa)

        def f_xaev(s):
            ctx.w(s, "pe", xa_mark[0])
            s.copy(xa_s[:, :], PS[4][0:64, 0:512])
            s.drain()
            xa_mark[1] = ctx.inc(s.nop(), "sc")
        b.scalar(f_xaev)

        if debug:
            def f_dbgxa(g, q=q):
                ctx.w(g, "sc", xa_mark[1])
                ctx.inc(g.dma_start(out=dbg["d_xa"][:, q * 512:(q + 1) * 512],
                                    in_=xa_s[:, :]), "dg", 16)
            b.gpsimd(f_dbgxa)

        zmark = {}

        def f_z(t):
            ctx.w(t, "sc", xa_mark[1])
            r = None
            for tcl in range(TCQ):
                r = t.matmul(PS[5][0:128, tcl * 36:tcl * 36 + 36],
                             xa_s[:, tcl * 128:(tcl + 1) * 128],
                             uallt_s[:, :], start=True, stop=True)
            zmark[0] = ctx.inc(r, "pe")
        b.tensor(f_z)
        rt = {"c3_sc": 0}
        return rt, lambda bb: emit_routing_fin(bb, q, zmark, rt)

    def emit_routing_fin(b, q, zmark, rt):
        def f_zev(s_):
            ctx.w(s_, "pe", zmark[0])
            s_.copy(zq[:, :], PS[5][0:128, 0:144])
            s_.activation(out=PS[6][:, :], in_=PS[6][:, :], func=AF.Copy)
            s_.drain()
            zmark[1] = ctx.inc(s_.nop(), "sc")
        b.scalar(f_zev)

        SQ = 12

        def seg(ap):
            return ap.rearrange("p (s e) -> p s e", e=12)

        m1 = rtg[:, 0 * SQ:1 * SQ]
        m2 = rtg[:, 1 * SQ:2 * SQ]
        m3 = rtg[:, 2 * SQ:3 * SQ]
        d2 = rtg[:, 3 * SQ:4 * SQ]
        d3 = rtg[:, 4 * SQ:5 * SQ]
        e2x = rtg[:, 5 * SQ:6 * SQ]
        e3x = rtg[:, 6 * SQ:7 * SQ]
        den = rtg[:, 7 * SQ:8 * SQ]
        w1 = rtg[:, 8 * SQ:9 * SQ]
        w2 = rtg[:, 9 * SQ:10 * SQ]
        w3 = rtg[:, 10 * SQ:11 * SQ]
        tmark = {}

        def f_t3a(v):
            ctx.w(v, "sc", zmark[1])
            v.tensor_reduce(m1, seg(zq[:, :]), AX.X, ALU.max)
            v.drain()
            v.tensor_tensor(seg(eq1[:, :]), seg(zq[:, :]),
                            m1.broadcast_to([128, SQ, 12]), ALU.is_equal)
            v.drain()
            v.scalar_tensor_tensor(zq[:, :], eq1[:, :], NEG, zq[:, :],
                                   ALU.mult, ALU.add)
            v.drain()
            v.tensor_reduce(m2, seg(zq[:, :]), AX.X, ALU.max)
            v.drain()
            v.tensor_tensor(seg(eq2[:, :]), seg(zq[:, :]),
                            m2.broadcast_to([128, SQ, 12]), ALU.is_equal)
            v.drain()
            v.scalar_tensor_tensor(zq[:, :], eq2[:, :], NEG, zq[:, :],
                                   ALU.mult, ALU.add)
            v.drain()
            v.tensor_reduce(m3, seg(zq[:, :]), AX.X, ALU.max)
            v.drain()
            v.tensor_tensor(seg(eq3[:, :]), seg(zq[:, :]),
                            m3.broadcast_to([128, SQ, 12]), ALU.is_equal)
            v.tensor_tensor(d2, m2, m1, ALU.subtract)
            v.tensor_tensor(d3, m3, m1, ALU.subtract)
            v.drain()
            tmark[0] = ctx.inc(v.nop(), "ve")
        b.vector(f_t3a)

        def f_t3exp(s_):
            ctx.w(s_, "ve", tmark[0])
            s_.activation(out=rtg[:, 5 * SQ:7 * SQ], in_=rtg[:, 3 * SQ:5 * SQ],
                          func=AF.Exp)
            s_.activation(out=PS[6][:, :], in_=PS[6][:, :], func=AF.Copy)
            s_.drain()
            tmark[1] = ctx.inc(s_.nop(), "sc")
        b.scalar(f_t3exp)

        def f_t3b(v, q=q):
            ctx.w(v, "sc", tmark[1])
            cdst = ctok[:, q * TCQ * 36:(q + 1) * TCQ * 36]
            v.tensor_tensor(den, e2x, e3x, ALU.add)
            v.drain()
            v.tensor_scalar(den, den, 1.0, None, ALU.add)
            v.drain()
            v.reciprocal(w1, den)
            v.drain()
            v.tensor_tensor(w2, e2x, w1, ALU.mult)
            v.tensor_tensor(w3, e3x, w1, ALU.mult)
            v.drain()
            v.tensor_tensor(seg(cdst), seg(eq1[:, :]),
                            w1.broadcast_to([128, SQ, 12]), ALU.mult)
            v.tensor_tensor(seg(eq2[:, :]), seg(eq2[:, :]),
                            w2.broadcast_to([128, SQ, 12]), ALU.mult)
            v.drain()
            v.tensor_tensor(cdst, cdst, eq2[:, :], ALU.add)
            v.tensor_tensor(seg(eq3[:, :]), seg(eq3[:, :]),
                            w3.broadcast_to([128, SQ, 12]), ALU.mult)
            v.drain()
            v.tensor_tensor(cdst, cdst, eq3[:, :], ALU.add)
            v.drain()
            ctx.inc(v.nop(), "ve")
        b.vector(f_t3b)

        ctprev = {"ev": 0}
        for tcl in range(TCQ):
            tc = q * TCQ + tcl
            cm = {}

            def f_ct(t, tc=tc, cm=cm):
                ctx.w(t, "ve")
                ctx.w(t, "sc", ctprev["ev"])
                r = t.transpose(PS[5][0:12, 0:128],
                                ctok[:, tc * 36 + 24: tc * 36 + 36],
                                identf_s[:, :])
                cm[0] = ctx.inc(r, "pe")
            b.tensor(f_ct)

            def f_cte(s, tcl=tcl, cm=cm):
                ctx.w(s, "pe", cm[0])
                r = s.copy(c3tq[:, tcl * 128:(tcl + 1) * 128], PS[5][0:12, 0:128])
                ctprev["ev"] = ctx.inc(r, "sc")
            b.scalar(f_cte)
        rt["c3_sc"] = ctprev["ev"]

        if debug:
            def f_dbgc(g, q=q):
                ctx.w(g, "ve")
                ctx.w(g, "sc")
                cols = slice(q * TCQ * 36, (q + 1) * TCQ * 36)
                ctx.inc(g.dma_start(out=dbg["d_C"][:, cols], in_=ctok[:, cols]),
                        "dg", 16)
            b.gpsimd(f_dbgc)

    def emit_mixture_expert(b, q, m, e, buf, wready, xh_t, xl_t, first,
                            seed_sc=0, xt_marks=None, vejobs=(), postjobs=(),
                            c3_wait=None):
        vejobs = list(vejobs)
        postjobs = list(postjobs)
        for tcl in range(TCQ):
            tc = q * TCQ + tcl
            pedone = {}

            def f_mm(t, tcl=tcl, buf=buf, wready=wready, pedone=pedone,
                     xh_t=xh_t, xl_t=xl_t, first=first, seed_sc=seed_sc,
                     xt_marks=xt_marks, c3_wait=c3_wait):
                if tcl == 2 and c3_wait is not None:
                    ctx.w(t, "sc", c3_wait["c3_sc"])
                if tcl == 0:
                    ctx.w(t, "dg", wready["dg"])
                    ctx.w(t, "ds", wready["ds"])
                    ctx.w(t, "ve", wready["marks"][3])
                    if seed_sc:
                        ctx.w(t, "sc", seed_sc)
                    if xt_marks is not None:
                        ctx.w(t, "sc", xt_marks[0])
                        ctx.w(t, "ve", xt_marks[1])
                if first and tcl == 0:
                    ctx.w(t, "sc")
                ctx.w(t, "ve", max(bank_guard[((tcl + e) % 3) * 2],
                                   bank_guard[((tcl + e) % 3) * 2 + 1]))
                r = None
                for k in range(KC):
                    xh_sl = xh_t[:, k * 512 + tcl * 128: k * 512 + (tcl + 1) * 128]
                    xl_sl = xl_t[:, k * 512 + tcl * 128: k * 512 + (tcl + 1) * 128]
                    for oh in range(2):
                        bank = PS[((tcl + e) % 3) * 2 + oh]
                        wh_sl = wbh[buf][:, k * D + oh * 512: k * D + (oh + 1) * 512]
                        wl_sl = wbl[buf][:, k * D + oh * 512: k * D + (oh + 1) * 512]
                        t.matmul(bank[:, :], xh_sl, wh_sl,
                                 start=(k == 0), stop=False)
                        t.matmul(bank[:, :], xh_sl, wl_sl, start=False, stop=False)
                        r = t.matmul(bank[:, :], xl_sl, wh_sl, start=False,
                                     stop=(k == KC - 1))
                pedone[0] = ctx.inc(r, "pe")
            b.tensor(f_mm)

            def f_ep(v, tcl=tcl, tc=tc, e=e, m=m, first=first, pedone=pedone,
                     seed_sc=seed_sc):
                ctx.w(v, "pe", pedone[0])
                if first and m == 1:
                    ctx.w(v, "sc", seed_sc)
                r = None
                for oh in range(2):
                    bank = PS[((tcl + e) % 3) * 2 + oh]
                    ysl = yq[:, tcl * D + oh * 512: tcl * D + (oh + 1) * 512]
                    cc = ctok[:, tc * 36 + m * 12 + e: tc * 36 + m * 12 + e + 1]
                    if first and m == 0:
                        r = v.tensor_scalar(ysl, bank[:, :], cc, None, ALU.mult)
                    else:
                        r = v.scalar_tensor_tensor(ysl, bank[:, :], cc, ysl,
                                                   ALU.mult, ALU.add)
                n = ctx.inc(r, "ve")
                bank_guard[((tcl + e) % 3) * 2] = n
                bank_guard[((tcl + e) % 3) * 2 + 1] = n
            if tcl < len(vejobs):
                vejobs[tcl](b)
            b.vector(f_ep)
            if tcl < len(postjobs):
                postjobs[tcl](b)

    def emit_head(b, q):
        xt_ready = emit_transpose_split(b, xln, q * TCQ * D, xth, xtl)
        rt, fin = emit_routing(b, q, xt_ready)
        return xt_ready, rt, fin

    def emit_tail(b, q):
        r2_ve = {}

        def f_r2(v, q=q):
            for tcl in range(TCQ):
                tc = q * TCQ + tcl
                xsl = xln[:, tc * D:(tc + 1) * D]
                v.scalar_tensor_tensor(xsl, yq[:, tcl * D:(tcl + 1) * D], 1.0,
                                       xsl, ALU.mult, ALU.add)
            v.drain()
            for tcl in range(TCQ):
                tc = q * TCQ + tcl
                xsl = xln[:, tc * D:(tc + 1) * D]
                v.bn_stats(bnb[:, tcl * 12:tcl * 12 + 6], xsl[:, 0:512])
                v.bn_stats(bnb[:, tcl * 12 + 6:tcl * 12 + 12], xsl[:, 512:1024])
            v.drain()
            for tcl in range(TCQ):
                v.bn_aggr(mvv[:, 2 * tcl:2 * tcl + 2],
                          bnb[:, tcl * 12:(tcl + 1) * 12])
            v.drain()
            for tcl in range(TCQ):
                v.tensor_scalar(rv[:, tcl:tcl + 1],
                                mvv[:, 2 * tcl + 1:2 * tcl + 2],
                                EPS_LN, None, ALU.add)
            v.drain()
            r2_ve[TCQ - 1] = ctx.inc(v.nop(), "ve")
        b.vector(f_r2)

        def f_rstd2(s_):
            ctx.w(s_, "ve", r2_ve[TCQ - 1])
            s_.activation(out=rv[:, 0:TCQ], in_=rv[:, 0:TCQ], func=AF.Ln)
            s_.drain()
            s_.activation(out=rv[:, 0:TCQ], in_=rv[:, 0:TCQ], func=AF.Exp,
                          scale=-0.5)
            s_.drain()
            ctx.inc(s_.nop(), "sc")
        b.scalar(f_rstd2)
        rstd2_sc = ctx.c["sc"]

        for tcl in range(TCQ):
            tc = q * TCQ + tcl
            out_m = {}

            def f_ap2(v, tcl=tcl, tc=tc, out_m=out_m):
                if tcl == 0:
                    ctx.w(v, "sc", rstd2_sc)
                xsl = xln[:, tc * D:(tc + 1) * D]
                v.tensor_scalar(xsl, xsl, mvv[:, 2 * tcl:2 * tcl + 1],
                                rv[:, tcl:tcl + 1], ALU.subtract, ALU.mult)
                v.drain()
                out_m[0] = ctx.inc(v.nop(), "ve")
            b.vector(f_ap2)

            def f_st(g, tcl=tcl, tc=tc, out_m=out_m):
                ctx.w(g, "ve", out_m[0])
                ctx.inc(g.dma_start(out=out_d[tc * 128:(tc + 1) * 128, :],
                                    in_=xln[:, tc * D:(tc + 1) * D]), "so", 16)
            b.gpsimd(f_st)

    with nc.Block() as b:
        xt_ready_q = emit_head(b, 0)
        pre_w1 = {0: emit_wload(b, w1t_d, 0, 0, wi=0, qq=0)}
        for q in range(NQ):
            wready = pre_w1
            w2e0 = {}
            gelu_sc = {}
            xt_marks_q, rt_q, fin_q = xt_ready_q
            for e in range(NE):
                buf = e % 2
                jobs = ()
                if e + 1 < NE:
                    wready[e + 1] = emit_wload(b, w1t_d, e + 1, (e + 1) % 2,
                                               defer_subs=True, wi=0, qq=q)
                    jobs = wready[e + 1]["jobs"]
                    if e == 0:
                        j0o = jobs[0]
                        jobs = [lambda bb, f=fin_q, g=j0o: (f(bb), g(bb))] \
                            + list(jobs[1:])
                else:
                    w2e0[0] = emit_wload(b, w2t_d, 0, 0, defer_subs=True,
                                         wi=1, qq=q)
                    wjobs = w2e0[0]["jobs"]

                    def mkgelu(tcl):
                        def j(bb):
                            def f_vf(v):
                                v.drain()
                                ctx.inc(v.nop(), "ve")
                            bb.vector(f_vf)

                            def f_g(s_, tcl=tcl):
                                ctx.w(s_, "ve")
                                r = s_.activation(
                                    out=yq[:, tcl * D:(tcl + 1) * D],
                                    in_=yq[:, tcl * D:(tcl + 1) * D],
                                    func=AF.Gelu)
                                gelu_sc[0] = ctx.inc(r, "sc")
                            bb.scalar(f_g)
                        return j
                    pjobs = [mkgelu(i) for i in range(4)]
                    jobs = wjobs
                emit_mixture_expert(b, q, 0, e, buf, wready[e], xth, xtl,
                                    first=(e == 0), vejobs=jobs,
                                    postjobs=(pjobs if e + 1 == NE else ()),
                                    xt_marks=(xt_marks_q if e == 0 else None),
                                    c3_wait=(rt_q if e == 0 else None))
                wbuf_guard[buf] = ctx.c["pe"]

            if debug:
                def f_dbgy1(g, q=q):
                    ctx.w(g, "ve")
                    for tcl in range(TCQ):
                        tc = q * TCQ + tcl
                        ctx.inc(g.dma_start(
                            out=dbg["d_y1"][tc * 128:(tc + 1) * 128, :],
                            in_=yq[:, tcl * D:(tcl + 1) * D]), "dg", 16)
                b.gpsimd(f_dbgy1)

            def f_wait_gelu(t):
                ctx.w(t, "sc", gelu_sc[0])
            b.tensor(f_wait_gelu)

            ht_ready = emit_transpose_split(b, yq, 0, hth, htl)
            h_done_pe = ctx.c["pe"]

            seed_fin = {}
            for tcl in range(TCQ):
                sm = {}

                def f_seed(t, tcl=tcl, sm=sm):
                    if tcl == 0:
                        ctx.w(t, "sc")
                    for oh in range(2):
                        ctx.w(t, "ve", bank_guard[(tcl % 2) * 2 + oh])
                    r = None
                    for oh in range(2):
                        r = t.matmul(PS[(tcl % 2) * 2 + oh][:, :],
                                     c3tq[:, tcl * 128:(tcl + 1) * 128],
                                     b2_s[:, oh * 512:(oh + 1) * 512],
                                     start=True, stop=True)
                    sm[0] = ctx.inc(r, "pe")
                b.tensor(f_seed)

                def f_seedev(s_, tcl=tcl, sm=sm):
                    ctx.w(s_, "pe", sm[0])
                    ctx.w(s_, "pe", h_done_pe)
                    r = None
                    for oh in range(2):
                        r = s_.copy(yq[:, tcl * D + oh * 512:
                                       tcl * D + (oh + 1) * 512],
                                    PS[(tcl % 2) * 2 + oh][:, :])
                    seed_fin[tcl] = ctx.inc(r, "sc")
                b.scalar(f_seedev)
            seed_all = seed_fin[TCQ - 1]

            wready2 = w2e0
            pre_w1 = {}
            for e in range(NE):
                buf = e % 2
                jobs = ()
                if e + 1 < NE:
                    wready2[e + 1] = emit_wload(b, w2t_d, e + 1, (e + 1) % 2,
                                                defer_subs=True, wi=1, qq=q)
                    jobs = wready2[e + 1]["jobs"]
                elif q + 1 < NQ:
                    pre_w1[0] = emit_wload(b, w1t_d, 0, 0, defer_subs=True,
                                           wi=0, qq=q + 1)
                    jobs = pre_w1[0]["jobs"]
                emit_mixture_expert(b, q, 1, e, buf, wready2[e], hth, htl,
                                    first=(e == 0), seed_sc=seed_all,
                                    xt_marks=(ht_ready if e == 0 else None),
                                    vejobs=jobs)
                wbuf_guard[buf] = ctx.c["pe"]

            if debug:
                def f_dbgy2(g, q=q):
                    ctx.w(g, "ve")
                    for tcl in range(TCQ):
                        tc = q * TCQ + tcl
                        ctx.inc(g.dma_start(
                            out=dbg["d_y2"][tc * 128:(tc + 1) * 128, :],
                            in_=yq[:, tcl * D:(tcl + 1) * D]), "dg", 16)
                b.gpsimd(f_dbgy2)

            if q + 1 < NQ:
                xt_ready_q = emit_head(b, q + 1)
            emit_tail(b, q)

    with nc.Block() as b:
        def f_fin(g):
            g.wait_ge(ctx.sem["so"], 16 * TC)
        b.gpsimd(f_fin)

    nc._keepalive = (es0, es2, es2b)
    return nc



def _rope_tables():
    pos = np.arange(T, dtype=np.float32)
    inv = (np.float32(32.0) ** (-(np.arange(0, D, 2).astype(np.float32))
                                / np.float32(D))).astype(np.float32)
    ang = pos[:, None] * inv[None, :]
    emb = np.concatenate([ang, ang], axis=1)
    cos = np.cos(emb).astype(np.float32)
    sin = np.sin(emb).astype(np.float32)
    cosT = np.ascontiguousarray(cos.T).astype(BFD)
    sinS = np.ascontiguousarray(sin.T)
    sinS[:D // 2, :] *= np.float32(-1.0)
    sinST = sinS.astype(BFD)
    return cosT, sinST


def make_in_maps(inputs):
    hs = np.asarray(inputs["L0_attn_hidden_states"], dtype=np.float32)
    Wq = np.asarray(inputs["L0_attn_Wq"], dtype=np.float32)
    Wk = np.asarray(inputs["L0_attn_Wk"], dtype=np.float32)
    fc1w = np.asarray(inputs["L0_attn_fc1_w"], dtype=np.float32)
    fc1b = np.asarray(inputs["L0_attn_fc1_b"], dtype=np.float32)
    fc2w = np.asarray(inputs["L0_attn_fc2_w"], dtype=np.float32)
    fc2b = np.asarray(inputs["L0_attn_fc2_b"], dtype=np.float32)
    Pw = np.asarray(inputs["L0_ffn_P_w"], dtype=np.float32)
    U1 = np.asarray(inputs["L0_ffn_U1"], dtype=np.float32)
    U2 = np.asarray(inputs["L0_ffn_U2"], dtype=np.float32)
    U3 = np.asarray(inputs["L0_ffn_U3"], dtype=np.float32)
    W1 = np.asarray(inputs["L0_ffn_W1"], dtype=np.float32)
    W2 = np.asarray(inputs["L0_ffn_W2"], dtype=np.float32)
    b2 = np.asarray(inputs["L0_ffn_b2"], dtype=np.float32)

    cosT, sinST = _rope_tables()
    shared = {
        "wqt": np.ascontiguousarray(Wq.T),
        "wkt": np.ascontiguousarray(Wk.T),
        "fc1wt": np.ascontiguousarray(
            np.concatenate([fc1w.T, np.zeros((1, 1), np.float32)], axis=1)),
        "fc1b": np.ascontiguousarray(
            np.concatenate([fc1b, np.ones((1,), np.float32)])[:, None]),
        "fc2wta": np.ascontiguousarray(
            np.concatenate([fc2w.T, fc2b[None, :]], axis=0)),
        "pwt": np.ascontiguousarray(Pw.T),
        "uallt": np.ascontiguousarray(
            np.concatenate([U1.T, U2.T, U3.T], axis=1)),
        "b2": np.ascontiguousarray(b2),
        "w1t": np.ascontiguousarray(W1.transpose(0, 2, 1)),
        "w2t": np.ascontiguousarray(W2.transpose(0, 2, 1)),
        "costab": cosT,
        "sintab": sinST,
        "trimask": np.triu(np.full((128, 128), np.float32(NEG),
                                   dtype=np.float32), k=1),
        "identf": np.eye(128, dtype=np.float32),
    }
    in_maps = []
    for c in range(8):
        m = dict(shared)
        m["x0"] = np.ascontiguousarray(hs[c])
        m["x0t"] = np.ascontiguousarray(hs[c].T)
        in_maps.append(m)
    return in_maps


def kernel(**inputs):
    in_maps = make_in_maps(inputs)
    nc = build_nc(debug=False)
    res = run_bass_kernel_spmd(nc, in_maps, core_ids=list(range(8)))
    out = np.stack([np.asarray(res.results[c]["out"]) for c in range(8)], axis=0)
    return out.astype(np.float32)
